# version 11
# baseline (speedup 1.0000x reference)
import contextlib
import numpy as np
import ml_dtypes
import concourse.bass as bass
import concourse.mybir as mybir
from concourse.bass_utils import run_bass_kernel_spmd

F32 = mybir.dt.float32
BF16 = mybir.dt.bfloat16
AF = mybir.ActivationFunctionType
ALU = mybir.AluOpType
AX = mybir.AxisListType

B, S, D, H, DK, E, DFF = 2, 8192, 1024, 16, 64, 8, 4096
T = B * S
TLOC = T // 8
NJ = TLOC // 128
C0 = np.float32(-12.0)
INV8 = np.float32(0.125)
EPS = np.float32(1e-5)
INV_D = np.float32(1.0 / D)

GROUPS4 = [[0, 1, 2, 3], [4, 5, 6, 7]]
GROUPS8 = [list(range(8))]


def make_in_maps(inputs):
    x = np.asarray(inputs["L0_attn_x"], np.float32).reshape(T, D)
    Wqkv1 = np.asarray(inputs["L0_attn_W_qkv"], np.float32)
    Wo1 = np.asarray(inputs["L0_attn_W_o"], np.float32)
    perm = np.concatenate([np.arange(0, 64, 2), np.arange(1, 64, 2)])

    invf = (10000.0 ** (-np.arange(0, DK, 2, dtype=np.float32) / DK)).astype(np.float32)
    ang = np.arange(S, dtype=np.float32)[:, None] * invf[None, :]
    ropeC = np.concatenate([np.cos(ang), np.cos(ang)], 1).astype(np.float32)
    ropeS = np.concatenate([np.sin(ang), np.sin(ang)], 1).astype(np.float32)
    ropeC = np.ascontiguousarray(ropeC.reshape(64, 128, 64).transpose(1, 0, 2).reshape(128, 4096))
    ropeS = np.ascontiguousarray(ropeS.reshape(64, 128, 64).transpose(1, 0, 2).reshape(128, 4096))

    kvv = np.arange(128)[:, None]
    qv = np.arange(512)[None, :]
    maskc = np.concatenate([(128 * j + kvv <= qv) for j in range(4)], 1).astype(ml_dtypes.bfloat16)
    maskc = np.ascontiguousarray(maskc)

    common = {
        "ropeC": ropeC, "ropeS": ropeS, "maskc": maskc,

        "wqkv2": np.ascontiguousarray(np.asarray(inputs["L0_ffn_w_qkv"], np.float32)
                                      .reshape(8, 128, 3 * D).transpose(1, 0, 2).reshape(128, 24 * D)),
        "bqkv": np.asarray(inputs["L0_ffn_b_qkv"], np.float32).reshape(1, 3 * D),
        "wo2": np.ascontiguousarray(np.asarray(inputs["L0_ffn_w_o"], np.float32)),
        "bo": np.asarray(inputs["L0_ffn_b_o"], np.float32).reshape(1, D),
        "g1": np.asarray(inputs["L0_ffn_ln1_g"], np.float32).reshape(1, D),
        "b1ln": np.asarray(inputs["L0_ffn_ln1_b"], np.float32).reshape(1, D),
        "g2": np.asarray(inputs["L0_ffn_ln2_g"], np.float32).reshape(1, D),
        "b2ln": np.asarray(inputs["L0_ffn_ln2_b"], np.float32).reshape(1, D),
        "gatew": np.ascontiguousarray(np.asarray(inputs["L0_ffn_gate_w"], np.float32)
                                      .reshape(8, 128, E).transpose(1, 0, 2).reshape(128, 8 * E)),
        "gateb": np.asarray(inputs["L0_ffn_gate_b"], np.float32).reshape(1, E),
    }
    W1 = np.asarray(inputs["L0_ffn_W1"], np.float32)
    b1 = np.asarray(inputs["L0_ffn_b1"], np.float32)
    W2 = np.asarray(inputs["L0_ffn_W2"], np.float32)
    b2 = np.asarray(inputs["L0_ffn_b2"], np.float32)

    in_maps = []
    for c in range(8):
        h0 = 2 * c
        rows = []
        for blk in range(2):
            for h in (h0, h0 + 1):
                rows.append(blk * D + h * DK + perm)
        for h in (h0, h0 + 1):
            rows.append(2 * D + h * DK + np.arange(DK))
        rows = np.concatenate(rows)
        m = dict(common)
        xm = x[TLOC * c: TLOC * (c + 1)].reshape(NJ, 128, D).transpose(1, 0, 2)
        m["x_my"] = np.ascontiguousarray(xm.reshape(128, NJ * D))
        m["wo1c"] = np.ascontiguousarray(Wo1.T[128 * c: 128 * (c + 1)])
        wt = Wqkv1[rows].T.reshape(8, 128, 384).transpose(1, 0, 2)
        m["wqkv1T"] = np.ascontiguousarray(wt.reshape(128, 8 * 384))
        m["w1e"] = np.ascontiguousarray(W1[c].reshape(8, 128, DFF).transpose(1, 0, 2).reshape(128, 8 * DFF))
        m["b1e"] = b1[c].reshape(1, DFF)
        m["w2e"] = np.ascontiguousarray(W2[c].reshape(32, 128, D).transpose(1, 0, 2).reshape(128, 32 * D))
        m["b2e"] = b2[c].reshape(1, D)
        sel = np.zeros((128, E), np.float32)
        sel[:, c] = 1.0
        m["esel"] = sel
        in_maps.append(m)
    return in_maps


class Bld:
    def __init__(self, nc, debug):
        self.nc = nc
        self.debug = debug
        self.sem = {}
        self.cnt = {}
        self.t = {}
        self.stack = contextlib.ExitStack()
        self.dbg = {}

    def mksem(self, *names):
        for n in names:
            self.sem[n] = self.stack.enter_context(self.nc.semaphore(n))
            self.cnt[n] = 0

    def inc(self, inst, name, n=1):
        inst.then_inc(self.sem[name], n)
        self.cnt[name] += n
        return inst

    def wait(self, eng, name, val=None):
        eng.wait_ge(self.sem[name], self.cnt[name] if val is None else val)

    def waitv(self, eng, name, val):
        eng.wait_ge(self.sem[name], val)

    def fence(self, v, inst):
        self.inc(inst, "dvs", 1)
        self.wait(v, "dvs")
        return inst

    def vred(self, v, op, out, in_, axis):
        inst = getattr(v, op)(out, in_, axis)
        self.inc(inst, "dvs", 1)
        self.wait(v, "dvs")
        return inst


def build_nc(debug=False):
    nc = bass.Bass("TRN2", target_bir_lowering=False, detect_race_conditions=False)
    b = Bld(nc, debug)
    t = b.t

    def din(name, shape, dt=F32):
        t[name] = nc.dram_tensor(name, list(shape), dt, kind="ExternalInput").ap()

    def dint(name, shape, dt=BF16):
        t[name] = nc.dram_tensor(name, list(shape), dt, kind="Internal").ap()

    for name, shape, dt in [
        ("x_my", (128, NJ * D), F32), ("wqkv1T", (128, 8 * 384), F32),
        ("ropeC", (128, 4096), F32), ("ropeS", (128, 4096), F32), ("maskc", (128, 2048), BF16),
        ("wo1c", (128, D), F32), ("wqkv2", (128, 24 * D), F32), ("bqkv", (1, 3 * D), F32),
        ("wo2", (D, D), F32), ("bo", (1, D), F32),
        ("g1", (1, D), F32), ("b1ln", (1, D), F32), ("g2", (1, D), F32), ("b2ln", (1, D), F32),
        ("gatew", (128, 8 * E), F32), ("esel", (128, E), F32), ("gateb", (1, E), F32),
        ("w1e", (128, 8 * DFF), F32), ("b1e", (1, DFF), F32), ("w2e", (128, 32 * D), F32), ("b2e", (1, D), F32),
    ]:
        din(name, shape, dt)

    t["y"] = nc.dram_tensor("y", [TLOC, D], F32, kind="ExternalOutput").ap()

    for name, shape, dt in [
        ("n1t_in", (D, TLOC), BF16), ("n1t_ag", (8 * D, TLOC), BF16),
        ("og_in", (T, 128), BF16),
        ("xo_rs_in", (T, D), BF16), ("xo_rs_out", (TLOC, D), BF16),
        ("k2t_in", (D, TLOC), BF16), ("k2t_ag", (4 * D, TLOC), BF16),
        ("v2h_in", (H * TLOC, 128), BF16), ("v2h_ag", (4 * H * TLOC, 128), BF16),
        ("h2t_in", (D, TLOC), BF16), ("h2t_ag", (8 * D, TLOC), BF16),
        ("rs_in", (T, D), BF16), ("rs_out", (TLOC, D), BF16),
        ("x1_pk", (128, NJ * D), F32), ("u_pk", (128, NJ * D), F32), ("u2_pk", (128, NJ * D), F32),
    ]:
        dint(name, shape, dt)

    t["identf_d"] = nc.inline_tensor(np.eye(128, dtype=np.float32), "identf").ap()
    t["identb_d"] = nc.inline_tensor(np.eye(128, dtype=ml_dtypes.bfloat16), "identb").ap()

    if debug:
        def dout(name, shape, dt):
            b.dbg[name] = nc.dram_tensor("dbg_" + name, list(shape), dt, kind="ExternalOutput").ap()
        dout("og", (T, 128), BF16)
        dout("x1", (128, NJ * D), F32)
        dout("xo", (TLOC, D), BF16)
        dout("u", (128, NJ * D), F32)
        dout("u2", (128, NJ * D), F32)
        dout("h2t", (8 * D, TLOC), BF16)
        dout("rso", (TLOC, D), BF16)
        dout("ews", (128, 8), F32)
        dout("rsin", (T, D), BF16)
        dout("h2tin", (D, TLOC), BF16)
        dout("k2t", (4 * D, TLOC), BF16)
        dout("v2h", (4 * H * TLOC, 128), BF16)


    MB = 22496
    R1o, SCRo, CSTo, BIGo = MB, MB + 65536, MB + 77824, MB + 90112
    BIG_END = BIGo + 116736
    assert BIG_END <= 229344

    R1B = nc.alloc_sbuf_tensor_at("R1B", [128, 32768], BF16, offset=R1o).ap()
    SCRB = nc.alloc_sbuf_tensor_at("SCRB", [128, 6144], BF16, offset=SCRo).ap()
    CSTB = nc.alloc_sbuf_tensor_at("CSTB", [128, 6144], BF16, offset=CSTo).ap()
    BIGB = nc.alloc_sbuf_tensor_at("BIGB", [128, 58368], BF16, offset=BIGo).ap()

    def sb(name, shape, dt, off):
        for base, bo, bcols in ((R1B, R1o, 32768), (SCRB, SCRo, 6144), (CSTB, CSTo, 6144),
                                (BIGB, BIGo, 58368)):
            rel = off - bo
            if 0 <= rel < bcols * 2:
                break
        else:
            raise AssertionError(name)
        nb = int(np.prod(shape[1:])) * (4 if dt == F32 else 2)
        assert rel % 2 == 0 and rel + nb <= bcols * 2, (name, off)
        v = base[:, rel // 2: (rel + nb) // 2]
        if dt == F32:
            v = v.bitcast(F32)
        if shape[0] != 128:
            v = v[0: shape[0]]
        if len(shape) != 2:
            raise AssertionError(name)
        assert v.shape == tuple(shape), (name, v.shape, shape)
        t[name] = v

    sb("r1f", (128, 16384), F32, R1o)
    sb("ropeCt", (128, 4096), F32, R1o)
    sb("ropeSt", (128, 4096), F32, R1o + 16384)
    sb("slab0", (128, 8192), BF16, R1o + 32768)
    sb("slab1", (128, 8192), BF16, R1o + 49152)
    sb("Q2T", (128, 16384), BF16, R1o)
    sb("K2TS", (128, 16384), BF16, R1o + 32768)
    sb("W1G", (128, 32768), BF16, R1o)

    sb("xc", (128, 1024), F32, SCRo)
    sb("sq", (128, 1024), F32, SCRo + 4096)
    sb("msc", (128, 1024), F32, SCRo + 8192)
    sb("bfs", (128, 2048), BF16, SCRo + 4096)
    sb("mscb", (128, 2048), BF16, SCRo + 8192)
    sb("rowf0", (1, 2048), F32, SCRo)

    o = CSTo
    def cst(name, shape, dt, size):
        nonlocal o
        sb(name, shape, dt, o)
        o += size
    cst("identf", (128, 128), F32, 512)
    cst("identb", (128, 128), BF16, 256)
    cst("ones_r", (1, 128), BF16, 256)
    cst("vecc", (128, 32), F32, 128)
    cst("rstdb", (128, 16), F32, 64)
    cst("stat", (128, 32), F32, 128)
    cst("gwg", (128, 64), BF16, 128)
    cst("b1effc", (128, 32), F32, 128)
    cst("ewtt", (128, 32), F32, 128)
    cst("gsc", (128, 64), F32, 256)
    cst("gmk", (128, 64), BF16, 128)
    cst("lrec", (128, 8), F32, 32)
    cst("eselS", (128, 8), F32, 32)
    cst("beffq", (1, 3 * D), BF16, 6144)
    cst("beffg", (1, E), BF16, 64)
    assert o <= BIGo, o

    sb("xmy", (128, 16384), F32, BIGo)
    sb("n1tS", (128, 16384), BF16, BIGo + 65536)
    sb("QT", (128, 16384), BF16, BIGo)
    sb("KT", (128, 16384), BF16, BIGo + 32768)
    sb("VP", (128, 16640), BF16, BIGo + 65536)
    sb("WQ1", (128, 3072), BF16, BIGo + 98816)
    sb("PB0", (128, 1536), BF16, BIGo + 104960)
    sb("PB1", (128, 1536), BF16, BIGo + 108032)
    sb("obst", (128, 512), BF16, BIGo + 111104)
    sb("masks", (128, 2048), BF16, BIGo + 112128)

    sb("WO1", (128, 1024), BF16, BIGo)
    sb("oglh", (128, 128), BF16, BIGo + 2048)
    sb("ogld0", (128, 128), BF16, BIGo + 2304)
    sb("ogld1", (128, 128), BF16, BIGo + 2560)
    sb("cstg0", (128, 4096), BF16, BIGo + 4096)
    sb("cstg1", (128, 4096), BF16, BIGo + 12288)
    sb("csq", (128, 1024), F32, BIGo + 20480)

    sb("WQ2", (128, 24576), BF16, BIGo)
    sb("V2ST", (128, 2048), BF16, BIGo + 49152)
    sb("brow", (1, 4096), BF16, BIGo + 53248)
    sb("dsq", (128, 1024), F32, BIGo + 61440)
    sb("n1ib", (128, 1024), BF16, BIGo + 65536)
    sb("n1it", (128, 1024), BF16, BIGo + 67584)
    sb("qsb", (128, 512), BF16, BIGo + 69632)
    sb("csq2", (128, 1024), F32, BIGo + 70656)
    sb("b1c_bf", (128, 8), BF16, BIGo + 74752)

    sb("KB", (128, 8192), BF16, BIGo)
    sb("VHA", (128, 8192), BF16, BIGo + 16384)
    sb("VHB", (128, 8192), BF16, BIGo + 32768)
    sb("PB0e", (128, 1536), BF16, BIGo + 49152)
    sb("PB1e", (128, 1536), BF16, BIGo + 52224)
    sb("O2N", (128, 16384), BF16, BIGo + 57344)
    sb("WO2", (128, 8192), BF16, BIGo)
    sb("H2TS", (128, 16384), BF16, BIGo + 16384)
    sb("esq", (128, 1024), F32, BIGo + 49152)
    sb("o2lh", (128, 1024), BF16, BIGo + 53248)
    sb("bob", (1, D), BF16, BIGo + 55296)
    sb("n2e", (128, 1024), BF16, BIGo + 90112)

    sb("W2S", (128, 32768), BF16, BIGo)
    sb("HID", (128, 16384), BF16, BIGo + 65536)
    sb("H2TB", (128, 4096), BF16, BIGo + 98304)
    sb("MOEST", (128, 4096), BF16, BIGo + 106496)
    sb("b2eb", (1, D), BF16, BIGo + 114688)
    sb("b1effr", (1, DFF), F32, BIGo + 65536)
    sb("growb", (1, DFF), BF16, BIGo + 81920)
    sb("gbrow", (1, 64), BF16, BIGo + 92160)
    sb("b2c_bf", (128, 8), BF16, BIGo + 92288)
    sb("hsq", (128, 1024), F32, BIGo + 98304)
    sb("moef", (128, 1024), F32, BIGo + 102400)

    b.mksem("ld", "ldg", "ld2", "ld3", "ldv", "ldx", "lda", "ldb", "ldc", "cc", "dvs", "gp2", "dbg",
            "pe", "pe2", "pe3",
            "act", "act2",
            "dve", "dve2", "dve3", "dve4",
            "gp", "st")

    with b.stack, nc.Block() as blk:
        phase_P0(b, blk)
        phase_A0(b, blk)
        phase_A(b, blk)
        phase_B(b, blk)
        phase_C(b, blk)
        phase_D(b, blk)
        phase_E(b, blk)
        phase_G(b, blk)
        phase_H(b, blk)
        if debug:
            @blk.gpsimd
            def _(g):
                b.wait(g, "ldx")
                b.inc(g.dma_start(out=b.dbg["og"], in_=t["og_in"]), "dbg", 16)
                b.inc(g.dma_start(out=b.dbg["x1"], in_=t["x1_pk"]), "dbg", 16)
                b.inc(g.dma_start(out=b.dbg["xo"], in_=t["xo_rs_out"]), "dbg", 16)
                b.wait(g, "cc")
                b.wait(g, "ldc")
                b.inc(g.dma_start(out=b.dbg["u"], in_=t["u_pk"]), "dbg", 16)
                b.inc(g.dma_start(out=b.dbg["k2t"], in_=t["k2t_ag"]), "dbg", 16)
                b.inc(g.dma_start(out=b.dbg["v2h"], in_=t["v2h_ag"]), "dbg", 16)
                b.inc(g.dma_start(out=b.dbg["u2"], in_=t["u2_pk"]), "dbg", 16)
                b.inc(g.dma_start(out=b.dbg["h2t"], in_=t["h2t_ag"]), "dbg", 16)
                b.inc(g.dma_start(out=b.dbg["h2tin"], in_=t["h2t_in"]), "dbg", 16)
                b.inc(g.dma_start(out=b.dbg["rso"], in_=t["rs_out"]), "dbg", 16)
                b.inc(g.dma_start(out=b.dbg["ews"], in_=t["ewtt"][:, 0:8]), "dbg", 16)
                b.inc(g.dma_start(out=b.dbg["rsin"], in_=t["rs_in"]), "dbg", 16)
        final_wait(b, blk)
    return nc, b


def dump_debug(b, blk, items, wait):
    @blk.gpsimd
    def _(g):
        for s, v in wait:
            b.wait(g, s, v)
        for name, ap in items:
            b.inc(g.dma_start(out=b.dbg[name], in_=ap[:, :]), "dbg", 16)


def final_wait(b, blk):
    @blk.gpsimd
    def _(g):
        b.wait(g, "st")
        b.wait(g, "dbg")


def phase_P0(b, blk):
    nc, t = b.nc, b.t

    @blk.sync
    def _(s):
        b.inc(s.dma_start(out=t["identf"][:, :], in_=t["identf_d"]), "ld", 16)
        b.inc(s.dma_start(out=t["identb"][:, :], in_=t["identb_d"]), "ld", 16)
        b.inc(s.dma_start(out=t["masks"][:, :], in_=t["maskc"]), "ld", 16)
        b.inc(s.dma_start(out=t["ropeCt"][:, :], in_=t["ropeC"]), "ld", 16)
        b.inc(s.dma_start(out=t["ropeSt"][:, :], in_=t["ropeS"]), "ld", 16)
        b.inc(s.dma_start(out=t["rowf0"][0:1, 0:1024], in_=t["g1"]), "ld", 16)
        b.inc(s.dma_start(out=t["rowf0"][0:1, 1024:2048], in_=t["b1ln"]), "ld", 16)

    @blk.gpsimd
    def _(g):
        b.inc(g.dma_start(out=t["WQ1"][:, :], in_=t["wqkv1T"]), "ldg", 16)
        g.memset(t["ones_r"][:, :], 1.0)

    pe0, dv0 = b.cnt["pe"], b.cnt["dve"]
    ld_p1 = b.cnt["ld"]

    @blk.sync
    def _(s):
        b.waitv(s, "pe", pe0 + 16)
        b.inc(s.dma_start(out=t["rowf0"][0:1, 0:1024], in_=t["g2"]), "ldv", 16)
        b.inc(s.dma_start(out=t["rowf0"][0:1, 1024:2048], in_=t["b2ln"]), "ldv", 16)
    ld_p2 = b.cnt["ldv"]

    with nc.psum_tensor("pp0", [128, 128], F32) as pp0:
        @blk.tensor
        def _(tt):
            for i in range(32):
                v, k = i // 8, i % 8
                col = 1024 * (v % 2) + 128 * k
                if i < 16:
                    b.waitv(tt, "ld", ld_p1)
                else:
                    b.waitv(tt, "ldv", ld_p2)
                b.waitv(tt, "dve", dv0 + i)
                b.inc(tt.transpose(pp0[:, 0:1], t["rowf0"][0:1, col: col + 128], t["identf"][0:1, 0:1]), "pe", 1)

        @blk.vector
        def _(v):
            for i in range(32):
                b.waitv(v, "pe", pe0 + i + 1)
                b.inc(v.tensor_copy(t["vecc"][:, i: i + 1], pp0[:, 0:1]), "dve", 1)


def phase_A0(b, blk):
    nc, t = b.nc, b.t

    @blk.sync
    def _(s):
        b.inc(s.dma_start(out=t["xmy"][:, :], in_=t["x_my"]), "ldx", 16)

    pe0, dv0, ac0 = b.cnt["pe"], b.cnt["dve"], b.cnt["act"]
    pa0_cm = nc.psum_tensor("pa0", [128, 2048], BF16)
    pa0 = pa0_cm.__enter__()

    @blk.vector
    def _(v):
        b.wait(v, "ld")
        b.wait(v, "ldx")
        for j in range(NJ):
            xj = t["xmy"][:, 1024 * j: 1024 * (j + 1)]
            b.vred(v, "reduce_sum", t["stat"][:, j: j + 1], xj, AX.X)
            b.fence(v, v.tensor_scalar(t["stat"][:, 16 + j: 16 + j + 1], t["stat"][:, j: j + 1],
                                       float(INV_D), None, ALU.mult))
            b.fence(v, v.tensor_scalar(t["xc"][:, :], xj, t["stat"][:, 16 + j: 16 + j + 1], None, ALU.subtract))
            b.fence(v, v.tensor_tensor(t["sq"][:, :], t["xc"][:, :], t["xc"][:, :], ALU.mult))
            b.vred(v, "reduce_sum", t["stat"][:, j: j + 1], t["sq"][:, :], AX.X)
            b.inc(v.tensor_scalar(t["stat"][:, j: j + 1], t["stat"][:, j: j + 1], float(INV_D), float(EPS),
                                  ALU.mult, ALU.add), "dve", 1)
            b.waitv(v, "act", ac0 + j + 1)
            b.waitv(v, "pe", pe0 + 8 * j)
            b.inc(v.tensor_scalar(t["bfs"][:, 0:1024], t["xc"][:, :], t["rstdb"][:, j: j + 1], None, ALU.mult),
                  "dve", 1)
            b.waitv(v, "pe", pe0 + 8 * (j + 1))
            for k in range(8):
                b.inc(v.tensor_copy(t["n1tS"][:, 2048 * k + 128 * j: 2048 * k + 128 * (j + 1)],
                                    pa0[:, 128 * k: 128 * (k + 1)]), "dve", 1)

    @blk.scalar
    def _(sc):
        for j in range(NJ):
            b.waitv(sc, "dve", dv0 + j * 10 + 1)
            sc.activation(t["stat"][:, 16 + j: 16 + j + 1], t["stat"][:, j: j + 1], AF.Ln)
            b.inc(sc.activation(t["rstdb"][:, j: j + 1], t["stat"][:, 16 + j: 16 + j + 1], AF.Exp,
                                scale=np.float32(-0.5)), "act", 1)

    @blk.tensor
    def _(tt):
        for j in range(NJ):
            b.waitv(tt, "dve", dv0 + j * 10 + 2)
            if j >= 1:
                b.waitv(tt, "dve", dv0 + j * 10)
            for k in range(8):
                b.inc(tt.transpose(pa0[:, 128 * k: 128 * (k + 1)],
                                   t["bfs"][:, 128 * k: 128 * (k + 1)], t["identb"][:, :]), "pe", 1)

    pa0_cm.__exit__(None, None, None)

    @blk.sync
    def _(s):
        b.wait(s, "dve")
        for k in range(8):
            b.inc(s.dma_start(out=t["n1t_in"][128 * k: 128 * (k + 1), :],
                              in_=t["n1tS"][:, 2048 * k: 2048 * (k + 1)]), "ld2", 16)

    @blk.gpsimd
    def _(g):
        b.wait(g, "ld2")
        b.inc(g.collective_compute("AllGather", ALU.bypass, replica_groups=GROUPS8,
                                   ins=[t["n1t_in"].opt()], outs=[t["n1t_ag"].opt()]), "cc", 1)


def phase_A(b, blk):
    nc, t = b.nc, b.t
    slabs = [t["slab0"], t["slab1"]]
    NU = 128

    cc_need = b.cnt["cc"]
    ld2_0, ld3_0 = b.cnt["ld2"], b.cnt["ld3"]
    pe0, pe2_0 = b.cnt["pe"], b.cnt["pe2"]
    dv2_0, dv3_0, dv4_0 = b.cnt["dve2"], b.cnt["dve3"], b.cnt["dve4"]

    @blk.vector
    def _(v):
        b.wait(v, "ld2")
        v.memset(t["VP"][:, :], 0.0)
        b.inc(v.memset(t["VP"].rearrange("p (n f) -> p n f", f=65)[:, :, 64:65], 1.0), "dve2", 1)

    @blk.sync
    def _(s):
        b.waitv(s, "cc", cc_need)
        for g in range(16):
            r, half = g // 2, g % 2
            if g >= 2:
                b.waitv(s, "pe", pe0 + (g - 1) * 8)
            sem = "ld2" if g % 2 == 0 else "ld3"
            for k in range(8):
                b.inc(s.dma_start(out=slabs[g % 2][:, 1024 * k: 1024 * (k + 1)],
                                  in_=t["n1t_ag"][1024 * r + 128 * k: 1024 * r + 128 * (k + 1),
                                                  1024 * half: 1024 * (half + 1)]), sem, 16)

    pcms = [nc.psum_tensor("paq0", [128, 384], F32), nc.psum_tensor("paq1", [128, 384], F32),
            nc.psum_tensor("pat0", [128, 256], BF16), nc.psum_tensor("pat1", [128, 256], BF16)]
    hs = [cm.__enter__() for cm in pcms]
    pq, pt = hs[0:2], hs[2:4]

    def emit_transposes(tt, up):
        b.waitv(tt, "dve3", dv3_0 + up + 1)
        if up >= 2:
            b.waitv(tt, "dve4", dv4_0 + up - 1)
        ptu = pt[up % 2]
        sl = 512 * (up % 2)
        tt.transpose(ptu[:, 0:128], t["bfs"][:, sl: sl + 128], t["identb"][:, :])
        b.inc(tt.transpose(ptu[:, 128:256], t["bfs"][:, sl + 128: sl + 256], t["identb"][:, :]), "pe2", 1)

    @blk.tensor
    def _(tt):
        b.wait(tt, "ldg")
        b.waitv(tt, "dve2", dv2_0 + 1)
        for u in range(NU):
            g, j2 = u // 8, u % 8
            if g % 2 == 0:
                b.waitv(tt, "ld2", ld2_0 + (g // 2 + 1) * 128)
            else:
                b.waitv(tt, "ld3", ld3_0 + (g // 2 + 1) * 128)
            if u >= 2:
                b.waitv(tt, "dve2", dv2_0 + 1 + (u - 1))
            pqu = pq[u % 2]
            for k in range(8):
                mm = tt.matmul(pqu[:, :], slabs[g % 2][:, 1024 * k + 128 * j2: 1024 * k + 128 * (j2 + 1)],
                               t["WQ1"][:, 384 * k: 384 * (k + 1)], start=(k == 0), stop=(k == 7))
            b.inc(mm, "pe", 1)
            if u >= 1:
                emit_transposes(tt, u - 1)
        emit_transposes(tt, NU - 1)

    @blk.vector
    def _(v):
        for u in range(NU):
            g, j2 = u // 8, u % 8
            r, half = g // 2, g % 2
            tile = 16 * r + 8 * half + j2
            npos = tile % 64
            pqu = pq[u % 2]
            b.waitv(v, "pe", pe0 + u + 1)
            if u >= 2:
                b.waitv(v, "pe2", pe2_0 + u - 1)
            cosT = t["ropeCt"][:, 64 * npos: 64 * (npos + 1)]
            sinT = t["ropeSt"][:, 64 * npos: 64 * (npos + 1)]
            cos4 = cosT.unsqueeze(1).broadcast_to((128, 4, 64))
            sin32 = sinT[:, 0:32].unsqueeze(1).broadcast_to((128, 4, 32))
            qk4 = pqu.rearrange("p (a f) -> p a f", f=64)[:, 0:4, :]
            qk42 = pqu.rearrange("p (a c f) -> p a c f", c=2, f=32)[:, 0:4, :, :]
            mv = t["msc"].rearrange("p (a f) -> p a f", f=64)[:, 0:4, :]
            rv = t["xc"].rearrange("p (a c f) -> p a c f", c=2, f=32)[:, 0:4, :, :]
            b.fence(v, v.tensor_tensor(mv, qk4, cos4, ALU.mult))
            b.fence(v, v.scalar_tensor_tensor(rv[:, :, 0, :], qk42[:, :, 1, :], -1.0, sin32, ALU.mult, ALU.mult))
            b.fence(v, v.scalar_tensor_tensor(rv[:, :, 1, :], qk42[:, :, 0, :], 1.0, sin32, ALU.mult, ALU.mult))
            sl = 512 * (u % 2)
            b.inc(v.tensor_add(t["bfs"][:, sl: sl + 256], t["msc"][:, 0:256], t["xc"][:, 0:256]), "dve3", 1)
            b.inc(v.tensor_copy(t["VP"].rearrange("p (n f) -> p n f", f=65)[:, 2 * tile: 2 * tile + 2, 0:64],
                                pqu.rearrange("p (n f) -> p n f", f=64)[:, 4:6, :]), "dve2", 1)
            if u >= 1:
                copy_trans(b, v, pt, pe2_0, dv4_0, u - 1)
        copy_trans(b, v, pt, pe2_0, dv4_0, NU - 1)

    for cm in reversed(pcms):
        cm.__exit__(None, None, None)


def copy_trans(b, v, pt, pe2_0, dv4_0, up):
    t = b.t
    gp_, j2p = up // 8, up % 8
    rp, halfp = gp_ // 2, gp_ % 2
    tp = 16 * rp + 8 * halfp + j2p
    b.waitv(v, "pe2", pe2_0 + up + 1)
    ptu = pt[up % 2]
    v.tensor_copy(t["QT"][:, 128 * tp: 128 * (tp + 1)], ptu[:, 0:128])
    b.inc(v.tensor_copy(t["KT"][:, 128 * tp: 128 * (tp + 1)], ptu[:, 128:256]), "dve4", 1)


def phase_B(b, blk):
    nc, t = b.nc, b.t

    units = []
    for bt in range(2):
        for i in range(2):
            for qb in range(16):
                nch = 4 * (qb + 1)
                groups = [list(range(g, min(g + 3, nch))) for g in range(0, nch, 3)]
                units.append((bt, i, qb, nch, groups))
    NUB = len(units)
    gg_of_unit = []
    tot = 0
    for (_, _, _, nch, groups) in units:
        gg_of_unit.append(tot)
        tot += len(groups)
    NGG = tot

    pe0 = b.cnt["pe"]
    pe2_0 = b.cnt["pe2"]
    pe3_0 = b.cnt["pe3"]
    ac0 = b.cnt["act"]
    dv0 = b.cnt["dve"]
    b.cnt_dv0_B = dv0
    dv2_0 = b.cnt["dve2"]
    dv3_0 = b.cnt["dve3"]
    ld2_0 = b.cnt["ld2"]
    PB = [t["PB0"], t["PB1"]]

    def kv_ap(bt, n):
        return t["KT"][:, 8192 * bt + 128 * n: 8192 * bt + 128 * (n + 1)]

    def masked_j(qb, n):
        j = n - 4 * qb
        return j if 0 <= j < 4 else None

    mask_cnt_before = []
    mk = 0
    unit_mask_counts = []
    for (bt, i, qb, nch, groups) in units:
        mask_cnt_before.append(mk)
        nm = sum(1 for n in range(nch) if masked_j(qb, n) is not None)
        unit_mask_counts.append(nm)
        mk += nm

    with nc.psum_tensor("bsa", [128, 1536], F32) as bsa,          nc.psum_tensor("bsb", [128, 1536], F32) as bsb,          nc.psum_tensor("bop", [65, 512], F32) as bop,          nc.psum_tensor("btr", [128, 264], BF16) as btr:
        SBUFS = [bsa, bsb]

        @blk.tensor
        def _(tt):
            for u, (bt, i, qb, nch, groups) in enumerate(units):
                base = gg_of_unit[u]
                qcol = 8192 * bt + 512 * qb
                rows = slice(64 * i, 64 * i + 64)
                for g, grp in enumerate(groups):
                    gg = base + g
                    if gg >= 2:
                        b.waitv(tt, "act", ac0 + gg - 1)
                    sb_ = SBUFS[gg % 2]
                    for _ in range(8):
                        tt.matmul(sb_[:, 0:64], t["KT"][:, 0:128], t["QT"][:, 0:64],
                                  start=True, stop=True)
                    for k, n in enumerate(grp):
                        mm = tt.matmul(sb_[:, 512 * k: 512 * (k + 1)], kv_ap(bt, n)[rows, :].copy() if False else t["KT"][64 * i: 64 * i + 64, 8192 * bt + 128 * n: 8192 * bt + 128 * (n + 1)],
                                       t["QT"][64 * i: 64 * i + 64, qcol: qcol + 512],
                                       start=True, stop=True)
                    b.inc(mm, "pe", 1)
                    if g == 0:
                        b.waitv(tt, "dve2", dv2_0 + u)
                    if g >= 1:
                        emit_av(b, tt, units, u, g - 1, bop, PB, ac0, dv0, gg_of_unit, mask_cnt_before)
                emit_av(b, tt, units, u, len(groups) - 1, bop, PB, ac0, dv0, gg_of_unit, mask_cnt_before)
                b.waitv(tt, "dve2", dv2_0 + u + 1)
                for tch in range(4):
                    mm = tt.transpose(btr[:, 66 * tch: 66 * tch + 65],
                                      t["mscb"][0:65, 128 * tch: 128 * (tch + 1)],
                                      t["identb"][0:65, 0:65])
                b.inc(mm, "pe3", 1)

        @blk.scalar
        def _(sc):
            for u, (bt, i, qb, nch, groups) in enumerate(units):
                base = gg_of_unit[u]
                for g, grp in enumerate(groups):
                    gg = base + g
                    b.waitv(sc, "pe", pe0 + gg + 1)
                    if gg >= 2:
                        b.waitv(sc, "pe2", pe2_0 + gg - 1)
                    sb_ = SBUFS[gg % 2]
                    b.inc(sc.activation(PB[gg % 2][:, 0: 512 * len(grp)], sb_[:, 0: 512 * len(grp)],
                                        AF.Exp, bias=C0, scale=INV8), "act", 1)

        @blk.vector
        def _(v):
            for u, (bt, i, qb, nch, groups) in enumerate(units):
                base = gg_of_unit[u]
                nm = 0
                for g, grp in enumerate(groups):
                    gg = base + g
                    if any(masked_j(qb, n) is not None for n in grp):
                        b.waitv(v, "act", ac0 + gg + 1)
                        for k, n in enumerate(grp):
                            j = masked_j(qb, n)
                            if j is not None:
                                sl = PB[gg % 2][:, 512 * k: 512 * (k + 1)]
                                b.inc(v.tensor_tensor(sl, sl, t["masks"][:, 512 * j: 512 * (j + 1)], ALU.mult),
                                      "dve", 1)
                                nm += 1
                assert nm == unit_mask_counts[u]
                b.waitv(v, "pe2", pe2_0 + gg_of_unit[u] + len(groups))
                b.waitv(v, "pe3", pe3_0 + u)
                b.inc(v.tensor_copy(t["mscb"][0:65, 0:512], bop[:, :]), "dve2", 1)
                b.waitv(v, "pe3", pe3_0 + u + 1)
                for tch in range(4):
                    b.fence(v, v.reciprocal(t["lrec"][:, tch: tch + 1], btr[:, 66 * tch + 64: 66 * tch + 65]))
                if u >= 2:
                    b.waitv(v, "ld2", ld2_0 + (u - 1) * 16)
                for tch in range(4):
                    i3 = v.tensor_scalar(t["obst"][:, 256 * (u % 2) + 64 * tch: 256 * (u % 2) + 64 * (tch + 1)],
                                         btr[:, 66 * tch: 66 * tch + 64], t["lrec"][:, tch: tch + 1], None, ALU.mult)
                b.inc(i3, "dve3", 1)

        @blk.sync
        def _(s):
            for u, (bt, i, qb, nch, groups) in enumerate(units):
                b.waitv(s, "dve3", dv3_0 + u + 1)
                dst = t["og_in"][8192 * bt + 512 * qb: 8192 * bt + 512 * (qb + 1), 64 * i: 64 * i + 64]
                src_ = t["obst"][:, 256 * (u % 2): 256 * (u % 2 + 1)].rearrange("p (tc f) -> p tc f", f=64)
                b.inc(s.dma_start(out=dst.rearrange("(tc p) f -> p tc f", p=128), in_=src_), "ld2", 16)



def phase_C(b, blk):
    nc, t = b.nc, b.t
    lda0 = b.cnt["lda"]
    ldb0 = b.cnt["ldb"]
    ldc0 = b.cnt["ldc"]
    ld2f = b.cnt["ld2"]
    pe0 = b.cnt["pe"]
    pe2_0 = b.cnt["pe2"]
    dv0 = b.cnt["dve"]
    dv2_0 = b.cnt["dve2"]
    cc0 = b.cnt["cc"]
    ogld = [t["ogld0"], t["ogld1"]]
    cstg = [t["cstg0"], t["cstg1"]]

    @blk.gpsimd
    def _(g):
        b.waitv(g, "ld2", ld2f)
        b.inc(g.dma_start(out=t["WO1"][:, :], in_=t["wo1c"]), "ldg", 16)
    ldg_w = b.cnt["ldg"]

    tiles = [(kp, cp, mm_) for kp in range(4) for cp in range(8) for mm_ in range(4)]

    with nc.psum_tensor("ctr0", [128, 128], BF16) as ctr0,          nc.psum_tensor("ctr1", [128, 128], BF16) as ctr1,          nc.psum_tensor("cxo0", [128, 1024], F32) as cxo0,          nc.psum_tensor("cxo1", [128, 1024], F32) as cxo1:
        ctr = [ctr0, ctr1]
        cxo = [cxo0, cxo1]

        @blk.sync
        def _(s):
            b.waitv(s, "ld2", ld2f)
            for u, (kp, cp, mm_) in enumerate(tiles):
                row = 2048 * cp + 512 * kp + 128 * mm_
                if u >= 2:
                    b.waitv(s, "pe", pe0 + u - 1)
                sem = "lda" if u % 2 == 0 else "ldb"
                b.inc(s.dma_start(out=ogld[u % 2][:, :], in_=t["og_in"][row: row + 128, :]), sem, 16)
                if u % 4 == 3:
                    grp = u // 4
                    kpg, cpg = grp // 8, grp % 8
                    b.waitv(s, "dve2", dv2_0 + (grp + 1) * 4)
                    dst = t["xo_rs_in"][4096 * kpg + 512 * cpg: 4096 * kpg + 512 * (cpg + 1), :]
                    b.inc(s.dma_start(out=dst.rearrange("(m p) f -> p m f", p=128),
                                      in_=cstg[grp % 2].rearrange("p (m f) -> p m f", f=1024)), "ldc", 16)

        @blk.tensor
        def _(tt):
            b.waitv(tt, "ldg", ldg_w)
            for u, (kp, cp, mm_) in enumerate(tiles):
                if u % 2 == 0:
                    b.waitv(tt, "lda", lda0 + (u // 2 + 1) * 16)
                else:
                    b.waitv(tt, "ldb", ldb0 + (u // 2 + 1) * 16)
                if u >= 2:
                    b.waitv(tt, "dve", dv0 + u - 1)
                b.inc(tt.transpose(ctr[u % 2][:, :], ogld[u % 2][:, :], t["identb"][:, :]), "pe", 1)
                b.waitv(tt, "dve", dv0 + u + 1)
                if u >= 2:
                    b.waitv(tt, "dve2", dv2_0 + u - 1)
                mm = tt.matmul(cxo[u % 2][:, 0:512], t["oglh"][:, :], t["WO1"][:, 0:512], start=True, stop=True)
                mm = tt.matmul(cxo[u % 2][:, 512:1024], t["oglh"][:, :], t["WO1"][:, 512:1024], start=True, stop=True)
                b.inc(mm, "pe2", 1)

        @blk.vector
        def _(v):
            for u, (kp, cp, mm_) in enumerate(tiles):
                b.waitv(v, "pe", pe0 + u + 1)
                if u >= 1:
                    b.waitv(v, "pe2", pe2_0 + u)
                b.inc(v.tensor_copy(t["oglh"][:, :], ctr[u % 2][:, :]), "dve", 1)
                b.waitv(v, "pe2", pe2_0 + u + 1)
                gpar = (u // 4) % 2
                if u >= 8:
                    b.waitv(v, "ldc", ldc0 + (u // 4 - 1) * 16)
                b.inc(v.tensor_copy(cstg[gpar][:, 1024 * mm_: 1024 * (mm_ + 1)], cxo[u % 2][:, :]), "dve2", 1)


    @blk.gpsimd
    def _(g):
        for kp in range(4):
            b.waitv(g, "ldc", ldc0 + (8 * kp + 8) * 16)
            b.inc(g.collective_compute("ReduceScatter", ALU.add, replica_groups=GROUPS8,
                                       ins=[t["xo_rs_in"][4096 * kp: 4096 * (kp + 1), :].opt()],
                                       outs=[t["xo_rs_out"][512 * kp: 512 * (kp + 1), :].opt()]), "cc", 1)

    @blk.sync
    def _(s):
        b.inc(s.dma_start(out=t["r1f"][:, :], in_=t["x_my"]), "ldx", 16)
    ldx_f = b.cnt["ldx"]

    dvf0 = b.cnt["dve3"]
    acf0 = b.cnt["act2"]

    @blk.gpsimd
    def _(g):
        for j in range(NJ):
            kp = j // 4
            b.waitv(g, "cc", cc0 + kp + 1)
            if j >= 2:
                b.waitv(g, "dve3", dvf0 + (j - 1) * 2)
            xb = t["xc"] if j % 2 == 0 else t["msc"]
            row = 128 * j
            b.inc(g.dma_start(out=xb[:, :], in_=t["xo_rs_out"][row: row + 128, :]),
                  "gp" if j % 2 == 0 else "gp2", 16)

    @blk.vector
    def _(v):
        b.waitv(v, "ldx", ldx_f)
        for j in range(NJ):
            xb = t["xc"] if j % 2 == 0 else t["msc"]
            sem = "gp" if j % 2 == 0 else "gp2"
            b.waitv(v, sem, b.cnt[sem] - 16 * (NJ // 2 - 1 - j // 2))
            b.vred(v, "reduce_sum", t["stat"][:, j: j + 1], xb[:, :], AX.X)
            b.fence(v, v.tensor_scalar(t["stat"][:, 16 + j: 16 + j + 1], t["stat"][:, j: j + 1],
                                       float(INV_D), None, ALU.mult))
            b.fence(v, v.tensor_scalar(t["sq"][:, :], xb[:, :], t["stat"][:, 16 + j: 16 + j + 1],
                                       None, ALU.subtract))
            b.fence(v, v.tensor_tensor(t["csq"][:, :], t["sq"][:, :], t["sq"][:, :], ALU.mult))
            b.vred(v, "reduce_sum", t["stat"][:, j: j + 1], t["csq"][:, :], AX.X)
            b.inc(v.tensor_scalar(t["stat"][:, j: j + 1], t["stat"][:, j: j + 1], float(INV_D), float(EPS),
                                  ALU.mult, ALU.add), "dve3", 1)
            b.waitv(v, "act2", acf0 + j + 1)
            b.fence(v, v.tensor_scalar(t["sq"][:, :], t["sq"][:, :], t["rstdb"][:, j: j + 1], None, ALU.mult))
            b.inc(v.tensor_tensor(t["r1f"][:, 1024 * j: 1024 * (j + 1)], t["r1f"][:, 1024 * j: 1024 * (j + 1)],
                                  t["sq"][:, :], ALU.add), "dve3", 1)

    @blk.scalar
    def _(sc):
        for j in range(NJ):
            b.waitv(sc, "dve3", dvf0 + 2 * j + 1)
            sc.activation(t["stat"][:, 16 + j: 16 + j + 1], t["stat"][:, j: j + 1], AF.Ln)
            b.inc(sc.activation(t["rstdb"][:, j: j + 1], t["stat"][:, 16 + j: 16 + j + 1], AF.Exp,
                                scale=np.float32(-0.5)), "act2", 1)

    @blk.sync
    def _(s):
        b.wait(s, "dve3")
        b.inc(s.dma_start(out=t["x1_pk"], in_=t["r1f"][:, :]), "ldx", 16)


def phase_D(b, blk):
    nc, t = b.nc, b.t
    dvf = b.cnt["dve3"]
    ldxf = b.cnt["ldx"]

    @blk.gpsimd
    def _(g):
        b.waitv(g, "dve3", dvf)
        b.inc(g.dma_start(out=t["WQ2"][:, :], in_=t["wqkv2"]), "ldg", 16)
        b.inc(g.dma_start(out=t["brow"][0:1, 0:3072], in_=t["bqkv"]), "ldg", 16)
    ldg_d = b.cnt["ldg"]

    pe_be0 = b.cnt["pe"]
    dv_be0 = b.cnt["dve"]
    with nc.psum_tensor("dbe", [1, 512], F32) as dbe:
        @blk.vector
        def _(v):
            b.inc(v.tensor_copy(t["b1c_bf"][:, 0:8], t["vecc"][:, 8:16]), "dve", 1)

        @blk.tensor
        def _(tt):
            b.waitv(tt, "ldg", ldg_d)
            b.waitv(tt, "dve", dv_be0 + 1)
            for cb in range(6):
                if cb >= 1:
                    b.waitv(tt, "dve", dv_be0 + 1 + cb)
                for k in range(8):
                    tt.matmul(dbe[:, :], t["b1c_bf"][:, k: k + 1],
                              t["WQ2"][:, 3072 * k + 512 * cb: 3072 * k + 512 * (cb + 1)],
                              start=(k == 0), stop=False)
                mm = tt.matmul(dbe[:, :], t["ones_r"][0:1, 0:1], t["brow"][0:1, 512 * cb: 512 * (cb + 1)],
                               start=False, stop=True)
                b.inc(mm, "pe", 1)

        @blk.vector
        def _(v):
            for cb in range(6):
                b.waitv(v, "pe", pe_be0 + cb + 1)
                b.inc(v.tensor_copy(t["beffq"][0:1, 512 * cb: 512 * (cb + 1)], dbe[0:1, :]), "dve", 1)
            for k in range(8):
                i2 = v.tensor_scalar(t["WQ2"][:, 3072 * k: 3072 * (k + 1)], t["WQ2"][:, 3072 * k: 3072 * (k + 1)],
                                     t["vecc"][:, k: k + 1], None, ALU.mult)
            b.inc(i2, "dve2", 1)
            v.memset(t["V2ST"][:, :], 0.0)
            b.inc(v.memset(t["V2ST"].rearrange("p (h f) -> p h f", f=128)[:, :, 64:65], 1.0), "dve2", 1)
    dv2_w = b.cnt["dve2"]

    pe0 = b.cnt["pe"]
    pe2_0 = b.cnt["pe2"]
    pe3_0 = b.cnt["pe3"]
    ac0 = b.cnt["act"]
    ac2_0 = b.cnt["act2"]
    dv0 = b.cnt["dve"]
    dv3_0 = b.cnt["dve3"]
    dv4_0 = b.cnt["dve4"]
    lda0, ldb0 = b.cnt["lda"], b.cnt["ldb"]
    ldc0 = b.cnt["ldc"]
    ld2_0, ld3_0 = b.cnt["ld2"], b.cnt["ld3"]

    with nc.psum_tensor("pdtr", [128, 1024], BF16) as pdtr, \
         nc.psum_tensor("pdm0", [128, 512], F32) as pdm0, \
         nc.psum_tensor("pdm1", [128, 512], F32) as pdm1, \
         nc.psum_tensor("pdq0", [128, 512], BF16) as pdq0, \
         nc.psum_tensor("pdq1", [128, 512], BF16) as pdq1:
        pdm = [pdm0, pdm1]
        pdq = [pdq0, pdq1]

        @blk.sync
        def _(s):
            b.waitv(s, "ldx", ldxf)
            for j in range(NJ):
                sem = "lda" if j % 2 == 0 else "ldb"
                if j >= 2:
                    b.waitv(s, "dve", dv0 + (j - 2) * 9 + 1)
                b.inc(s.dma_start(out=(t["xc"] if j % 2 == 0 else t["dsq"])[:, :],
                                  in_=t["x1_pk"][:, 1024 * j: 1024 * (j + 1)]), sem, 16)
                if j >= 1:
                    b.waitv(s, "dve", dv0 + (j - 1) * 9 + 2)
                    b.inc(s.dma_start(out=t["u_pk"][:, 1024 * (j - 1): 1024 * j], in_=t["msc"][:, :]), "ldc", 16)
                    b.waitv(s, "dve", dv0 + 9 * (j - 1) + 9)
                    dstv = t["v2h_in"].rearrange("(h jj p) f -> p h jj f", h=H, p=128)[:, :, j - 1, :]
                    b.inc(s.dma_start(out=dstv, in_=t["V2ST"].rearrange("p (h f) -> p h f", f=128)), "ld2", 16)
            b.waitv(s, "dve", dv0 + (NJ - 1) * 9 + 2)
            b.inc(s.dma_start(out=t["u_pk"][:, 1024 * (NJ - 1): 1024 * NJ], in_=t["msc"][:, :]), "ldc", 16)
            b.waitv(s, "dve", dv0 + 9 * NJ)
            dstv = t["v2h_in"].rearrange("(h jj p) f -> p h jj f", h=H, p=128)[:, :, NJ - 1, :]
            b.inc(s.dma_start(out=dstv, in_=t["V2ST"].rearrange("p (h f) -> p h f", f=128)), "ld2", 16)
            b.waitv(s, "dve4", dv4_0 + 4 * NJ)
            for k in range(8):
                b.inc(s.dma_start(out=t["k2t_in"][128 * k: 128 * (k + 1), :],
                                  in_=t["K2TS"][:, 2048 * k: 2048 * (k + 1)]), "ld3", 16)

        @blk.vector
        def _(v):
            for j in range(NJ):
                xb = t["xc"] if j % 2 == 0 else t["dsq"]
                sqs = t["sq"]
                st_ = t["stat"]
                sem = "lda" if j % 2 == 0 else "ldb"
                b.waitv(v, sem, (lda0 if j % 2 == 0 else ldb0) + (j // 2 + 1) * 16)
                if j >= 1:
                    b.waitv(v, "ldc", ldc0 + j * 16)
                b.vred(v, "reduce_sum", st_[:, 0:1], xb[:, :], AX.X)
                b.fence(v, v.tensor_scalar(st_[:, 1:2], st_[:, 0:1], float(INV_D), None, ALU.mult))
                b.fence(v, v.tensor_scalar(sqs[:, :], xb[:, :], st_[:, 1:2], None, ALU.subtract))
                b.fence(v, v.tensor_tensor(t["csq2"][:, :], sqs[:, :], sqs[:, :], ALU.mult))
                b.vred(v, "reduce_sum", st_[:, 0:1], t["csq2"][:, :], AX.X)
                b.inc(v.tensor_scalar(st_[:, 0:1], st_[:, 0:1], float(INV_D), float(EPS),
                                      ALU.mult, ALU.add), "dve3", 1)
                b.waitv(v, "act", ac0 + j + 1)
                b.inc(v.tensor_scalar(t["msc"][:, :], sqs[:, :], t["rstdb"][:, 0:1], None, ALU.mult), "dve", 1)
                b.fence(v, v.tensor_copy(t["gsc"][:, 0:1], t["msc"][:, 1023:1024]))
                b.vred(v, "reduce_sum", st_[:, 2:3], t["msc"][:, :], AX.X)
                b.fence(v, v.tensor_scalar(st_[:, 3:4], st_[:, 2:3], float(INV_D), None, ALU.mult))
                b.fence(v, v.tensor_scalar(sqs[:, :], t["msc"][:, :], st_[:, 3:4], None, ALU.subtract))
                b.fence(v, v.tensor_tensor(t["csq2"][:, :], sqs[:, :], sqs[:, :], ALU.mult))
                b.vred(v, "reduce_sum", st_[:, 2:3], t["csq2"][:, :], AX.X)
                b.inc(v.tensor_scalar(st_[:, 2:3], st_[:, 2:3], float(INV_D), float(EPS),
                                      ALU.mult, ALU.add), "dve3", 1)
                b.waitv(v, "act2", ac2_0 + j + 1)
                if j >= 1:
                    b.waitv(v, "pe", pe0 + j)
                b.inc(v.tensor_scalar(t["n1ib"][:, :], sqs[:, :], t["rstdb"][:, 1:2], None, ALU.mult), "dve", 1)
                b.waitv(v, "pe", pe0 + j + 1)
                if j >= 1:
                    b.waitv(v, "pe2", pe2_0 + 6 * j)
                b.inc(v.tensor_copy(t["n1it"][:, :], pdtr[:, :]), "dve", 1)
                def qkcopy(cb2):
                    b.waitv(v, "pe3", pe3_0 + j * 4 + cb2 + 1)
                    pq = pdq[cb2 % 2]
                    dstb = t["Q2T"] if cb2 < 2 else t["K2TS"]
                    for s4 in range(4):
                        kp = 4 * (cb2 % 2) + s4
                        i5 = v.tensor_copy(dstb[:, 2048 * kp + 128 * j: 2048 * kp + 128 * (j + 1)],
                                           pq[:, 128 * s4: 128 * (s4 + 1)])
                    b.inc(i5, "dve4", 1)
                for cb in range(6):
                    b.waitv(v, "pe2", pe2_0 + 6 * j + cb + 1)
                    pm = pdm[cb % 2]
                    if cb < 4:
                        nprev = j * 4 + cb
                        if nprev >= 1:
                            b.waitv(v, "pe3", pe3_0 + nprev)
                        b.inc(v.tensor_copy(t["qsb"][:, :], pm[:, :]), "dve", 1)
                    else:
                        hb = 8 * (cb - 4)
                        if cb == 4 and j >= 1:
                            b.waitv(v, "ld2", ld2_0 + 16 * j)
                        b.inc(v.tensor_copy(
                            t["V2ST"].rearrange("p (h f) -> p h f", f=128)[:, hb: hb + 8, 0:64],
                            pm.rearrange("p (h f) -> p h f", f=64)[:, 0:8, :]), "dve", 1)
                    if 1 <= cb <= 4:
                        qkcopy(cb - 1)

        @blk.scalar
        def _(sc):
            for j in range(NJ):
                b.waitv(sc, "dve3", dv3_0 + 2 * j + 1)
                sc.activation(t["stat"][:, 1:2], t["stat"][:, 0:1], AF.Ln)
                b.inc(sc.activation(t["rstdb"][:, 0:1], t["stat"][:, 1:2], AF.Exp, scale=np.float32(-0.5)),
                      "act", 1)
                b.waitv(sc, "dve3", dv3_0 + 2 * j + 2)
                sc.activation(t["stat"][:, 3:4], t["stat"][:, 2:3], AF.Ln)
                b.inc(sc.activation(t["rstdb"][:, 1:2], t["stat"][:, 3:4], AF.Exp, scale=np.float32(-0.5)),
                      "act2", 1)

        @blk.tensor
        def _(tt):
            b.waitv(tt, "dve2", dv2_w)
            b.waitv(tt, "ldx", ldxf)
            for j in range(NJ):
                b.waitv(tt, "dve", dv0 + 9 * j + 2)
                for k in range(8):
                    mm = tt.transpose(pdtr[:, 128 * k: 128 * (k + 1)], t["n1ib"][:, 128 * k: 128 * (k + 1)],
                                      t["identb"][:, :])
                b.inc(mm, "pe", 1)
                b.waitv(tt, "dve", dv0 + 9 * j + 3)
                for cb in range(6):
                    pm = pdm[cb % 2]
                    nprv = j * 6 + cb
                    if nprv >= 2:
                        pj, pcb = (nprv - 2) // 6, (nprv - 2) % 6
                        b.waitv(tt, "dve", dv0 + 9 * pj + 4 + pcb)
                    for k in range(8):
                        tt.matmul(pm[:, :], t["n1it"][:, 128 * k: 128 * (k + 1)],
                                  t["WQ2"][:, 3072 * k + 512 * cb: 3072 * k + 512 * (cb + 1)],
                                  start=(k == 0), stop=False)
                    mm = tt.matmul(pm[:, :], t["ones_r"][0:1, 0:128], t["beffq"][0:1, 512 * cb: 512 * (cb + 1)],
                                   start=False, stop=True)
                    b.inc(mm, "pe2", 1)
                    if cb < 4:
                        b.waitv(tt, "dve", dv0 + 9 * j + 4 + cb)
                        pq = pdq[cb % 2]
                        ntr = j * 4 + cb
                        if ntr >= 2:
                            b.waitv(tt, "dve4", dv4_0 + ntr - 1)
                        for s4 in range(4):
                            mm2 = tt.transpose(pq[:, 128 * s4: 128 * (s4 + 1)],
                                               t["qsb"][:, 128 * s4: 128 * (s4 + 1)], t["identb"][:, :])
                        b.inc(mm2, "pe3", 1)

    @blk.gpsimd
    def _(g):
        b.wait(g, "ld3")
        b.inc(g.collective_compute("AllGather", ALU.bypass, replica_groups=GROUPS4,
                                   ins=[t["k2t_in"].opt()], outs=[t["k2t_ag"].opt()]), "cc", 1)
        b.wait(g, "ld2")
        b.inc(g.collective_compute("AllGather", ALU.bypass, replica_groups=GROUPS4,
                                   ins=[t["v2h_in"].opt()], outs=[t["v2h_ag"].opt()]), "cc", 1)


def phase_E(b, blk):
    nc, t = b.nc, b.t
    ccD = b.cnt["cc"]
    PBe = [t["PB0e"], t["PB1e"]]

    units = [(hp, qb) for hp in range(8) for qb in range(4)]
    NGE = 64

    pe0 = b.cnt["pe"]
    pe2_0 = b.cnt["pe2"]
    pe3_0 = b.cnt["pe3"]
    ac0 = b.cnt["act"]
    dv2_0 = b.cnt["dve2"]
    dv3_0 = b.cnt["dve3"]
    lda0, ldb0 = b.cnt["lda"], b.cnt["ldb"]

    @blk.sync
    def _(s):
        b.waitv(s, "cc", ccD)
        for hp in range(8):
            sem = "lda" if hp % 2 == 0 else "ldb"
            if hp >= 1:
                b.waitv(s, "pe2", pe2_0 + NGE * 4 * hp)
            for h in (2 * hp, 2 * hp + 1):
                for r2 in range(4):
                    b.inc(s.dma_start(out=t["KB"][64 * (h % 2): 64 * (h % 2) + 64, 2048 * r2: 2048 * (r2 + 1)],
                                      in_=t["k2t_ag"][1024 * r2 + 64 * h: 1024 * r2 + 64 * (h + 1), :]), sem, 16)
                vh = t["VHA"] if h % 2 == 0 else t["VHB"]
                for r2 in range(4):
                    b.inc(s.dma_start(
                        out=vh.rearrange("p (rb tt f) -> p rb tt f", rb=4, f=128)[:, r2, :, :],
                        in_=t["v2h_ag"][32768 * r2 + 2048 * h: 32768 * r2 + 2048 * (h + 1), :]
                            .rearrange("(tt p) f -> p tt f", p=128)), sem, 16)

    with nc.psum_tensor("spa", [128, 1024], F32) as spa, \
         nc.psum_tensor("spb", [128, 1024], F32) as spb, \
         nc.psum_tensor("eop0", [128, 512], F32) as eop0, \
         nc.psum_tensor("eop1", [128, 512], F32) as eop1, \
         nc.psum_tensor("etr", [128, 264], BF16) as etr:
        SB2 = [spa, spb]
        EOP = [eop0, eop1]

        @blk.tensor
        def _(tt):
            for u, (hp, qb) in enumerate(units):
                qcol = 2048 * hp + 512 * qb
                if qb == 0:
                    sem = "lda" if hp % 2 == 0 else "ldb"
                    b.waitv(tt, sem, (lda0 if hp % 2 == 0 else ldb0) + (hp // 2 + 1) * 16 * 16)
                for g in range(NGE):
                    gg = NGE * u + g
                    if gg >= 2:
                        b.waitv(tt, "act", ac0 + gg - 1)
                    sb_ = SB2[gg % 2]
                    tt.matmul(sb_[:, 0:512],
                              t["KB"][0:64, 128 * g: 128 * (g + 1)],
                              t["Q2T"][0:64, qcol: qcol + 512],
                              start=True, stop=True)
                    mm = tt.matmul(sb_[:, 512:1024],
                                   t["KB"][64:128, 128 * g: 128 * (g + 1)],
                                   t["Q2T"][64:128, qcol: qcol + 512],
                                   start=True, stop=True)
                    b.inc(mm, "pe", 1)
                    if g == 0:
                        b.waitv(tt, "dve2", dv2_0 + 2 * u)
                    if g >= 1:
                        emit_av_e(b, tt, u, g - 1, EOP, PBe, ac0)
                emit_av_e(b, tt, u, NGE - 1, EOP, PBe, ac0)
                for p in range(2):
                    b.waitv(tt, "dve2", dv2_0 + 2 * u + p + 1)
                    for tch in range(4):
                        mm = tt.transpose(etr[:, 66 * tch: 66 * tch + 65],
                                          t["mscb"][0:65, 128 * tch: 128 * (tch + 1)],
                                          t["identb"][0:65, 0:65])
                    b.inc(mm, "pe3", 1)

        @blk.scalar
        def _(sc):
            for u in range(len(units)):
                for g in range(NGE):
                    gg = NGE * u + g
                    b.waitv(sc, "pe", pe0 + gg + 1)
                    if gg >= 2:
                        b.waitv(sc, "pe2", pe2_0 + gg - 1)
                    b.inc(sc.activation(PBe[gg % 2][:, 0:1024], SB2[gg % 2][:, 0:1024],
                                        AF.Exp, bias=C0, scale=INV8), "act", 1)

        @blk.vector
        def _(v):
            for u, (hp, qb) in enumerate(units):
                b.waitv(v, "pe2", pe2_0 + NGE * (u + 1))
                for p in range(2):
                    h = 2 * hp + p
                    b.waitv(v, "pe3", pe3_0 + 2 * u + p)
                    b.inc(v.tensor_copy(t["mscb"][0:65, 0:512], EOP[p][0:65, :]), "dve2", 1)
                    b.waitv(v, "pe3", pe3_0 + 2 * u + p + 1)
                    for tch in range(4):
                        b.fence(v, v.reciprocal(t["lrec"][:, tch: tch + 1], etr[:, 66 * tch + 64: 66 * tch + 65]))
                    for tch in range(4):
                        j = 4 * qb + tch
                        i3 = v.tensor_scalar(t["O2N"][:, 1024 * j + 64 * h: 1024 * j + 64 * (h + 1)],
                                             etr[:, 66 * tch: 66 * tch + 64], t["lrec"][:, tch: tch + 1],
                                             None, ALU.mult)
                b.inc(i3, "dve3", 1)

    phase_E_end(b, blk)


def phase_G(b, blk):
    nc, t = b.nc, b.t
    ccE = b.cnt["cc"]
    dvEe = b.cnt["dve"]
    ldcE = b.cnt["ldc"]
    ld3E = b.cnt["ld3"]

    @blk.gpsimd
    def _(g):
        b.waitv(g, "dve", dvEe)
        b.waitv(g, "ldc", ldcE)
        b.waitv(g, "ld3", ld3E)
        b.inc(g.dma_start(out=t["W1G"][:, :], in_=t["w1e"]), "ldg", 16)
        b.inc(g.dma_start(out=t["W2S"][:, :], in_=t["w2e"]), "ldg", 16)
        b.inc(g.dma_start(out=t["growb"][0:1, :], in_=t["b1e"]), "ldg", 16)
        b.inc(g.dma_start(out=t["gbrow"][0:1, 0:E], in_=t["gateb"]), "ldg", 16)
        b.inc(g.dma_start(out=t["b2eb"][0:1, :], in_=t["b2e"]), "ldg", 16)
        b.inc(g.dma_start(out=t["gwg"][:, :], in_=t["gatew"]), "ldg", 16)
        b.inc(g.dma_start(out=t["eselS"][:, :], in_=t["esel"]), "ldg", 16)
    ldg_g = b.cnt["ldg"]

    pe_p0 = b.cnt["pe"]
    dv_p0 = b.cnt["dve"]
    with nc.psum_tensor("gbe", [1, 512], F32) as gbe, \
         nc.psum_tensor("gbc", [128, 128], F32) as gbc:
        @blk.vector
        def _(v):
            b.waitv(v, "ldg", ldg_g)
            b.inc(v.tensor_copy(t["b2c_bf"][:, 0:8], t["vecc"][:, 24:32]), "dve", 1)

        @blk.tensor
        def _(tt):
            b.waitv(tt, "ldg", ldg_g)
            b.waitv(tt, "dve", dv_p0 + 1)
            for fb in range(8):
                if fb >= 1:
                    b.waitv(tt, "dve", dv_p0 + 1 + fb)
                for k in range(8):
                    tt.matmul(gbe[:, :], t["b2c_bf"][:, k: k + 1],
                              t["W1G"][:, 4096 * k + 512 * fb: 4096 * k + 512 * (fb + 1)],
                              start=(k == 0), stop=False)
                mm = tt.matmul(gbe[:, :], t["ones_r"][0:1, 0:1], t["growb"][0:1, 512 * fb: 512 * (fb + 1)],
                               start=False, stop=True)
                b.inc(mm, "pe", 1)
            b.waitv(tt, "dve", dv_p0 + 9)
            for k in range(8):
                tt.matmul(gbe[:, 0:E], t["b2c_bf"][:, k: k + 1], t["gwg"][:, 8 * k: 8 * (k + 1)],
                          start=(k == 0), stop=False)
            mm = tt.matmul(gbe[:, 0:E], t["ones_r"][0:1, 0:1], t["gbrow"][0:1, 0:E],
                           start=False, stop=True)
            b.inc(mm, "pe", 1)
            for m in range(32):
                b.waitv(tt, "dve", dv_p0 + 11 + m)
                b.inc(tt.transpose(gbc[:, 0:1], t["b1effr"][0:1, 128 * m: 128 * (m + 1)],
                                   t["identf"][0:1, 0:1]), "pe", 1)

        @blk.vector
        def _(v):
            for fb in range(8):
                b.waitv(v, "pe", pe_p0 + fb + 1)
                b.inc(v.tensor_copy(t["b1effr"][0:1, 512 * fb: 512 * (fb + 1)], gbe[0:1, :]), "dve", 1)
            b.waitv(v, "pe", pe_p0 + 9)
            b.inc(v.tensor_copy(t["beffg"][0:1, 0:E], gbe[0:1, 0:E]), "dve", 1)
            for k in range(8):
                v.tensor_scalar(t["W1G"][:, 4096 * k: 4096 * (k + 1)], t["W1G"][:, 4096 * k: 4096 * (k + 1)],
                                t["vecc"][:, 16 + k: 17 + k], None, ALU.mult)
            i2 = None
            for k in range(8):
                i2 = v.tensor_scalar(t["gwg"][:, 8 * k: 8 * (k + 1)], t["gwg"][:, 8 * k: 8 * (k + 1)],
                                     t["vecc"][:, 16 + k: 17 + k], None, ALU.mult)
            b.inc(i2, "dve", 1)
            for m in range(32):
                b.waitv(v, "pe", pe_p0 + 10 + m)
                b.inc(v.tensor_copy(t["b1effc"][:, m: m + 1], gbc[:, 0:1]), "dve", 1)
    dv_pw = b.cnt["dve"]

    pe0 = b.cnt["pe"]
    pe2_0 = b.cnt["pe2"]
    pe3_0 = b.cnt["pe3"]
    ac0 = b.cnt["act"]
    dv0 = b.cnt["dve2"]
    dv3_0 = b.cnt["dve3"]
    dv4_0 = b.cnt["dve4"]
    lda0, ldb0 = b.cnt["lda"], b.cnt["ldb"]
    ldc0 = b.cnt["ldc"]
    cc0 = b.cnt["cc"]
    order = [(kp, cp) for kp in range(4) for cp in range(8)]

    with nc.psum_tensor("glg0", [128, 16], F32) as glg0, \
         nc.psum_tensor("glg1", [128, 16], F32) as glg1, \
         nc.psum_tensor("gph0", [128, 512], F32) as gph0, \
         nc.psum_tensor("gph1", [128, 512], F32) as gph1, \
         nc.psum_tensor("gpm0", [128, 512], F32) as gpm0, \
         nc.psum_tensor("gpm1", [128, 512], F32) as gpm1:
        glg = [glg0, glg1]
        gph = [gph0, gph1]
        gpm = [gpm0, gpm1]

        @blk.sync
        def _(s):
            b.waitv(s, "cc", ccE)
            for u, (kp, cp) in enumerate(order):
                TT = 4 * cp + kp
                rr, colb = TT // 4, TT % 4
                sem = "lda" if u % 2 == 0 else "ldb"
                if u >= 1:
                    b.waitv(s, "pe", pe0 + 32 * u)
                for k in range(8):
                    b.inc(s.dma_start(out=t["H2TB"][:, 512 * k: 512 * (k + 1)],
                                      in_=t["h2t_ag"][1024 * rr + 128 * k: 1024 * rr + 128 * (k + 1),
                                                      512 * colb: 512 * (colb + 1)]), sem, 16)
                if u >= 1:
                    pkp, pcp = order[u - 1]
                    pTT = 4 * pcp + pkp
                    b.waitv(s, "dve4", dv4_0 + 4 * u)
                    dst = t["rs_in"][4096 * pkp + 512 * pcp: 4096 * pkp + 512 * (pcp + 1), :]
                    b.inc(s.dma_start(out=dst.rearrange("(m p) f -> p m f", p=128),
                                      in_=t["MOEST"].rearrange("p (m f) -> p m f", f=1024)), "ldc", 16)
            pkp, pcp = order[-1]
            b.waitv(s, "dve4", dv4_0 + 4 * 32)
            dst = t["rs_in"][4096 * pkp + 512 * pcp: 4096 * pkp + 512 * (pcp + 1), :]
            b.inc(s.dma_start(out=dst.rearrange("(m p) f -> p m f", p=128),
                              in_=t["MOEST"].rearrange("p (m f) -> p m f", f=1024)), "ldc", 16)

        @blk.tensor
        def _(tt):
            b.waitv(tt, "dve", dv_pw)
            b.waitv(tt, "cc", ccE)
            for u, (kp, cp) in enumerate(order):
                sem = "lda" if u % 2 == 0 else "ldb"
                b.waitv(tt, sem, (lda0 if u % 2 == 0 else ldb0) + (u // 2 + 1) * 8 * 16)
                for j2 in range(4):
                    if 4 * u + j2 >= 2:
                        if j2 >= 2:
                            b.waitv(tt, "dve2", dv0 + 36 * u + 5 * (j2 - 2) + 1)
                        else:
                            b.waitv(tt, "dve2", dv0 + 36 * (u - 1) + 5 * (j2 + 2) + 1)
                    pl = glg[j2 % 2]
                    for k in range(8):
                        tt.matmul(pl[:, 0:E], t["H2TB"][:, 512 * k + 128 * j2: 512 * k + 128 * (j2 + 1)],
                                  t["gwg"][:, 8 * k: 8 * (k + 1)], start=(k == 0), stop=False)
                    mm = tt.matmul(pl[:, 0:E], t["ones_r"][0:1, 0:128], t["beffg"][0:1, 0:E],
                                   start=False, stop=True)
                    b.inc(mm, "pe3", 1)
                for m in range(32):
                    ph = gph[m % 2]
                    if 32 * u + m >= 2:
                        b.waitv(tt, "dve3", dv3_0 + 32 * u + m - 1)
                    for k in range(8):
                        mm = tt.matmul(ph[:, :], t["W1G"][:, 4096 * k + 128 * m: 4096 * k + 128 * (m + 1)],
                                       t["H2TB"][:, 512 * k: 512 * (k + 1)], start=(k == 0), stop=(k == 7))
                    b.inc(mm, "pe", 1)
                for j2 in range(4):
                    b.waitv(tt, "dve3", dv3_0 + 32 * (u + 1))
                    if 4 * u + j2 >= 1:
                        b.waitv(tt, "dve4", dv4_0 + 4 * u + j2)
                    b.waitv(tt, "dve2", dv0 + 36 * u + 20 + 4 * (j2 + 1))
                    for cb in range(2):
                        pm = gpm[cb]
                        for m in range(32):
                            tt.matmul(pm[:, :], t["HID"][:, 512 * m + 128 * j2: 512 * m + 128 * (j2 + 1)],
                                      t["W2S"][:, 1024 * m + 512 * cb: 1024 * m + 512 * (cb + 1)],
                                      start=(m == 0), stop=False)
                        mm = tt.matmul(pm[:, :], t["ones_r"][0:1, 0:128],
                                       t["b2eb"][0:1, 512 * cb: 512 * (cb + 1)], start=False, stop=True)
                    b.inc(mm, "pe2", 1)

        @blk.scalar
        def _(sc):
            for u in range(32):
                b.waitv(sc, "dve2", dv0 + 36 * u + 20)
                b.inc(sc.activation(t["gsc"][:, 44: 48], t["gsc"][:, 40: 44], AF.Exp), "act", 1)

        @blk.vector
        def _(v):
            for u, (kp, cp) in enumerate(order):
                for j2 in range(4):
                    gid = 4 * u + j2
                    b.waitv(v, "pe3", pe3_0 + gid + 1)
                    pl = glg[j2 % 2]
                    g8 = t["gsc"][:, 0:8]
                    b.fence(v, v.tensor_copy(g8, pl[:, 0:E]))
                    b.inc(v.tensor_copy(t["gsc"][:, 8:9], pl[:, 0:1]), "dve2", 1)
                    b.vred(v, "reduce_max", t["gsc"][:, 16 + j2: 17 + j2], g8, AX.X)
                    b.fence(v, v.tensor_scalar(t["gmk"][:, 8 * j2: 8 * j2 + 8], g8,
                                               t["gsc"][:, 16 + j2: 17 + j2], None, ALU.is_equal))
                    b.fence(v, v.scalar_tensor_tensor(t["gsc"][:, 24: 32], t["gmk"][:, 8 * j2: 8 * j2 + 8],
                                                      -1e30, g8, ALU.mult, ALU.add))
                    b.vred(v, "reduce_max", t["gsc"][:, 20 + j2: 21 + j2], t["gsc"][:, 24:32], AX.X)
                    b.fence(v, v.tensor_scalar(t["gmk"][:, 32 + 8 * j2: 40 + 8 * j2], t["gsc"][:, 24:32],
                                               t["gsc"][:, 20 + j2: 21 + j2], None, ALU.is_equal))
                    b.inc(v.tensor_tensor(t["gsc"][:, 40 + j2: 41 + j2], t["gsc"][:, 20 + j2: 21 + j2],
                                          t["gsc"][:, 16 + j2: 17 + j2], ALU.subtract), "dve2", 4)
                b.waitv(v, "act", ac0 + u + 1)
                for j2 in range(4):
                    gid = 4 * u + j2
                    b.fence(v, v.tensor_scalar(t["gsc"][:, 48 + j2: 49 + j2], t["gsc"][:, 44 + j2: 45 + j2],
                                               1.0, None, ALU.add))
                    b.fence(v, v.reciprocal(t["gsc"][:, 52 + j2: 53 + j2], t["gsc"][:, 48 + j2: 49 + j2]))
                    b.fence(v, v.tensor_tensor(t["gsc"][:, 56 + j2: 57 + j2], t["gsc"][:, 44 + j2: 45 + j2],
                                               t["gsc"][:, 52 + j2: 53 + j2], ALU.mult))
                    v.tensor_scalar(t["gsc"][:, 0:8], t["gmk"][:, 8 * j2: 8 * j2 + 8],
                                    t["gsc"][:, 52 + j2: 53 + j2], None, ALU.mult)
                    b.fence(v, v.tensor_scalar(t["gsc"][:, 8:16], t["gmk"][:, 32 + 8 * j2: 40 + 8 * j2],
                                               t["gsc"][:, 56 + j2: 57 + j2], None, ALU.mult))
                    b.fence(v, v.tensor_tensor(t["gsc"][:, 0:8], t["gsc"][:, 0:8], t["gsc"][:, 8:16], ALU.add))
                    b.fence(v, v.tensor_tensor(t["gsc"][:, 0:8], t["gsc"][:, 0:8], t["eselS"][:, :], ALU.mult))
                    b.vred(v, "reduce_sum", t["ewtt"][:, j2: j2 + 1], t["gsc"][:, 0:8], AX.X)
                    b.inc(v.tensor_copy(t["ewtt"][:, 4 + j2: 5 + j2], t["ewtt"][:, j2: j2 + 1]), "dve2", 4)
                for m in range(32):
                    b.waitv(v, "pe", pe0 + 32 * u + m + 1)
                    if m == 0 and u >= 1:
                        b.waitv(v, "pe2", pe2_0 + 4 * u)
                    ph = gph[m % 2]
                    b.inc(v.tensor_scalar(t["HID"][:, 512 * m: 512 * (m + 1)], ph[:, :],
                                          t["b1effc"][:, m: m + 1], 0.0, ALU.add, ALU.max), "dve3", 1)
                for j2 in range(4):
                    b.waitv(v, "pe2", pe2_0 + 4 * u + j2 + 1)
                    if u >= 1 and j2 == 0:
                        b.waitv(v, "ldc", ldc0 + u * 16)
                    for cb in range(2):
                        i6 = v.tensor_scalar(t["MOEST"][:, 1024 * j2 + 512 * cb: 1024 * j2 + 512 * (cb + 1)],
                                             gpm[cb][:, :], t["ewtt"][:, j2: j2 + 1], None, ALU.mult)
                    b.inc(i6, "dve4", 1)

    @blk.gpsimd
    def _(g):
        for kp in range(4):
            b.waitv(g, "ldc", ldc0 + (8 * kp + 8) * 16)
            b.inc(g.collective_compute("ReduceScatter", ALU.add, replica_groups=GROUPS8,
                                       ins=[t["rs_in"][4096 * kp: 4096 * (kp + 1), :].opt()],
                                       outs=[t["rs_out"][512 * kp: 512 * (kp + 1), :].opt()]), "cc", 1)


def phase_H(b, blk):
    nc, t = b.nc, b.t
    ccH = b.cnt["cc"]
    dv0 = b.cnt["dve"]
    dv3_0 = b.cnt["dve3"]
    ac0 = b.cnt["act2"]
    lda0, ldb0 = b.cnt["lda"], b.cnt["ldb"]
    gp0, gp20 = b.cnt["gp"], b.cnt["gp2"]
    ldv0 = b.cnt["ldv"]
    st0 = b.cnt["st"]

    @blk.sync
    def _(s):
        b.waitv(s, "cc", ccH)
        for j in range(NJ):
            sem = "lda" if j % 2 == 0 else "ldb"
            if j >= 2:
                b.waitv(s, "dve", dv0 + 2 * (j - 2) + 1)
            b.inc(s.dma_start(out=(t["xc"] if j % 2 == 0 else t["msc"])[:, :],
                              in_=t["u2_pk"][:, 1024 * j: 1024 * (j + 1)]), sem, 16)
            if j >= 1:
                b.waitv(s, "dve", dv0 + 2 * (j - 1) + 1)
                if j >= 2:
                    b.waitv(s, "st", st0 + 16 * (j - 1))
                b.inc(s.dma_start(out=t["moef"][:, :], in_=t["x1_pk"][:, 1024 * (j - 1): 1024 * j]), "ldv", 16)
                b.waitv(s, "dve", dv0 + 2 * (j - 1) + 2)
                b.inc(s.dma_start(out=t["y"][128 * (j - 1): 128 * j, :], in_=t["moef"][:, :]), "st", 16)
        b.waitv(s, "dve", dv0 + 2 * (NJ - 1) + 1)
        b.waitv(s, "st", st0 + 16 * (NJ - 1))
        b.inc(s.dma_start(out=t["moef"][:, :], in_=t["x1_pk"][:, 1024 * (NJ - 1): 1024 * NJ]), "ldv", 16)
        b.waitv(s, "dve", dv0 + 2 * NJ)
        b.inc(s.dma_start(out=t["y"][128 * (NJ - 1): 128 * NJ, :], in_=t["moef"][:, :]), "st", 16)

    @blk.gpsimd
    def _(g):
        b.waitv(g, "cc", ccH)
        for j in range(NJ):
            if j >= 2:
                b.waitv(g, "dve", dv0 + 2 * (j - 2) + 1)
            b.inc(g.dma_start(out=(t["hsq"] if j % 2 == 0 else t["dsq"])[:, :],
                              in_=t["rs_out"][128 * j: 128 * (j + 1), :]),
                  "gp" if j % 2 == 0 else "gp2", 16)

    @blk.vector
    def _(v):
        for j in range(NJ):
            xb = t["xc"] if j % 2 == 0 else t["msc"]
            mb = t["hsq"] if j % 2 == 0 else t["dsq"]
            sem = "lda" if j % 2 == 0 else "ldb"
            b.waitv(v, sem, (lda0 if j % 2 == 0 else ldb0) + (j // 2 + 1) * 16)
            sem2 = "gp" if j % 2 == 0 else "gp2"
            b.waitv(v, sem2, (gp0 if j % 2 == 0 else gp20) + (j // 2 + 1) * 16)
            b.inc(v.tensor_tensor(xb[:, :], xb[:, :], mb[:, :], ALU.add), "dve", 1)
            st_ = t["stat"]
            b.vred(v, "reduce_sum", st_[:, j: j + 1], xb[:, :], AX.X)
            b.fence(v, v.tensor_scalar(st_[:, 16 + j: 17 + j], st_[:, j: j + 1], float(INV_D), None, ALU.mult))
            b.fence(v, v.tensor_scalar(t["sq"][:, :], xb[:, :], st_[:, 16 + j: 17 + j], None, ALU.subtract))
            b.fence(v, v.tensor_tensor(t["csq"][:, :], t["sq"][:, :], t["sq"][:, :], ALU.mult))
            b.vred(v, "reduce_sum", st_[:, j: j + 1], t["csq"][:, :], AX.X)
            b.inc(v.tensor_scalar(st_[:, j: j + 1], st_[:, j: j + 1], float(INV_D), float(EPS),
                                  ALU.mult, ALU.add), "dve3", 1)
            b.waitv(v, "act2", ac0 + j + 1)
            b.fence(v, v.tensor_scalar(t["sq"][:, :], t["sq"][:, :], t["rstdb"][:, j: j + 1], None, ALU.mult))
            b.waitv(v, "ldv", ldv0 + 16 * (j + 1))
            b.inc(v.tensor_tensor(t["moef"][:, :], t["moef"][:, :], t["sq"][:, :], ALU.add), "dve", 1)

    @blk.scalar
    def _(sc):
        for j in range(NJ):
            b.waitv(sc, "dve3", dv3_0 + j + 1)
            sc.activation(t["stat"][:, 16 + j: 17 + j], t["stat"][:, j: j + 1], AF.Ln)
            b.inc(sc.activation(t["rstdb"][:, j: j + 1], t["stat"][:, 16 + j: 17 + j], AF.Exp,
                                scale=np.float32(-0.5)), "act2", 1)


def emit_av_e(b, tt, u, g, EOP, PBe, ac0):
    t = b.t
    gg = 64 * u + g
    b.waitv(tt, "act", ac0 + gg + 1)
    r2, tile = g // 16, g % 16
    col = 2048 * r2 + 128 * tile
    tt.matmul(EOP[0][:, :], t["VHA"][:, col: col + 128],
              PBe[gg % 2][:, 0:512], start=(g == 0), stop=(g == 63))
    mm = tt.matmul(EOP[1][:, :], t["VHB"][:, col: col + 128],
                   PBe[gg % 2][:, 512:1024], start=(g == 0), stop=(g == 63))
    b.inc(mm, "pe2", 1)


def phase_E_end(b, blk):
    nc, t = b.nc, b.t
    dv3_all = b.cnt["dve3"]

    @blk.gpsimd
    def _(g):
        b.waitv(g, "dve3", dv3_all)
        b.inc(g.dma_start(out=t["WO2"].rearrange("p (k f) -> p k f", f=1024),
                          in_=t["wo2"].rearrange("(k p) f -> p k f", p=128)), "ldg", 16)
        b.inc(g.dma_start(out=t["bob"][0:1, :], in_=t["bo"]), "ldg", 16)
    ldg_e = b.cnt["ldg"]

    pe0 = b.cnt["pe"]
    pe2_0 = b.cnt["pe2"]
    pe3_0 = b.cnt["pe3"]
    ac0 = b.cnt["act2"]
    dv0 = b.cnt["dve"]
    lda0, ldb0 = b.cnt["lda"], b.cnt["ldb"]
    ldc0 = b.cnt["ldc"]
    ld30 = b.cnt["ld3"]

    with nc.psum_tensor("etr2", [128, 1024], BF16) as etr2, \
         nc.psum_tensor("exo0", [128, 512], F32) as exo0, \
         nc.psum_tensor("exo1", [128, 512], F32) as exo1, \
         nc.psum_tensor("etr3", [128, 1024], BF16) as etr3:
        exo = [exo0, exo1]

        @blk.sync
        def _(s):
            b.waitv(s, "dve3", dv3_all)
            for j in range(NJ):
                sem = "lda" if j % 2 == 0 else "ldb"
                if j >= 2:
                    b.waitv(s, "dve", dv0 + 4 * (j - 2) + 2)
                b.inc(s.dma_start(out=(t["xc"] if j % 2 == 0 else t["msc"])[:, :],
                                  in_=t["u_pk"][:, 1024 * j: 1024 * (j + 1)]), sem, 16)
                if j >= 1:
                    b.waitv(s, "dve", dv0 + 4 * (j - 1) + 3)
                    b.inc(s.dma_start(out=t["u2_pk"][:, 1024 * (j - 1): 1024 * j],
                                      in_=(t["xc"] if (j - 1) % 2 == 0 else t["msc"])[:, :]), "ldc", 16)
            b.waitv(s, "dve", dv0 + 4 * (NJ - 1) + 3)
            b.inc(s.dma_start(out=t["u2_pk"][:, 1024 * (NJ - 1): 1024 * NJ],
                              in_=(t["xc"] if (NJ - 1) % 2 == 0 else t["msc"])[:, :]), "ldc", 16)
            b.waitv(s, "dve", dv0 + 4 * NJ)
            for k in range(8):
                b.inc(s.dma_start(out=t["h2t_in"][128 * k: 128 * (k + 1), :],
                                  in_=t["H2TS"][:, 2048 * k: 2048 * (k + 1)]), "ld3", 16)

        @blk.tensor
        def _(tt):
            b.waitv(tt, "ldg", ldg_e)
            for j in range(NJ):
                if j >= 1:
                    b.waitv(tt, "dve", dv0 + 4 * (j - 1) + 1)
                for k in range(8):
                    mm = tt.transpose(etr2[:, 128 * k: 128 * (k + 1)],
                                      t["O2N"][:, 1024 * j + 128 * k: 1024 * j + 128 * (k + 1)],
                                      t["identb"][:, :])
                b.inc(mm, "pe", 1)
                b.waitv(tt, "dve", dv0 + 4 * j + 1)
                if j >= 1:
                    b.waitv(tt, "dve", dv0 + 4 * (j - 1) + 2)
                for cb in range(2):
                    pm = exo[cb]
                    for k in range(8):
                        tt.matmul(pm[:, :], t["o2lh"][:, 128 * k: 128 * (k + 1)],
                                  t["WO2"][:, 1024 * k + 512 * cb: 1024 * k + 512 * (cb + 1)],
                                  start=(k == 0), stop=False)
                    mm = tt.matmul(pm[:, :], t["ones_r"][0:1, 0:128], t["bob"][0:1, 512 * cb: 512 * (cb + 1)],
                                   start=False, stop=True)
                b.inc(mm, "pe2", 1)
                b.waitv(tt, "dve", dv0 + 4 * j + 3)
                for k in range(8):
                    mm = tt.transpose(etr3[:, 128 * k: 128 * (k + 1)], t["n2e"][:, 128 * k: 128 * (k + 1)],
                                      t["identb"][:, :])
                b.inc(mm, "pe3", 1)

        @blk.vector
        def _(v):
            for j in range(NJ):
                xb = t["xc"] if j % 2 == 0 else t["msc"]
                sem = "lda" if j % 2 == 0 else "ldb"
                b.waitv(v, sem, (lda0 if j % 2 == 0 else ldb0) + (j // 2 + 1) * 16)
                b.waitv(v, "pe", pe0 + j + 1)
                if j >= 1:
                    b.waitv(v, "pe2", pe2_0 + j)
                b.inc(v.tensor_copy(t["o2lh"][:, :], etr2[:, :]), "dve", 1)
                b.waitv(v, "pe2", pe2_0 + j + 1)
                v.tensor_tensor(xb[:, 0:512], xb[:, 0:512], exo[0][:, :], ALU.add)
                b.fence(v, v.tensor_tensor(xb[:, 512:1024], xb[:, 512:1024], exo[1][:, :], ALU.add))
                b.inc(v.tensor_copy(t["gsc"][:, 1:2], xb[:, 1023:1024]), "dve", 1)
                st_ = t["stat"]
                b.vred(v, "reduce_sum", st_[:, 4:5], xb[:, :], AX.X)
                b.fence(v, v.tensor_scalar(st_[:, 5:6], st_[:, 4:5], float(INV_D), None, ALU.mult))
                b.fence(v, v.tensor_scalar(t["sq"][:, :], xb[:, :], st_[:, 5:6], None, ALU.subtract))
                b.fence(v, v.tensor_tensor(t["esq"][:, :], t["sq"][:, :], t["sq"][:, :], ALU.mult))
                b.vred(v, "reduce_sum", st_[:, 4:5], t["esq"][:, :], AX.X)
                b.inc(v.tensor_scalar(st_[:, 4:5], st_[:, 4:5], float(INV_D), float(EPS),
                                      ALU.mult, ALU.add), "dve3", 1)
                b.waitv(v, "act2", ac0 + j + 1)
                if j >= 1:
                    b.waitv(v, "pe3", pe3_0 + j)
                b.inc(v.tensor_scalar(t["n2e"][:, :], t["sq"][:, :], t["rstdb"][:, 2:3], None, ALU.mult),
                      "dve", 1)
                b.waitv(v, "pe3", pe3_0 + j + 1)
                for k in range(8):
                    i5 = v.tensor_copy(t["H2TS"][:, 2048 * k + 128 * j: 2048 * k + 128 * (j + 1)],
                                       etr3[:, 128 * k: 128 * (k + 1)])
                b.inc(i5, "dve", 1)

        @blk.scalar
        def _(sc):
            dv3e = b.cnt["dve3"] - NJ
            for j in range(NJ):
                b.waitv(sc, "dve3", dv3e + j + 1)
                sc.activation(t["stat"][:, 5:6], t["stat"][:, 4:5], AF.Ln)
                b.inc(sc.activation(t["rstdb"][:, 2:3], t["stat"][:, 5:6], AF.Exp, scale=np.float32(-0.5)),
                      "act2", 1)

    @blk.gpsimd
    def _(g):
        b.wait(g, "ld3")
        b.inc(g.collective_compute("AllGather", ALU.bypass, replica_groups=GROUPS8,
                                   ins=[t["h2t_in"].opt()], outs=[t["h2t_ag"].opt()]), "cc", 1)


def emit_av(b, tt, units, u, g, bop, PB, ac0, dv0, gg_of_unit, mask_cnt_before):
    t = b.t
    bt, i, qb, nch, groups = units[u]
    base = gg_of_unit[u]
    gg = base + g
    grp = groups[g]
    b.waitv(tt, "act", ac0 + gg + 1)
    nmask_before_and_in = mask_cnt_before[u] + sum(
        1 for gp in range(g + 1) for n in groups[gp]
        if (n - 4 * qb) >= 0 and (n - 4 * qb) < 4)
    if nmask_before_and_in > 0:
        b.waitv(tt, "dve", b.cnt_dv0_B + nmask_before_and_in)
    for k, n in enumerate(grp):
        tile = 64 * bt + n
        mm = tt.matmul(bop[:, :], t["VP"][:, 130 * tile + 65 * i: 130 * tile + 65 * (i + 1)],
                       PB[gg % 2][:, 512 * k: 512 * (k + 1)],
                       start=(grp[0] == 0 and k == 0), stop=(n == nch - 1))
    b.inc(mm, "pe2", 1)


def kernel(**inputs):
    in_maps = make_in_maps(inputs)
    nc, _ = build_nc(debug=False)
    res = run_bass_kernel_spmd(nc, in_maps, core_ids=list(range(8)))
    return np.concatenate([res.results[cid]["y"] for cid in range(8)], 0).reshape(B, S, D)


# version 13
# speedup vs baseline: 1.0341x; 1.0341x over previous
import contextlib
import numpy as np
import ml_dtypes
import concourse.bass as bass
import concourse.mybir as mybir
from concourse.bass_utils import run_bass_kernel_spmd

F32 = mybir.dt.float32
BF16 = mybir.dt.bfloat16
AF = mybir.ActivationFunctionType
ALU = mybir.AluOpType
AX = mybir.AxisListType

B, S, D, H, DK, E, DFF = 2, 8192, 1024, 16, 64, 8, 4096
T = B * S
TLOC = T // 8
NJ = TLOC // 128
C0 = np.float32(-12.0)
INV8 = np.float32(0.125)
EPS = np.float32(1e-5)
INV_D = np.float32(1.0 / D)

GROUPS4 = [[0, 1, 2, 3], [4, 5, 6, 7]]
GROUPS8 = [list(range(8))]


def make_in_maps(inputs):
    x = np.asarray(inputs["L0_attn_x"], np.float32).reshape(T, D)
    Wqkv1 = np.asarray(inputs["L0_attn_W_qkv"], np.float32)
    Wo1 = np.asarray(inputs["L0_attn_W_o"], np.float32)
    perm = np.concatenate([np.arange(0, 64, 2), np.arange(1, 64, 2)])

    invf = (10000.0 ** (-np.arange(0, DK, 2, dtype=np.float32) / DK)).astype(np.float32)
    ang = np.arange(S, dtype=np.float32)[:, None] * invf[None, :]
    ropeC = np.concatenate([np.cos(ang), np.cos(ang)], 1).astype(np.float32)
    ropeS = np.concatenate([np.sin(ang), np.sin(ang)], 1).astype(np.float32)
    ropeC = np.ascontiguousarray(ropeC.reshape(64, 128, 64).transpose(1, 0, 2).reshape(128, 4096))
    ropeS = np.ascontiguousarray(ropeS.reshape(64, 128, 64).transpose(1, 0, 2).reshape(128, 4096))

    kvv = np.arange(128)[:, None]
    qv = np.arange(512)[None, :]
    maskc = np.concatenate([(128 * j + kvv <= qv) for j in range(4)], 1).astype(ml_dtypes.bfloat16)
    maskc = np.ascontiguousarray(maskc)

    common = {
        "ropeC": ropeC, "ropeS": ropeS, "maskc": maskc,

        "wqkv2": np.ascontiguousarray(np.asarray(inputs["L0_ffn_w_qkv"], np.float32)
                                      .reshape(8, 128, 3 * D).transpose(1, 0, 2).reshape(128, 24 * D)),
        "bqkv": np.asarray(inputs["L0_ffn_b_qkv"], np.float32).reshape(1, 3 * D),
        "wo2": np.ascontiguousarray(np.asarray(inputs["L0_ffn_w_o"], np.float32)),
        "bo": np.asarray(inputs["L0_ffn_b_o"], np.float32).reshape(1, D),
        "g1": np.asarray(inputs["L0_ffn_ln1_g"], np.float32).reshape(1, D),
        "b1ln": np.asarray(inputs["L0_ffn_ln1_b"], np.float32).reshape(1, D),
        "g2": np.asarray(inputs["L0_ffn_ln2_g"], np.float32).reshape(1, D),
        "b2ln": np.asarray(inputs["L0_ffn_ln2_b"], np.float32).reshape(1, D),
        "gatew": np.ascontiguousarray(np.asarray(inputs["L0_ffn_gate_w"], np.float32)
                                      .reshape(8, 128, E).transpose(1, 0, 2).reshape(128, 8 * E)),
        "gateb": np.asarray(inputs["L0_ffn_gate_b"], np.float32).reshape(1, E),
    }
    W1 = np.asarray(inputs["L0_ffn_W1"], np.float32)
    b1 = np.asarray(inputs["L0_ffn_b1"], np.float32)
    W2 = np.asarray(inputs["L0_ffn_W2"], np.float32)
    b2 = np.asarray(inputs["L0_ffn_b2"], np.float32)

    in_maps = []
    for c in range(8):
        h0 = 2 * c
        rows = []
        for blk in range(2):
            for h in (h0, h0 + 1):
                rows.append(blk * D + h * DK + perm)
        for h in (h0, h0 + 1):
            rows.append(2 * D + h * DK + np.arange(DK))
        rows = np.concatenate(rows)
        m = dict(common)
        xm = x[TLOC * c: TLOC * (c + 1)].reshape(NJ, 128, D).transpose(1, 0, 2)
        m["x_my"] = np.ascontiguousarray(xm.reshape(128, NJ * D))
        m["wo1c"] = np.ascontiguousarray(Wo1.T[128 * c: 128 * (c + 1)])
        wt = Wqkv1[rows].T.reshape(8, 128, 384).transpose(1, 0, 2)
        m["wqkv1T"] = np.ascontiguousarray(wt.reshape(128, 8 * 384))
        m["w1e"] = np.ascontiguousarray(W1[c].reshape(8, 128, DFF).transpose(1, 0, 2).reshape(128, 8 * DFF))
        m["b1e"] = b1[c].reshape(1, DFF)
        m["w2e"] = np.ascontiguousarray(W2[c].reshape(32, 128, D).transpose(1, 0, 2).reshape(128, 32 * D))
        m["b2e"] = b2[c].reshape(1, D)
        sel = np.zeros((128, E), np.float32)
        sel[:, c] = 1.0
        m["esel"] = sel
        in_maps.append(m)
    return in_maps


class Bld:
    def __init__(self, nc, debug):
        self.nc = nc
        self.debug = debug
        self.sem = {}
        self.cnt = {}
        self.t = {}
        self.stack = contextlib.ExitStack()
        self.dbg = {}

    def mksem(self, *names):
        for n in names:
            self.sem[n] = self.stack.enter_context(self.nc.semaphore(n))
            self.cnt[n] = 0

    def inc(self, inst, name, n=1):
        inst.then_inc(self.sem[name], n)
        self.cnt[name] += n
        return inst

    def wait(self, eng, name, val=None):
        eng.wait_ge(self.sem[name], self.cnt[name] if val is None else val)

    def waitv(self, eng, name, val):
        eng.wait_ge(self.sem[name], val)

    def fence(self, v, inst):
        self.inc(inst, "dvs", 1)
        self.wait(v, "dvs")
        return inst

    def vred(self, v, op, out, in_, axis):
        inst = getattr(v, op)(out, in_, axis)
        self.inc(inst, "dvs", 1)
        self.wait(v, "dvs")
        return inst


def build_nc(debug=False):
    nc = bass.Bass("TRN2", target_bir_lowering=False, detect_race_conditions=False)
    b = Bld(nc, debug)
    t = b.t

    def din(name, shape, dt=F32):
        t[name] = nc.dram_tensor(name, list(shape), dt, kind="ExternalInput").ap()

    def dint(name, shape, dt=BF16):
        t[name] = nc.dram_tensor(name, list(shape), dt, kind="Internal").ap()

    for name, shape, dt in [
        ("x_my", (128, NJ * D), F32), ("wqkv1T", (128, 8 * 384), F32),
        ("ropeC", (128, 4096), F32), ("ropeS", (128, 4096), F32), ("maskc", (128, 2048), BF16),
        ("wo1c", (128, D), F32), ("wqkv2", (128, 24 * D), F32), ("bqkv", (1, 3 * D), F32),
        ("wo2", (D, D), F32), ("bo", (1, D), F32),
        ("g1", (1, D), F32), ("b1ln", (1, D), F32), ("g2", (1, D), F32), ("b2ln", (1, D), F32),
        ("gatew", (128, 8 * E), F32), ("esel", (128, E), F32), ("gateb", (1, E), F32),
        ("w1e", (128, 8 * DFF), F32), ("b1e", (1, DFF), F32), ("w2e", (128, 32 * D), F32), ("b2e", (1, D), F32),
    ]:
        din(name, shape, dt)

    t["y"] = nc.dram_tensor("y", [TLOC, D], F32, kind="ExternalOutput").ap()

    for name, shape, dt in [
        ("n1t_in", (D, TLOC), BF16), ("n1t_ag", (8 * D, TLOC), BF16),
        ("og_in", (T, 128), BF16),
        ("xo_rs_in", (T, D), BF16), ("xo_rs_out", (TLOC, D), BF16),
        ("k2t_in", (D, TLOC), BF16), ("k2t_ag", (4 * D, TLOC), BF16),
        ("v2h_in", (H * TLOC, 128), BF16), ("v2h_ag", (4 * H * TLOC, 128), BF16),
        ("h2t_in", (D, TLOC), BF16), ("h2t_ag", (8 * D, TLOC), BF16),
        ("rs_in", (T, D), BF16), ("rs_out", (TLOC, D), BF16),
        ("x1_pk", (128, NJ * D), F32), ("u_pk", (128, NJ * D), F32), ("u2_pk", (128, NJ * D), F32),
    ]:
        dint(name, shape, dt)

    t["identf_d"] = nc.inline_tensor(np.eye(128, dtype=np.float32), "identf").ap()
    t["identb_d"] = nc.inline_tensor(np.eye(128, dtype=ml_dtypes.bfloat16), "identb").ap()

    if debug:
        def dout(name, shape, dt):
            b.dbg[name] = nc.dram_tensor("dbg_" + name, list(shape), dt, kind="ExternalOutput").ap()
        dout("og", (T, 128), BF16)
        dout("x1", (128, NJ * D), F32)
        dout("xo", (TLOC, D), BF16)
        dout("u", (128, NJ * D), F32)
        dout("u2", (128, NJ * D), F32)
        dout("h2t", (8 * D, TLOC), BF16)
        dout("rso", (TLOC, D), BF16)
        dout("ews", (128, 8), F32)
        dout("rsin", (T, D), BF16)
        dout("h2tin", (D, TLOC), BF16)
        dout("k2t", (4 * D, TLOC), BF16)
        dout("v2h", (4 * H * TLOC, 128), BF16)


    MB = 22496
    R1o, SCRo, CSTo, BIGo = MB, MB + 65536, MB + 77824, MB + 90112
    BIG_END = BIGo + 116736
    assert BIG_END <= 229344

    R1B = nc.alloc_sbuf_tensor_at("R1B", [128, 32768], BF16, offset=R1o).ap()
    SCRB = nc.alloc_sbuf_tensor_at("SCRB", [128, 6144], BF16, offset=SCRo).ap()
    CSTB = nc.alloc_sbuf_tensor_at("CSTB", [128, 6144], BF16, offset=CSTo).ap()
    BIGB = nc.alloc_sbuf_tensor_at("BIGB", [128, 58368], BF16, offset=BIGo).ap()

    def sb(name, shape, dt, off):
        for base, bo, bcols in ((R1B, R1o, 32768), (SCRB, SCRo, 6144), (CSTB, CSTo, 6144),
                                (BIGB, BIGo, 58368)):
            rel = off - bo
            if 0 <= rel < bcols * 2:
                break
        else:
            raise AssertionError(name)
        nb = int(np.prod(shape[1:])) * (4 if dt == F32 else 2)
        assert rel % 2 == 0 and rel + nb <= bcols * 2, (name, off)
        v = base[:, rel // 2: (rel + nb) // 2]
        if dt == F32:
            v = v.bitcast(F32)
        if shape[0] != 128:
            v = v[0: shape[0]]
        if len(shape) != 2:
            raise AssertionError(name)
        assert v.shape == tuple(shape), (name, v.shape, shape)
        t[name] = v

    sb("r1f", (128, 16384), F32, R1o)
    sb("ropeCt", (128, 4096), F32, R1o)
    sb("ropeSt", (128, 4096), F32, R1o + 16384)
    sb("slab0", (128, 8192), BF16, R1o + 32768)
    sb("slab1", (128, 8192), BF16, R1o + 49152)
    sb("Q2T", (128, 16384), BF16, R1o)
    sb("K2TS", (128, 16384), BF16, R1o + 32768)
    sb("W1G", (128, 32768), BF16, R1o)

    sb("xc", (128, 1024), F32, SCRo)
    sb("sq", (128, 1024), F32, SCRo + 4096)
    sb("msc", (128, 1024), F32, SCRo + 8192)
    sb("bfs", (128, 2048), BF16, SCRo + 4096)
    sb("mscb", (128, 2048), BF16, SCRo + 8192)
    sb("rowf0", (1, 2048), F32, SCRo)

    o = CSTo
    def cst(name, shape, dt, size):
        nonlocal o
        sb(name, shape, dt, o)
        o += size
    cst("identf", (128, 128), F32, 512)
    cst("identb", (128, 128), BF16, 256)
    cst("ones_r", (1, 128), BF16, 256)
    cst("vecc", (128, 32), F32, 128)
    cst("rstdb", (128, 16), F32, 64)
    cst("stat", (128, 32), F32, 128)
    cst("gwg", (128, 64), BF16, 128)
    cst("b1effc", (128, 32), F32, 128)
    cst("ewtt", (128, 32), F32, 128)
    cst("gsc", (128, 64), F32, 256)
    cst("gmk", (128, 64), BF16, 128)
    cst("lrec", (128, 8), F32, 32)
    cst("eselS", (128, 8), F32, 32)
    cst("beffq", (1, 3 * D), BF16, 6144)
    cst("beffg", (1, E), BF16, 64)
    assert o <= BIGo, o

    sb("xmy", (128, 16384), F32, BIGo)
    sb("n1tS", (128, 16384), BF16, BIGo + 65536)
    sb("QT", (128, 16384), BF16, BIGo)
    sb("KT", (128, 16384), BF16, BIGo + 32768)
    sb("VP", (128, 16640), BF16, BIGo + 65536)
    sb("WQ1", (128, 3072), BF16, BIGo + 98816)
    sb("PB0", (128, 1536), BF16, BIGo + 104960)
    sb("PB1", (128, 1536), BF16, BIGo + 108032)
    sb("obst", (128, 512), BF16, BIGo + 111104)
    sb("masks", (128, 2048), BF16, BIGo + 112128)

    sb("WO1", (128, 1024), BF16, BIGo)
    sb("oglh", (128, 128), BF16, BIGo + 2048)
    sb("ogld0", (128, 128), BF16, BIGo + 2304)
    sb("ogld1", (128, 128), BF16, BIGo + 2560)
    sb("cstg0", (128, 4096), BF16, BIGo + 4096)
    sb("cstg1", (128, 4096), BF16, BIGo + 12288)
    sb("csq", (128, 1024), F32, BIGo + 20480)

    sb("WQ2", (128, 24576), BF16, BIGo)
    sb("V2ST", (128, 2048), BF16, BIGo + 49152)
    sb("brow", (1, 4096), BF16, BIGo + 53248)
    sb("dsq", (128, 1024), F32, BIGo + 61440)
    sb("n1ib", (128, 1024), BF16, BIGo + 65536)
    sb("n1it", (128, 1024), BF16, BIGo + 67584)
    sb("qsb", (128, 512), BF16, BIGo + 69632)
    sb("csq2", (128, 1024), F32, BIGo + 70656)
    sb("b1c_bf", (128, 8), BF16, BIGo + 74752)

    sb("KB", (128, 8192), BF16, BIGo)
    sb("VHA", (128, 8192), BF16, BIGo + 16384)
    sb("VHB", (128, 8192), BF16, BIGo + 32768)
    sb("PB0e", (128, 1536), BF16, BIGo + 49152)
    sb("PB1e", (128, 1536), BF16, BIGo + 52224)
    sb("O2N", (128, 16384), BF16, BIGo + 57344)
    sb("WO2", (128, 8192), BF16, BIGo)
    sb("H2TS", (128, 16384), BF16, BIGo + 16384)
    sb("esq", (128, 1024), F32, BIGo + 49152)
    sb("o2lh", (128, 1024), BF16, BIGo + 53248)
    sb("bob", (1, D), BF16, BIGo + 55296)
    sb("n2e", (128, 1024), BF16, BIGo + 90112)

    sb("W2S", (128, 32768), BF16, BIGo)
    sb("HID", (128, 16384), BF16, BIGo + 65536)
    sb("H2TB", (128, 4096), BF16, BIGo + 98304)
    sb("MOEST", (128, 4096), BF16, BIGo + 106496)
    sb("b2eb", (1, D), BF16, BIGo + 114688)
    sb("b1effr", (1, DFF), F32, BIGo + 65536)
    sb("growb", (1, DFF), BF16, BIGo + 81920)
    sb("gbrow", (1, 64), BF16, BIGo + 92160)
    sb("b2c_bf", (128, 8), BF16, BIGo + 92288)
    sb("hsq", (128, 1024), F32, BIGo + 98304)
    sb("moef", (128, 1024), F32, BIGo + 102400)

    b.mksem("ld", "ldg", "ld2", "ld3", "ldv", "ldx", "lda", "ldb", "ldc", "cc", "dvs", "gp2", "dbg",
            "pe", "pe2", "pe3",
            "act", "act2",
            "dve", "dve2", "dve3", "dve4",
            "gp", "st")

    with b.stack, nc.Block() as blk:
        phase_P0(b, blk)
        phase_A0(b, blk)
        phase_A(b, blk)
        phase_B(b, blk)
        phase_C(b, blk)
        phase_D(b, blk)
        phase_E(b, blk)
        phase_G(b, blk)
        phase_H(b, blk)
        if debug:
            @blk.gpsimd
            def _(g):
                b.wait(g, "ldx")
                b.inc(g.dma_start(out=b.dbg["og"], in_=t["og_in"]), "dbg", 16)
                b.inc(g.dma_start(out=b.dbg["x1"], in_=t["x1_pk"]), "dbg", 16)
                b.inc(g.dma_start(out=b.dbg["xo"], in_=t["xo_rs_out"]), "dbg", 16)
                b.wait(g, "cc")
                b.wait(g, "ldc")
                b.inc(g.dma_start(out=b.dbg["u"], in_=t["u_pk"]), "dbg", 16)
                b.inc(g.dma_start(out=b.dbg["k2t"], in_=t["k2t_ag"]), "dbg", 16)
                b.inc(g.dma_start(out=b.dbg["v2h"], in_=t["v2h_ag"]), "dbg", 16)
                b.inc(g.dma_start(out=b.dbg["u2"], in_=t["u2_pk"]), "dbg", 16)
                b.inc(g.dma_start(out=b.dbg["h2t"], in_=t["h2t_ag"]), "dbg", 16)
                b.inc(g.dma_start(out=b.dbg["h2tin"], in_=t["h2t_in"]), "dbg", 16)
                b.inc(g.dma_start(out=b.dbg["rso"], in_=t["rs_out"]), "dbg", 16)
                b.inc(g.dma_start(out=b.dbg["ews"], in_=t["ewtt"][:, 0:8]), "dbg", 16)
                b.inc(g.dma_start(out=b.dbg["rsin"], in_=t["rs_in"]), "dbg", 16)
        final_wait(b, blk)
    return nc, b


def dump_debug(b, blk, items, wait):
    @blk.gpsimd
    def _(g):
        for s, v in wait:
            b.wait(g, s, v)
        for name, ap in items:
            b.inc(g.dma_start(out=b.dbg[name], in_=ap[:, :]), "dbg", 16)


def final_wait(b, blk):
    @blk.gpsimd
    def _(g):
        b.wait(g, "st")
        b.wait(g, "dbg")


def phase_P0(b, blk):
    nc, t = b.nc, b.t

    @blk.sync
    def _(s):
        b.inc(s.dma_start(out=t["identf"][:, :], in_=t["identf_d"]), "ld", 16)
        b.inc(s.dma_start(out=t["identb"][:, :], in_=t["identb_d"]), "ld", 16)
        b.inc(s.dma_start(out=t["masks"][:, :], in_=t["maskc"]), "ld", 16)
        b.inc(s.dma_start(out=t["ropeCt"][:, :], in_=t["ropeC"]), "ld", 16)
        b.inc(s.dma_start(out=t["ropeSt"][:, :], in_=t["ropeS"]), "ld", 16)
        b.inc(s.dma_start(out=t["rowf0"][0:1, 0:1024], in_=t["g1"]), "ld", 16)
        b.inc(s.dma_start(out=t["rowf0"][0:1, 1024:2048], in_=t["b1ln"]), "ld", 16)

    @blk.gpsimd
    def _(g):
        b.inc(g.dma_start(out=t["WQ1"][:, :], in_=t["wqkv1T"]), "ldg", 16)
        g.memset(t["ones_r"][:, :], 1.0)

    pe0, dv0 = b.cnt["pe"], b.cnt["dve"]
    ld_p1 = b.cnt["ld"]

    @blk.sync
    def _(s):
        b.waitv(s, "pe", pe0 + 16)
        b.inc(s.dma_start(out=t["rowf0"][0:1, 0:1024], in_=t["g2"]), "ldv", 16)
        b.inc(s.dma_start(out=t["rowf0"][0:1, 1024:2048], in_=t["b2ln"]), "ldv", 16)
    ld_p2 = b.cnt["ldv"]

    with nc.psum_tensor("pp0", [128, 128], F32) as pp0:
        @blk.tensor
        def _(tt):
            for i in range(32):
                v, k = i // 8, i % 8
                col = 1024 * (v % 2) + 128 * k
                if i < 16:
                    b.waitv(tt, "ld", ld_p1)
                else:
                    b.waitv(tt, "ldv", ld_p2)
                b.waitv(tt, "dve", dv0 + i)
                b.inc(tt.transpose(pp0[:, 0:1], t["rowf0"][0:1, col: col + 128], t["identf"][0:1, 0:1]), "pe", 1)

        @blk.vector
        def _(v):
            for i in range(32):
                b.waitv(v, "pe", pe0 + i + 1)
                b.inc(v.tensor_copy(t["vecc"][:, i: i + 1], pp0[:, 0:1]), "dve", 1)


def phase_A0(b, blk):
    nc, t = b.nc, b.t

    @blk.sync
    def _(s):
        b.inc(s.dma_start(out=t["xmy"][:, :], in_=t["x_my"]), "ldx", 16)

    pe0, dv0, ac0 = b.cnt["pe"], b.cnt["dve"], b.cnt["act"]
    pa0_cm = nc.psum_tensor("pa0", [128, 2048], BF16)
    pa0 = pa0_cm.__enter__()

    @blk.vector
    def _(v):
        b.wait(v, "ld")
        b.wait(v, "ldx")
        for j in range(NJ):
            xj = t["xmy"][:, 1024 * j: 1024 * (j + 1)]
            b.vred(v, "reduce_sum", t["stat"][:, j: j + 1], xj, AX.X)
            b.fence(v, v.tensor_scalar(t["stat"][:, 16 + j: 16 + j + 1], t["stat"][:, j: j + 1],
                                       float(INV_D), None, ALU.mult))
            b.fence(v, v.tensor_scalar(t["xc"][:, :], xj, t["stat"][:, 16 + j: 16 + j + 1], None, ALU.subtract))
            b.fence(v, v.tensor_tensor(t["sq"][:, :], t["xc"][:, :], t["xc"][:, :], ALU.mult))
            b.vred(v, "reduce_sum", t["stat"][:, j: j + 1], t["sq"][:, :], AX.X)
            b.inc(v.tensor_scalar(t["stat"][:, j: j + 1], t["stat"][:, j: j + 1], float(INV_D), float(EPS),
                                  ALU.mult, ALU.add), "dve", 1)
            b.waitv(v, "act", ac0 + j + 1)
            b.waitv(v, "pe", pe0 + 8 * j)
            b.inc(v.tensor_scalar(t["bfs"][:, 0:1024], t["xc"][:, :], t["rstdb"][:, j: j + 1], None, ALU.mult),
                  "dve", 1)
            b.waitv(v, "pe", pe0 + 8 * (j + 1))
            for k in range(8):
                b.inc(v.tensor_copy(t["n1tS"][:, 2048 * k + 128 * j: 2048 * k + 128 * (j + 1)],
                                    pa0[:, 128 * k: 128 * (k + 1)]), "dve", 1)

    @blk.scalar
    def _(sc):
        for j in range(NJ):
            b.waitv(sc, "dve", dv0 + j * 10 + 1)
            sc.activation(t["stat"][:, 16 + j: 16 + j + 1], t["stat"][:, j: j + 1], AF.Ln)
            b.inc(sc.activation(t["rstdb"][:, j: j + 1], t["stat"][:, 16 + j: 16 + j + 1], AF.Exp,
                                scale=np.float32(-0.5)), "act", 1)

    @blk.tensor
    def _(tt):
        for j in range(NJ):
            b.waitv(tt, "dve", dv0 + j * 10 + 2)
            if j >= 1:
                b.waitv(tt, "dve", dv0 + j * 10)
            for k in range(8):
                b.inc(tt.transpose(pa0[:, 128 * k: 128 * (k + 1)],
                                   t["bfs"][:, 128 * k: 128 * (k + 1)], t["identb"][:, :]), "pe", 1)

    pa0_cm.__exit__(None, None, None)

    @blk.sync
    def _(s):
        b.wait(s, "dve")
        for k in range(8):
            b.inc(s.dma_start(out=t["n1t_in"][128 * k: 128 * (k + 1), :],
                              in_=t["n1tS"][:, 2048 * k: 2048 * (k + 1)]), "ld2", 16)

    @blk.gpsimd
    def _(g):
        b.wait(g, "ld2")
        b.inc(g.collective_compute("AllGather", ALU.bypass, replica_groups=GROUPS8,
                                   ins=[t["n1t_in"].opt()], outs=[t["n1t_ag"].opt()]), "cc", 1)


def phase_A(b, blk):
    nc, t = b.nc, b.t
    slabs = [t["slab0"], t["slab1"]]
    NU = 128

    cc_need = b.cnt["cc"]
    ld2_0, ld3_0 = b.cnt["ld2"], b.cnt["ld3"]
    pe0, pe2_0 = b.cnt["pe"], b.cnt["pe2"]
    dv2_0, dv3_0, dv4_0 = b.cnt["dve2"], b.cnt["dve3"], b.cnt["dve4"]

    @blk.vector
    def _(v):
        b.wait(v, "ld2")
        v.memset(t["VP"][:, :], 0.0)
        b.inc(v.memset(t["VP"].rearrange("p (n f) -> p n f", f=65)[:, :, 64:65], 1.0), "dve2", 1)

    @blk.sync
    def _(s):
        b.waitv(s, "cc", cc_need)
        for g in range(16):
            r, half = g // 2, g % 2
            if g >= 2:
                b.waitv(s, "pe", pe0 + (g - 1) * 8)
            sem = "ld2" if g % 2 == 0 else "ld3"
            for k in range(8):
                b.inc(s.dma_start(out=slabs[g % 2][:, 1024 * k: 1024 * (k + 1)],
                                  in_=t["n1t_ag"][1024 * r + 128 * k: 1024 * r + 128 * (k + 1),
                                                  1024 * half: 1024 * (half + 1)]), sem, 16)

    pcms = [nc.psum_tensor("paq0", [128, 384], F32), nc.psum_tensor("paq1", [128, 384], F32),
            nc.psum_tensor("pat0", [128, 256], BF16), nc.psum_tensor("pat1", [128, 256], BF16)]
    hs = [cm.__enter__() for cm in pcms]
    pq, pt = hs[0:2], hs[2:4]

    def emit_transposes(tt, up):
        b.waitv(tt, "dve3", dv3_0 + up + 1)
        if up >= 2:
            b.waitv(tt, "dve4", dv4_0 + up - 1)
        ptu = pt[up % 2]
        sl = 512 * (up % 2)
        tt.transpose(ptu[:, 0:128], t["bfs"][:, sl: sl + 128], t["identb"][:, :])
        b.inc(tt.transpose(ptu[:, 128:256], t["bfs"][:, sl + 128: sl + 256], t["identb"][:, :]), "pe2", 1)

    @blk.tensor
    def _(tt):
        b.wait(tt, "ldg")
        b.waitv(tt, "dve2", dv2_0 + 1)
        for u in range(NU):
            g, j2 = u // 8, u % 8
            if g % 2 == 0:
                b.waitv(tt, "ld2", ld2_0 + (g // 2 + 1) * 128)
            else:
                b.waitv(tt, "ld3", ld3_0 + (g // 2 + 1) * 128)
            if u >= 2:
                b.waitv(tt, "dve2", dv2_0 + 1 + (u - 1))
            pqu = pq[u % 2]
            for k in range(8):
                mm = tt.matmul(pqu[:, :], slabs[g % 2][:, 1024 * k + 128 * j2: 1024 * k + 128 * (j2 + 1)],
                               t["WQ1"][:, 384 * k: 384 * (k + 1)], start=(k == 0), stop=(k == 7))
            b.inc(mm, "pe", 1)
            if u >= 1:
                emit_transposes(tt, u - 1)
        emit_transposes(tt, NU - 1)

    @blk.vector
    def _(v):
        for u in range(NU):
            g, j2 = u // 8, u % 8
            r, half = g // 2, g % 2
            tile = 16 * r + 8 * half + j2
            npos = tile % 64
            pqu = pq[u % 2]
            b.waitv(v, "pe", pe0 + u + 1)
            if u >= 2:
                b.waitv(v, "pe2", pe2_0 + u - 1)
            cosT = t["ropeCt"][:, 64 * npos: 64 * (npos + 1)]
            sinT = t["ropeSt"][:, 64 * npos: 64 * (npos + 1)]
            cos4 = cosT.unsqueeze(1).broadcast_to((128, 4, 64))
            sin32 = sinT[:, 0:32].unsqueeze(1).broadcast_to((128, 4, 32))
            qk4 = pqu.rearrange("p (a f) -> p a f", f=64)[:, 0:4, :]
            qk42 = pqu.rearrange("p (a c f) -> p a c f", c=2, f=32)[:, 0:4, :, :]
            mv = t["msc"].rearrange("p (a f) -> p a f", f=64)[:, 0:4, :]
            rv = t["xc"].rearrange("p (a c f) -> p a c f", c=2, f=32)[:, 0:4, :, :]
            b.fence(v, v.tensor_tensor(mv, qk4, cos4, ALU.mult))
            b.fence(v, v.scalar_tensor_tensor(rv[:, :, 0, :], qk42[:, :, 1, :], -1.0, sin32, ALU.mult, ALU.mult))
            b.fence(v, v.scalar_tensor_tensor(rv[:, :, 1, :], qk42[:, :, 0, :], 1.0, sin32, ALU.mult, ALU.mult))
            sl = 512 * (u % 2)
            b.inc(v.tensor_add(t["bfs"][:, sl: sl + 256], t["msc"][:, 0:256], t["xc"][:, 0:256]), "dve3", 1)
            b.inc(v.tensor_copy(t["VP"].rearrange("p (n f) -> p n f", f=65)[:, 2 * tile: 2 * tile + 2, 0:64],
                                pqu.rearrange("p (n f) -> p n f", f=64)[:, 4:6, :]), "dve2", 1)
            if u >= 1:
                copy_trans(b, v, pt, pe2_0, dv4_0, u - 1)
        copy_trans(b, v, pt, pe2_0, dv4_0, NU - 1)

    for cm in reversed(pcms):
        cm.__exit__(None, None, None)


def copy_trans(b, v, pt, pe2_0, dv4_0, up):
    t = b.t
    gp_, j2p = up // 8, up % 8
    rp, halfp = gp_ // 2, gp_ % 2
    tp = 16 * rp + 8 * halfp + j2p
    b.waitv(v, "pe2", pe2_0 + up + 1)
    ptu = pt[up % 2]
    v.tensor_copy(t["QT"][:, 128 * tp: 128 * (tp + 1)], ptu[:, 0:128])
    b.inc(v.tensor_copy(t["KT"][:, 128 * tp: 128 * (tp + 1)], ptu[:, 128:256]), "dve4", 1)


def phase_B(b, blk):
    nc, t = b.nc, b.t

    units = []
    for bt in range(2):
        for i in range(2):
            for qb in range(16):
                nch = 4 * (qb + 1)
                groups = [list(range(g, min(g + 3, nch))) for g in range(0, nch, 3)]
                units.append((bt, i, qb, nch, groups))
    NUB = len(units)
    gg_of_unit = []
    tot = 0
    for (_, _, _, nch, groups) in units:
        gg_of_unit.append(tot)
        tot += len(groups)
    NGG = tot

    pe0 = b.cnt["pe"]
    pe2_0 = b.cnt["pe2"]
    pe3_0 = b.cnt["pe3"]
    ac0 = b.cnt["act"]
    dv0 = b.cnt["dve"]
    b.cnt_dv0_B = dv0
    dv2_0 = b.cnt["dve2"]
    dv3_0 = b.cnt["dve3"]
    ld2_0 = b.cnt["ld2"]
    PB = [t["PB0"], t["PB1"]]

    def kv_ap(bt, n):
        return t["KT"][:, 8192 * bt + 128 * n: 8192 * bt + 128 * (n + 1)]

    def masked_j(qb, n):
        j = n - 4 * qb
        return j if 0 <= j < 4 else None

    mask_cnt_before = []
    mk = 0
    unit_mask_counts = []
    for (bt, i, qb, nch, groups) in units:
        mask_cnt_before.append(mk)
        nm = sum(1 for n in range(nch) if masked_j(qb, n) is not None)
        unit_mask_counts.append(nm)
        mk += nm

    with nc.psum_tensor("bsa", [128, 1536], F32) as bsa,          nc.psum_tensor("bsb", [128, 1536], F32) as bsb,          nc.psum_tensor("bop", [65, 512], F32) as bop,          nc.psum_tensor("btr", [128, 264], BF16) as btr:
        SBUFS = [bsa, bsb]

        @blk.tensor
        def _(tt):
            for u, (bt, i, qb, nch, groups) in enumerate(units):
                base = gg_of_unit[u]
                qcol = 8192 * bt + 512 * qb
                rows = slice(64 * i, 64 * i + 64)
                for g, grp in enumerate(groups):
                    gg = base + g
                    if gg >= 2:
                        b.waitv(tt, "act", ac0 + gg - 1)
                    sb_ = SBUFS[gg % 2]
                    for _ in range(16):
                        tt.matmul(sb_[:, 0:64], t["KT"][:, 0:128], t["QT"][:, 0:64],
                                  start=True, stop=True)
                    for k, n in enumerate(grp):
                        mm = tt.matmul(sb_[:, 512 * k: 512 * (k + 1)], kv_ap(bt, n)[rows, :].copy() if False else t["KT"][64 * i: 64 * i + 64, 8192 * bt + 128 * n: 8192 * bt + 128 * (n + 1)],
                                       t["QT"][64 * i: 64 * i + 64, qcol: qcol + 512],
                                       start=True, stop=True)
                    b.inc(mm, "pe", 1)
                    if g == 0:
                        b.waitv(tt, "dve2", dv2_0 + u)
                    if g >= 1:
                        emit_av(b, tt, units, u, g - 1, bop, PB, ac0, dv0, gg_of_unit, mask_cnt_before)
                emit_av(b, tt, units, u, len(groups) - 1, bop, PB, ac0, dv0, gg_of_unit, mask_cnt_before)
                for _ in range(8):
                    tt.matmul(SBUFS[0][:, 0:64], t["KT"][:, 0:128], t["QT"][:, 0:64],
                              start=True, stop=True)
                b.waitv(tt, "dve2", dv2_0 + u + 1)
                for tch in range(4):
                    mm = tt.transpose(btr[:, 66 * tch: 66 * tch + 65],
                                      t["mscb"][0:65, 128 * tch: 128 * (tch + 1)],
                                      t["identb"][0:65, 0:65])
                b.inc(mm, "pe3", 1)

        @blk.scalar
        def _(sc):
            for u, (bt, i, qb, nch, groups) in enumerate(units):
                base = gg_of_unit[u]
                for g, grp in enumerate(groups):
                    gg = base + g
                    b.waitv(sc, "pe", pe0 + gg + 1)
                    if gg >= 2:
                        b.waitv(sc, "pe2", pe2_0 + gg - 1)
                    sb_ = SBUFS[gg % 2]
                    b.inc(sc.activation(PB[gg % 2][:, 0: 512 * len(grp)], sb_[:, 0: 512 * len(grp)],
                                        AF.Exp, bias=C0, scale=INV8), "act", 1)

        @blk.vector
        def _(v):
            for u, (bt, i, qb, nch, groups) in enumerate(units):
                base = gg_of_unit[u]
                nm = 0
                for g, grp in enumerate(groups):
                    gg = base + g
                    if any(masked_j(qb, n) is not None for n in grp):
                        b.waitv(v, "act", ac0 + gg + 1)
                        for k, n in enumerate(grp):
                            j = masked_j(qb, n)
                            if j is not None:
                                sl = PB[gg % 2][:, 512 * k: 512 * (k + 1)]
                                b.inc(v.tensor_tensor(sl, sl, t["masks"][:, 512 * j: 512 * (j + 1)], ALU.mult),
                                      "dve", 1)
                                nm += 1
                assert nm == unit_mask_counts[u]
                b.waitv(v, "pe2", pe2_0 + gg_of_unit[u] + len(groups))
                b.waitv(v, "pe3", pe3_0 + u)
                b.inc(v.tensor_copy(t["mscb"][0:65, 0:512], bop[:, :]), "dve2", 1)
                b.waitv(v, "pe3", pe3_0 + u + 1)
                for tch in range(4):
                    b.fence(v, v.reciprocal(t["lrec"][:, tch: tch + 1], btr[:, 66 * tch + 64: 66 * tch + 65]))
                if u >= 2:
                    b.waitv(v, "ld2", ld2_0 + (u - 1) * 16)
                for tch in range(4):
                    i3 = v.tensor_scalar(t["obst"][:, 256 * (u % 2) + 64 * tch: 256 * (u % 2) + 64 * (tch + 1)],
                                         btr[:, 66 * tch: 66 * tch + 64], t["lrec"][:, tch: tch + 1], None, ALU.mult)
                b.inc(i3, "dve3", 1)

        @blk.sync
        def _(s):
            for u, (bt, i, qb, nch, groups) in enumerate(units):
                b.waitv(s, "dve3", dv3_0 + u + 1)
                dst = t["og_in"][8192 * bt + 512 * qb: 8192 * bt + 512 * (qb + 1), 64 * i: 64 * i + 64]
                src_ = t["obst"][:, 256 * (u % 2): 256 * (u % 2 + 1)].rearrange("p (tc f) -> p tc f", f=64)
                b.inc(s.dma_start(out=dst.rearrange("(tc p) f -> p tc f", p=128), in_=src_), "ld2", 16)



def phase_C(b, blk):
    nc, t = b.nc, b.t
    lda0 = b.cnt["lda"]
    ldb0 = b.cnt["ldb"]
    ldc0 = b.cnt["ldc"]
    ld2f = b.cnt["ld2"]
    pe0 = b.cnt["pe"]
    pe2_0 = b.cnt["pe2"]
    dv0 = b.cnt["dve"]
    dv2_0 = b.cnt["dve2"]
    cc0 = b.cnt["cc"]
    ogld = [t["ogld0"], t["ogld1"]]
    cstg = [t["cstg0"], t["cstg1"]]

    @blk.gpsimd
    def _(g):
        b.waitv(g, "ld2", ld2f)
        b.inc(g.dma_start(out=t["WO1"][:, :], in_=t["wo1c"]), "ldg", 16)
    ldg_w = b.cnt["ldg"]

    tiles = [(kp, cp, mm_) for kp in range(4) for cp in range(8) for mm_ in range(4)]

    with nc.psum_tensor("ctr0", [128, 128], BF16) as ctr0,          nc.psum_tensor("ctr1", [128, 128], BF16) as ctr1,          nc.psum_tensor("cxo0", [128, 1024], F32) as cxo0,          nc.psum_tensor("cxo1", [128, 1024], F32) as cxo1:
        ctr = [ctr0, ctr1]
        cxo = [cxo0, cxo1]

        @blk.sync
        def _(s):
            b.waitv(s, "ld2", ld2f)
            for u, (kp, cp, mm_) in enumerate(tiles):
                row = 2048 * cp + 512 * kp + 128 * mm_
                if u >= 2:
                    b.waitv(s, "pe", pe0 + u - 1)
                sem = "lda" if u % 2 == 0 else "ldb"
                b.inc(s.dma_start(out=ogld[u % 2][:, :], in_=t["og_in"][row: row + 128, :]), sem, 16)
                if u % 4 == 3:
                    grp = u // 4
                    kpg, cpg = grp // 8, grp % 8
                    b.waitv(s, "dve2", dv2_0 + (grp + 1) * 4)
                    dst = t["xo_rs_in"][4096 * kpg + 512 * cpg: 4096 * kpg + 512 * (cpg + 1), :]
                    b.inc(s.dma_start(out=dst.rearrange("(m p) f -> p m f", p=128),
                                      in_=cstg[grp % 2].rearrange("p (m f) -> p m f", f=1024)), "ldc", 16)

        @blk.tensor
        def _(tt):
            b.waitv(tt, "ldg", ldg_w)
            for u, (kp, cp, mm_) in enumerate(tiles):
                if u % 2 == 0:
                    b.waitv(tt, "lda", lda0 + (u // 2 + 1) * 16)
                else:
                    b.waitv(tt, "ldb", ldb0 + (u // 2 + 1) * 16)
                if u >= 2:
                    b.waitv(tt, "dve", dv0 + u - 1)
                b.inc(tt.transpose(ctr[u % 2][:, :], ogld[u % 2][:, :], t["identb"][:, :]), "pe", 1)
                b.waitv(tt, "dve", dv0 + u + 1)
                if u >= 2:
                    b.waitv(tt, "dve2", dv2_0 + u - 1)
                mm = tt.matmul(cxo[u % 2][:, 0:512], t["oglh"][:, :], t["WO1"][:, 0:512], start=True, stop=True)
                mm = tt.matmul(cxo[u % 2][:, 512:1024], t["oglh"][:, :], t["WO1"][:, 512:1024], start=True, stop=True)
                b.inc(mm, "pe2", 1)

        @blk.vector
        def _(v):
            for u, (kp, cp, mm_) in enumerate(tiles):
                b.waitv(v, "pe", pe0 + u + 1)
                if u >= 1:
                    b.waitv(v, "pe2", pe2_0 + u)
                b.inc(v.tensor_copy(t["oglh"][:, :], ctr[u % 2][:, :]), "dve", 1)
                b.waitv(v, "pe2", pe2_0 + u + 1)
                gpar = (u // 4) % 2
                if u >= 8:
                    b.waitv(v, "ldc", ldc0 + (u // 4 - 1) * 16)
                b.inc(v.tensor_copy(cstg[gpar][:, 1024 * mm_: 1024 * (mm_ + 1)], cxo[u % 2][:, :]), "dve2", 1)


    @blk.gpsimd
    def _(g):
        for kp in range(4):
            b.waitv(g, "ldc", ldc0 + (8 * kp + 8) * 16)
            b.inc(g.collective_compute("ReduceScatter", ALU.add, replica_groups=GROUPS8,
                                       ins=[t["xo_rs_in"][4096 * kp: 4096 * (kp + 1), :].opt()],
                                       outs=[t["xo_rs_out"][512 * kp: 512 * (kp + 1), :].opt()]), "cc", 1)

    @blk.sync
    def _(s):
        b.inc(s.dma_start(out=t["r1f"][:, :], in_=t["x_my"]), "ldx", 16)
    ldx_f = b.cnt["ldx"]

    dvf0 = b.cnt["dve3"]
    acf0 = b.cnt["act2"]

    @blk.gpsimd
    def _(g):
        for j in range(NJ):
            kp = j // 4
            b.waitv(g, "cc", cc0 + kp + 1)
            if j >= 2:
                b.waitv(g, "dve3", dvf0 + (j - 1) * 2)
            xb = t["xc"] if j % 2 == 0 else t["msc"]
            row = 128 * j
            b.inc(g.dma_start(out=xb[:, :], in_=t["xo_rs_out"][row: row + 128, :]),
                  "gp" if j % 2 == 0 else "gp2", 16)

    @blk.vector
    def _(v):
        b.waitv(v, "ldx", ldx_f)
        for j in range(NJ):
            xb = t["xc"] if j % 2 == 0 else t["msc"]
            sem = "gp" if j % 2 == 0 else "gp2"
            b.waitv(v, sem, b.cnt[sem] - 16 * (NJ // 2 - 1 - j // 2))
            b.vred(v, "reduce_sum", t["stat"][:, j: j + 1], xb[:, :], AX.X)
            b.fence(v, v.tensor_scalar(t["stat"][:, 16 + j: 16 + j + 1], t["stat"][:, j: j + 1],
                                       float(INV_D), None, ALU.mult))
            b.fence(v, v.tensor_scalar(t["sq"][:, :], xb[:, :], t["stat"][:, 16 + j: 16 + j + 1],
                                       None, ALU.subtract))
            b.fence(v, v.tensor_tensor(t["csq"][:, :], t["sq"][:, :], t["sq"][:, :], ALU.mult))
            b.vred(v, "reduce_sum", t["stat"][:, j: j + 1], t["csq"][:, :], AX.X)
            b.inc(v.tensor_scalar(t["stat"][:, j: j + 1], t["stat"][:, j: j + 1], float(INV_D), float(EPS),
                                  ALU.mult, ALU.add), "dve3", 1)
            b.waitv(v, "act2", acf0 + j + 1)
            b.fence(v, v.tensor_scalar(t["sq"][:, :], t["sq"][:, :], t["rstdb"][:, j: j + 1], None, ALU.mult))
            b.inc(v.tensor_tensor(t["r1f"][:, 1024 * j: 1024 * (j + 1)], t["r1f"][:, 1024 * j: 1024 * (j + 1)],
                                  t["sq"][:, :], ALU.add), "dve3", 1)

    @blk.scalar
    def _(sc):
        for j in range(NJ):
            b.waitv(sc, "dve3", dvf0 + 2 * j + 1)
            sc.activation(t["stat"][:, 16 + j: 16 + j + 1], t["stat"][:, j: j + 1], AF.Ln)
            b.inc(sc.activation(t["rstdb"][:, j: j + 1], t["stat"][:, 16 + j: 16 + j + 1], AF.Exp,
                                scale=np.float32(-0.5)), "act2", 1)

    @blk.sync
    def _(s):
        b.wait(s, "dve3")
        b.inc(s.dma_start(out=t["x1_pk"], in_=t["r1f"][:, :]), "ldx", 16)


def phase_D(b, blk):
    nc, t = b.nc, b.t
    dvf = b.cnt["dve3"]
    ldxf = b.cnt["ldx"]

    @blk.gpsimd
    def _(g):
        b.waitv(g, "dve3", dvf)
        b.inc(g.dma_start(out=t["WQ2"][:, :], in_=t["wqkv2"]), "ldg", 16)
        b.inc(g.dma_start(out=t["brow"][0:1, 0:3072], in_=t["bqkv"]), "ldg", 16)
    ldg_d = b.cnt["ldg"]

    pe_be0 = b.cnt["pe"]
    dv_be0 = b.cnt["dve"]
    with nc.psum_tensor("dbe", [1, 512], F32) as dbe:
        @blk.vector
        def _(v):
            b.inc(v.tensor_copy(t["b1c_bf"][:, 0:8], t["vecc"][:, 8:16]), "dve", 1)

        @blk.tensor
        def _(tt):
            b.waitv(tt, "ldg", ldg_d)
            b.waitv(tt, "dve", dv_be0 + 1)
            for cb in range(6):
                if cb >= 1:
                    b.waitv(tt, "dve", dv_be0 + 1 + cb)
                for k in range(8):
                    tt.matmul(dbe[:, :], t["b1c_bf"][:, k: k + 1],
                              t["WQ2"][:, 3072 * k + 512 * cb: 3072 * k + 512 * (cb + 1)],
                              start=(k == 0), stop=False)
                mm = tt.matmul(dbe[:, :], t["ones_r"][0:1, 0:1], t["brow"][0:1, 512 * cb: 512 * (cb + 1)],
                               start=False, stop=True)
                b.inc(mm, "pe", 1)

        @blk.vector
        def _(v):
            for cb in range(6):
                b.waitv(v, "pe", pe_be0 + cb + 1)
                b.inc(v.tensor_copy(t["beffq"][0:1, 512 * cb: 512 * (cb + 1)], dbe[0:1, :]), "dve", 1)
            for k in range(8):
                i2 = v.tensor_scalar(t["WQ2"][:, 3072 * k: 3072 * (k + 1)], t["WQ2"][:, 3072 * k: 3072 * (k + 1)],
                                     t["vecc"][:, k: k + 1], None, ALU.mult)
            b.inc(i2, "dve2", 1)
            v.memset(t["V2ST"][:, :], 0.0)
            b.inc(v.memset(t["V2ST"].rearrange("p (h f) -> p h f", f=128)[:, :, 64:65], 1.0), "dve2", 1)
    dv2_w = b.cnt["dve2"]

    pe0 = b.cnt["pe"]
    pe2_0 = b.cnt["pe2"]
    pe3_0 = b.cnt["pe3"]
    ac0 = b.cnt["act"]
    ac2_0 = b.cnt["act2"]
    dv0 = b.cnt["dve"]
    dv3_0 = b.cnt["dve3"]
    dv4_0 = b.cnt["dve4"]
    lda0, ldb0 = b.cnt["lda"], b.cnt["ldb"]
    ldc0 = b.cnt["ldc"]
    ld2_0, ld3_0 = b.cnt["ld2"], b.cnt["ld3"]

    with nc.psum_tensor("pdtr", [128, 1024], BF16) as pdtr, \
         nc.psum_tensor("pdm0", [128, 512], F32) as pdm0, \
         nc.psum_tensor("pdm1", [128, 512], F32) as pdm1, \
         nc.psum_tensor("pdq0", [128, 512], BF16) as pdq0, \
         nc.psum_tensor("pdq1", [128, 512], BF16) as pdq1:
        pdm = [pdm0, pdm1]
        pdq = [pdq0, pdq1]

        @blk.sync
        def _(s):
            b.waitv(s, "ldx", ldxf)
            for j in range(NJ):
                sem = "lda" if j % 2 == 0 else "ldb"
                if j >= 2:
                    b.waitv(s, "dve", dv0 + (j - 2) * 9 + 1)
                b.inc(s.dma_start(out=(t["xc"] if j % 2 == 0 else t["dsq"])[:, :],
                                  in_=t["x1_pk"][:, 1024 * j: 1024 * (j + 1)]), sem, 16)
                if j >= 1:
                    b.waitv(s, "dve", dv0 + (j - 1) * 9 + 2)
                    b.inc(s.dma_start(out=t["u_pk"][:, 1024 * (j - 1): 1024 * j], in_=t["msc"][:, :]), "ldc", 16)
                    b.waitv(s, "dve", dv0 + 9 * (j - 1) + 9)
                    dstv = t["v2h_in"].rearrange("(h jj p) f -> p h jj f", h=H, p=128)[:, :, j - 1, :]
                    b.inc(s.dma_start(out=dstv, in_=t["V2ST"].rearrange("p (h f) -> p h f", f=128)), "ld2", 16)
            b.waitv(s, "dve", dv0 + (NJ - 1) * 9 + 2)
            b.inc(s.dma_start(out=t["u_pk"][:, 1024 * (NJ - 1): 1024 * NJ], in_=t["msc"][:, :]), "ldc", 16)
            b.waitv(s, "dve", dv0 + 9 * NJ)
            dstv = t["v2h_in"].rearrange("(h jj p) f -> p h jj f", h=H, p=128)[:, :, NJ - 1, :]
            b.inc(s.dma_start(out=dstv, in_=t["V2ST"].rearrange("p (h f) -> p h f", f=128)), "ld2", 16)
            b.waitv(s, "dve4", dv4_0 + 4 * NJ)
            for k in range(8):
                b.inc(s.dma_start(out=t["k2t_in"][128 * k: 128 * (k + 1), :],
                                  in_=t["K2TS"][:, 2048 * k: 2048 * (k + 1)]), "ld3", 16)

        @blk.vector
        def _(v):
            for j in range(NJ):
                xb = t["xc"] if j % 2 == 0 else t["dsq"]
                sqs = t["sq"]
                st_ = t["stat"]
                sem = "lda" if j % 2 == 0 else "ldb"
                b.waitv(v, sem, (lda0 if j % 2 == 0 else ldb0) + (j // 2 + 1) * 16)
                if j >= 1:
                    b.waitv(v, "ldc", ldc0 + j * 16)
                b.vred(v, "reduce_sum", st_[:, 0:1], xb[:, :], AX.X)
                b.fence(v, v.tensor_scalar(st_[:, 1:2], st_[:, 0:1], float(INV_D), None, ALU.mult))
                b.fence(v, v.tensor_scalar(sqs[:, :], xb[:, :], st_[:, 1:2], None, ALU.subtract))
                b.fence(v, v.tensor_tensor(t["csq2"][:, :], sqs[:, :], sqs[:, :], ALU.mult))
                b.vred(v, "reduce_sum", st_[:, 0:1], t["csq2"][:, :], AX.X)
                b.inc(v.tensor_scalar(st_[:, 0:1], st_[:, 0:1], float(INV_D), float(EPS),
                                      ALU.mult, ALU.add), "dve3", 1)
                b.waitv(v, "act", ac0 + j + 1)
                b.inc(v.tensor_scalar(t["msc"][:, :], sqs[:, :], t["rstdb"][:, 0:1], None, ALU.mult), "dve", 1)
                b.fence(v, v.tensor_copy(t["gsc"][:, 0:1], t["msc"][:, 1023:1024]))
                b.vred(v, "reduce_sum", st_[:, 2:3], t["msc"][:, :], AX.X)
                b.fence(v, v.tensor_scalar(st_[:, 3:4], st_[:, 2:3], float(INV_D), None, ALU.mult))
                b.fence(v, v.tensor_scalar(sqs[:, :], t["msc"][:, :], st_[:, 3:4], None, ALU.subtract))
                b.fence(v, v.tensor_tensor(t["csq2"][:, :], sqs[:, :], sqs[:, :], ALU.mult))
                b.vred(v, "reduce_sum", st_[:, 2:3], t["csq2"][:, :], AX.X)
                b.inc(v.tensor_scalar(st_[:, 2:3], st_[:, 2:3], float(INV_D), float(EPS),
                                      ALU.mult, ALU.add), "dve3", 1)
                b.waitv(v, "act2", ac2_0 + j + 1)
                if j >= 1:
                    b.waitv(v, "pe", pe0 + j)
                b.inc(v.tensor_scalar(t["n1ib"][:, :], sqs[:, :], t["rstdb"][:, 1:2], None, ALU.mult), "dve", 1)
                b.waitv(v, "pe", pe0 + j + 1)
                if j >= 1:
                    b.waitv(v, "pe2", pe2_0 + 6 * j)
                b.inc(v.tensor_copy(t["n1it"][:, :], pdtr[:, :]), "dve", 1)
                def qkcopy(cb2):
                    b.waitv(v, "pe3", pe3_0 + j * 4 + cb2 + 1)
                    pq = pdq[cb2 % 2]
                    dstb = t["Q2T"] if cb2 < 2 else t["K2TS"]
                    for s4 in range(4):
                        kp = 4 * (cb2 % 2) + s4
                        i5 = v.tensor_copy(dstb[:, 2048 * kp + 128 * j: 2048 * kp + 128 * (j + 1)],
                                           pq[:, 128 * s4: 128 * (s4 + 1)])
                    b.inc(i5, "dve4", 1)
                for cb in range(6):
                    b.waitv(v, "pe2", pe2_0 + 6 * j + cb + 1)
                    pm = pdm[cb % 2]
                    if cb < 4:
                        nprev = j * 4 + cb
                        if nprev >= 1:
                            b.waitv(v, "pe3", pe3_0 + nprev)
                        b.inc(v.tensor_copy(t["qsb"][:, :], pm[:, :]), "dve", 1)
                    else:
                        hb = 8 * (cb - 4)
                        if cb == 4 and j >= 1:
                            b.waitv(v, "ld2", ld2_0 + 16 * j)
                        b.inc(v.tensor_copy(
                            t["V2ST"].rearrange("p (h f) -> p h f", f=128)[:, hb: hb + 8, 0:64],
                            pm.rearrange("p (h f) -> p h f", f=64)[:, 0:8, :]), "dve", 1)
                    if 1 <= cb <= 4:
                        qkcopy(cb - 1)

        @blk.scalar
        def _(sc):
            for j in range(NJ):
                b.waitv(sc, "dve3", dv3_0 + 2 * j + 1)
                sc.activation(t["stat"][:, 1:2], t["stat"][:, 0:1], AF.Ln)
                b.inc(sc.activation(t["rstdb"][:, 0:1], t["stat"][:, 1:2], AF.Exp, scale=np.float32(-0.5)),
                      "act", 1)
                b.waitv(sc, "dve3", dv3_0 + 2 * j + 2)
                sc.activation(t["stat"][:, 3:4], t["stat"][:, 2:3], AF.Ln)
                b.inc(sc.activation(t["rstdb"][:, 1:2], t["stat"][:, 3:4], AF.Exp, scale=np.float32(-0.5)),
                      "act2", 1)

        @blk.tensor
        def _(tt):
            b.waitv(tt, "dve2", dv2_w)
            b.waitv(tt, "ldx", ldxf)
            for j in range(NJ):
                b.waitv(tt, "dve", dv0 + 9 * j + 2)
                for k in range(8):
                    mm = tt.transpose(pdtr[:, 128 * k: 128 * (k + 1)], t["n1ib"][:, 128 * k: 128 * (k + 1)],
                                      t["identb"][:, :])
                b.inc(mm, "pe", 1)
                b.waitv(tt, "dve", dv0 + 9 * j + 3)
                for cb in range(6):
                    pm = pdm[cb % 2]
                    nprv = j * 6 + cb
                    if nprv >= 2:
                        pj, pcb = (nprv - 2) // 6, (nprv - 2) % 6
                        b.waitv(tt, "dve", dv0 + 9 * pj + 4 + pcb)
                    for k in range(8):
                        tt.matmul(pm[:, :], t["n1it"][:, 128 * k: 128 * (k + 1)],
                                  t["WQ2"][:, 3072 * k + 512 * cb: 3072 * k + 512 * (cb + 1)],
                                  start=(k == 0), stop=False)
                    mm = tt.matmul(pm[:, :], t["ones_r"][0:1, 0:128], t["beffq"][0:1, 512 * cb: 512 * (cb + 1)],
                                   start=False, stop=True)
                    b.inc(mm, "pe2", 1)
                    if cb < 4:
                        b.waitv(tt, "dve", dv0 + 9 * j + 4 + cb)
                        pq = pdq[cb % 2]
                        ntr = j * 4 + cb
                        if ntr >= 2:
                            b.waitv(tt, "dve4", dv4_0 + ntr - 1)
                        for s4 in range(4):
                            mm2 = tt.transpose(pq[:, 128 * s4: 128 * (s4 + 1)],
                                               t["qsb"][:, 128 * s4: 128 * (s4 + 1)], t["identb"][:, :])
                        b.inc(mm2, "pe3", 1)

    @blk.gpsimd
    def _(g):
        b.wait(g, "ld3")
        b.inc(g.collective_compute("AllGather", ALU.bypass, replica_groups=GROUPS4,
                                   ins=[t["k2t_in"].opt()], outs=[t["k2t_ag"].opt()]), "cc", 1)
        b.wait(g, "ld2")
        b.inc(g.collective_compute("AllGather", ALU.bypass, replica_groups=GROUPS4,
                                   ins=[t["v2h_in"].opt()], outs=[t["v2h_ag"].opt()]), "cc", 1)


def phase_E(b, blk):
    nc, t = b.nc, b.t
    ccD = b.cnt["cc"]
    PBe = [t["PB0e"], t["PB1e"]]

    units = [(hp, qb) for hp in range(8) for qb in range(4)]
    NGE = 64

    pe0 = b.cnt["pe"]
    pe2_0 = b.cnt["pe2"]
    pe3_0 = b.cnt["pe3"]
    ac0 = b.cnt["act"]
    dv2_0 = b.cnt["dve2"]
    dv3_0 = b.cnt["dve3"]
    lda0, ldb0 = b.cnt["lda"], b.cnt["ldb"]

    @blk.sync
    def _(s):
        b.waitv(s, "cc", ccD)
        for hp in range(8):
            sem = "lda" if hp % 2 == 0 else "ldb"
            if hp >= 1:
                b.waitv(s, "pe2", pe2_0 + NGE * 4 * hp)
            for h in (2 * hp, 2 * hp + 1):
                for r2 in range(4):
                    b.inc(s.dma_start(out=t["KB"][64 * (h % 2): 64 * (h % 2) + 64, 2048 * r2: 2048 * (r2 + 1)],
                                      in_=t["k2t_ag"][1024 * r2 + 64 * h: 1024 * r2 + 64 * (h + 1), :]), sem, 16)
                vh = t["VHA"] if h % 2 == 0 else t["VHB"]
                for r2 in range(4):
                    b.inc(s.dma_start(
                        out=vh.rearrange("p (rb tt f) -> p rb tt f", rb=4, f=128)[:, r2, :, :],
                        in_=t["v2h_ag"][32768 * r2 + 2048 * h: 32768 * r2 + 2048 * (h + 1), :]
                            .rearrange("(tt p) f -> p tt f", p=128)), sem, 16)

    with nc.psum_tensor("spa", [128, 1024], F32) as spa, \
         nc.psum_tensor("spb", [128, 1024], F32) as spb, \
         nc.psum_tensor("eop0", [128, 512], F32) as eop0, \
         nc.psum_tensor("eop1", [128, 512], F32) as eop1, \
         nc.psum_tensor("etr", [128, 264], BF16) as etr:
        SB2 = [spa, spb]
        EOP = [eop0, eop1]

        @blk.tensor
        def _(tt):
            for u, (hp, qb) in enumerate(units):
                qcol = 2048 * hp + 512 * qb
                if qb == 0:
                    sem = "lda" if hp % 2 == 0 else "ldb"
                    b.waitv(tt, sem, (lda0 if hp % 2 == 0 else ldb0) + (hp // 2 + 1) * 16 * 16)
                for g in range(NGE):
                    gg = NGE * u + g
                    if gg >= 2:
                        b.waitv(tt, "act", ac0 + gg - 1)
                    sb_ = SB2[gg % 2]
                    tt.matmul(sb_[:, 0:512],
                              t["KB"][0:64, 128 * g: 128 * (g + 1)],
                              t["Q2T"][0:64, qcol: qcol + 512],
                              start=True, stop=True)
                    mm = tt.matmul(sb_[:, 512:1024],
                                   t["KB"][64:128, 128 * g: 128 * (g + 1)],
                                   t["Q2T"][64:128, qcol: qcol + 512],
                                   start=True, stop=True)
                    b.inc(mm, "pe", 1)
                    if g == 0:
                        b.waitv(tt, "dve2", dv2_0 + 2 * u)
                    if g >= 1:
                        emit_av_e(b, tt, u, g - 1, EOP, PBe, ac0)
                emit_av_e(b, tt, u, NGE - 1, EOP, PBe, ac0)
                for p in range(2):
                    b.waitv(tt, "dve2", dv2_0 + 2 * u + p + 1)
                    for tch in range(4):
                        mm = tt.transpose(etr[:, 66 * tch: 66 * tch + 65],
                                          t["mscb"][0:65, 128 * tch: 128 * (tch + 1)],
                                          t["identb"][0:65, 0:65])
                    b.inc(mm, "pe3", 1)

        @blk.scalar
        def _(sc):
            for u in range(len(units)):
                for g in range(NGE):
                    gg = NGE * u + g
                    b.waitv(sc, "pe", pe0 + gg + 1)
                    if gg >= 2:
                        b.waitv(sc, "pe2", pe2_0 + gg - 1)
                    b.inc(sc.activation(PBe[gg % 2][:, 0:1024], SB2[gg % 2][:, 0:1024],
                                        AF.Exp, bias=C0, scale=INV8), "act", 1)

        @blk.vector
        def _(v):
            for u, (hp, qb) in enumerate(units):
                b.waitv(v, "pe2", pe2_0 + NGE * (u + 1))
                for p in range(2):
                    h = 2 * hp + p
                    b.waitv(v, "pe3", pe3_0 + 2 * u + p)
                    b.inc(v.tensor_copy(t["mscb"][0:65, 0:512], EOP[p][0:65, :]), "dve2", 1)
                    b.waitv(v, "pe3", pe3_0 + 2 * u + p + 1)
                    for tch in range(4):
                        b.fence(v, v.reciprocal(t["lrec"][:, tch: tch + 1], etr[:, 66 * tch + 64: 66 * tch + 65]))
                    for tch in range(4):
                        j = 4 * qb + tch
                        i3 = v.tensor_scalar(t["O2N"][:, 1024 * j + 64 * h: 1024 * j + 64 * (h + 1)],
                                             etr[:, 66 * tch: 66 * tch + 64], t["lrec"][:, tch: tch + 1],
                                             None, ALU.mult)
                b.inc(i3, "dve3", 1)

    phase_E_end(b, blk)


def phase_G(b, blk):
    nc, t = b.nc, b.t
    ccE = b.cnt["cc"]
    dvEe = b.cnt["dve"]
    ldcE = b.cnt["ldc"]
    ld3E = b.cnt["ld3"]

    @blk.gpsimd
    def _(g):
        b.waitv(g, "dve", dvEe)
        b.waitv(g, "ldc", ldcE)
        b.waitv(g, "ld3", ld3E)
        b.inc(g.dma_start(out=t["W1G"][:, :], in_=t["w1e"]), "ldg", 16)
        b.inc(g.dma_start(out=t["W2S"][:, :], in_=t["w2e"]), "ldg", 16)
        b.inc(g.dma_start(out=t["growb"][0:1, :], in_=t["b1e"]), "ldg", 16)
        b.inc(g.dma_start(out=t["gbrow"][0:1, 0:E], in_=t["gateb"]), "ldg", 16)
        b.inc(g.dma_start(out=t["b2eb"][0:1, :], in_=t["b2e"]), "ldg", 16)
        b.inc(g.dma_start(out=t["gwg"][:, :], in_=t["gatew"]), "ldg", 16)
        b.inc(g.dma_start(out=t["eselS"][:, :], in_=t["esel"]), "ldg", 16)
    ldg_g = b.cnt["ldg"]

    pe_p0 = b.cnt["pe"]
    dv_p0 = b.cnt["dve"]
    with nc.psum_tensor("gbe", [1, 512], F32) as gbe, \
         nc.psum_tensor("gbc", [128, 128], F32) as gbc:
        @blk.vector
        def _(v):
            b.waitv(v, "ldg", ldg_g)
            b.inc(v.tensor_copy(t["b2c_bf"][:, 0:8], t["vecc"][:, 24:32]), "dve", 1)

        @blk.tensor
        def _(tt):
            b.waitv(tt, "ldg", ldg_g)
            b.waitv(tt, "dve", dv_p0 + 1)
            for fb in range(8):
                if fb >= 1:
                    b.waitv(tt, "dve", dv_p0 + 1 + fb)
                for k in range(8):
                    tt.matmul(gbe[:, :], t["b2c_bf"][:, k: k + 1],
                              t["W1G"][:, 4096 * k + 512 * fb: 4096 * k + 512 * (fb + 1)],
                              start=(k == 0), stop=False)
                mm = tt.matmul(gbe[:, :], t["ones_r"][0:1, 0:1], t["growb"][0:1, 512 * fb: 512 * (fb + 1)],
                               start=False, stop=True)
                b.inc(mm, "pe", 1)
            b.waitv(tt, "dve", dv_p0 + 9)
            for k in range(8):
                tt.matmul(gbe[:, 0:E], t["b2c_bf"][:, k: k + 1], t["gwg"][:, 8 * k: 8 * (k + 1)],
                          start=(k == 0), stop=False)
            mm = tt.matmul(gbe[:, 0:E], t["ones_r"][0:1, 0:1], t["gbrow"][0:1, 0:E],
                           start=False, stop=True)
            b.inc(mm, "pe", 1)
            for m in range(32):
                b.waitv(tt, "dve", dv_p0 + 11 + m)
                b.inc(tt.transpose(gbc[:, 0:1], t["b1effr"][0:1, 128 * m: 128 * (m + 1)],
                                   t["identf"][0:1, 0:1]), "pe", 1)

        @blk.vector
        def _(v):
            for fb in range(8):
                b.waitv(v, "pe", pe_p0 + fb + 1)
                b.inc(v.tensor_copy(t["b1effr"][0:1, 512 * fb: 512 * (fb + 1)], gbe[0:1, :]), "dve", 1)
            b.waitv(v, "pe", pe_p0 + 9)
            b.inc(v.tensor_copy(t["beffg"][0:1, 0:E], gbe[0:1, 0:E]), "dve", 1)
            for k in range(8):
                v.tensor_scalar(t["W1G"][:, 4096 * k: 4096 * (k + 1)], t["W1G"][:, 4096 * k: 4096 * (k + 1)],
                                t["vecc"][:, 16 + k: 17 + k], None, ALU.mult)
            i2 = None
            for k in range(8):
                i2 = v.tensor_scalar(t["gwg"][:, 8 * k: 8 * (k + 1)], t["gwg"][:, 8 * k: 8 * (k + 1)],
                                     t["vecc"][:, 16 + k: 17 + k], None, ALU.mult)
            b.inc(i2, "dve", 1)
            for m in range(32):
                b.waitv(v, "pe", pe_p0 + 10 + m)
                b.inc(v.tensor_copy(t["b1effc"][:, m: m + 1], gbc[:, 0:1]), "dve", 1)
    dv_pw = b.cnt["dve"]

    pe0 = b.cnt["pe"]
    pe2_0 = b.cnt["pe2"]
    pe3_0 = b.cnt["pe3"]
    ac0 = b.cnt["act"]
    dv0 = b.cnt["dve2"]
    dv3_0 = b.cnt["dve3"]
    dv4_0 = b.cnt["dve4"]
    lda0, ldb0 = b.cnt["lda"], b.cnt["ldb"]
    ldc0 = b.cnt["ldc"]
    cc0 = b.cnt["cc"]
    order = [(kp, cp) for kp in range(4) for cp in range(8)]

    with nc.psum_tensor("glg0", [128, 16], F32) as glg0, \
         nc.psum_tensor("glg1", [128, 16], F32) as glg1, \
         nc.psum_tensor("gph0", [128, 512], F32) as gph0, \
         nc.psum_tensor("gph1", [128, 512], F32) as gph1, \
         nc.psum_tensor("gpm0", [128, 512], F32) as gpm0, \
         nc.psum_tensor("gpm1", [128, 512], F32) as gpm1:
        glg = [glg0, glg1]
        gph = [gph0, gph1]
        gpm = [gpm0, gpm1]

        @blk.sync
        def _(s):
            b.waitv(s, "cc", ccE)
            for u, (kp, cp) in enumerate(order):
                TT = 4 * cp + kp
                rr, colb = TT // 4, TT % 4
                sem = "lda" if u % 2 == 0 else "ldb"
                if u >= 1:
                    b.waitv(s, "pe", pe0 + 32 * u)
                for k in range(8):
                    b.inc(s.dma_start(out=t["H2TB"][:, 512 * k: 512 * (k + 1)],
                                      in_=t["h2t_ag"][1024 * rr + 128 * k: 1024 * rr + 128 * (k + 1),
                                                      512 * colb: 512 * (colb + 1)]), sem, 16)
                if u >= 1:
                    pkp, pcp = order[u - 1]
                    pTT = 4 * pcp + pkp
                    b.waitv(s, "dve4", dv4_0 + 4 * u)
                    dst = t["rs_in"][4096 * pkp + 512 * pcp: 4096 * pkp + 512 * (pcp + 1), :]
                    b.inc(s.dma_start(out=dst.rearrange("(m p) f -> p m f", p=128),
                                      in_=t["MOEST"].rearrange("p (m f) -> p m f", f=1024)), "ldc", 16)
            pkp, pcp = order[-1]
            b.waitv(s, "dve4", dv4_0 + 4 * 32)
            dst = t["rs_in"][4096 * pkp + 512 * pcp: 4096 * pkp + 512 * (pcp + 1), :]
            b.inc(s.dma_start(out=dst.rearrange("(m p) f -> p m f", p=128),
                              in_=t["MOEST"].rearrange("p (m f) -> p m f", f=1024)), "ldc", 16)

        @blk.tensor
        def _(tt):
            b.waitv(tt, "dve", dv_pw)
            b.waitv(tt, "cc", ccE)
            for u, (kp, cp) in enumerate(order):
                sem = "lda" if u % 2 == 0 else "ldb"
                b.waitv(tt, sem, (lda0 if u % 2 == 0 else ldb0) + (u // 2 + 1) * 8 * 16)
                for j2 in range(4):
                    if 4 * u + j2 >= 2:
                        if j2 >= 2:
                            b.waitv(tt, "dve2", dv0 + 36 * u + 5 * (j2 - 2) + 1)
                        else:
                            b.waitv(tt, "dve2", dv0 + 36 * (u - 1) + 5 * (j2 + 2) + 1)
                    pl = glg[j2 % 2]
                    for k in range(8):
                        tt.matmul(pl[:, 0:E], t["H2TB"][:, 512 * k + 128 * j2: 512 * k + 128 * (j2 + 1)],
                                  t["gwg"][:, 8 * k: 8 * (k + 1)], start=(k == 0), stop=False)
                    mm = tt.matmul(pl[:, 0:E], t["ones_r"][0:1, 0:128], t["beffg"][0:1, 0:E],
                                   start=False, stop=True)
                    b.inc(mm, "pe3", 1)
                for m in range(32):
                    ph = gph[m % 2]
                    if 32 * u + m >= 2:
                        b.waitv(tt, "dve3", dv3_0 + 32 * u + m - 1)
                    for k in range(8):
                        mm = tt.matmul(ph[:, :], t["W1G"][:, 4096 * k + 128 * m: 4096 * k + 128 * (m + 1)],
                                       t["H2TB"][:, 512 * k: 512 * (k + 1)], start=(k == 0), stop=(k == 7))
                    b.inc(mm, "pe", 1)
                for j2 in range(4):
                    b.waitv(tt, "dve3", dv3_0 + 32 * (u + 1))
                    if 4 * u + j2 >= 1:
                        b.waitv(tt, "dve4", dv4_0 + 4 * u + j2)
                    b.waitv(tt, "dve2", dv0 + 36 * u + 20 + 4 * (j2 + 1))
                    for cb in range(2):
                        pm = gpm[cb]
                        for m in range(32):
                            tt.matmul(pm[:, :], t["HID"][:, 512 * m + 128 * j2: 512 * m + 128 * (j2 + 1)],
                                      t["W2S"][:, 1024 * m + 512 * cb: 1024 * m + 512 * (cb + 1)],
                                      start=(m == 0), stop=False)
                        mm = tt.matmul(pm[:, :], t["ones_r"][0:1, 0:128],
                                       t["b2eb"][0:1, 512 * cb: 512 * (cb + 1)], start=False, stop=True)
                    b.inc(mm, "pe2", 1)

        @blk.scalar
        def _(sc):
            for u in range(32):
                b.waitv(sc, "dve2", dv0 + 36 * u + 20)
                b.inc(sc.activation(t["gsc"][:, 44: 48], t["gsc"][:, 40: 44], AF.Exp), "act", 1)

        @blk.vector
        def _(v):
            for u, (kp, cp) in enumerate(order):
                for j2 in range(4):
                    gid = 4 * u + j2
                    b.waitv(v, "pe3", pe3_0 + gid + 1)
                    pl = glg[j2 % 2]
                    g8 = t["gsc"][:, 0:8]
                    b.fence(v, v.tensor_copy(g8, pl[:, 0:E]))
                    b.inc(v.tensor_copy(t["gsc"][:, 8:9], pl[:, 0:1]), "dve2", 1)
                    b.vred(v, "reduce_max", t["gsc"][:, 16 + j2: 17 + j2], g8, AX.X)
                    b.fence(v, v.tensor_scalar(t["gmk"][:, 8 * j2: 8 * j2 + 8], g8,
                                               t["gsc"][:, 16 + j2: 17 + j2], None, ALU.is_equal))
                    b.fence(v, v.scalar_tensor_tensor(t["gsc"][:, 24: 32], t["gmk"][:, 8 * j2: 8 * j2 + 8],
                                                      -1e30, g8, ALU.mult, ALU.add))
                    b.vred(v, "reduce_max", t["gsc"][:, 20 + j2: 21 + j2], t["gsc"][:, 24:32], AX.X)
                    b.fence(v, v.tensor_scalar(t["gmk"][:, 32 + 8 * j2: 40 + 8 * j2], t["gsc"][:, 24:32],
                                               t["gsc"][:, 20 + j2: 21 + j2], None, ALU.is_equal))
                    b.inc(v.tensor_tensor(t["gsc"][:, 40 + j2: 41 + j2], t["gsc"][:, 20 + j2: 21 + j2],
                                          t["gsc"][:, 16 + j2: 17 + j2], ALU.subtract), "dve2", 4)
                b.waitv(v, "act", ac0 + u + 1)
                for j2 in range(4):
                    gid = 4 * u + j2
                    b.fence(v, v.tensor_scalar(t["gsc"][:, 48 + j2: 49 + j2], t["gsc"][:, 44 + j2: 45 + j2],
                                               1.0, None, ALU.add))
                    b.fence(v, v.reciprocal(t["gsc"][:, 52 + j2: 53 + j2], t["gsc"][:, 48 + j2: 49 + j2]))
                    b.fence(v, v.tensor_tensor(t["gsc"][:, 56 + j2: 57 + j2], t["gsc"][:, 44 + j2: 45 + j2],
                                               t["gsc"][:, 52 + j2: 53 + j2], ALU.mult))
                    v.tensor_scalar(t["gsc"][:, 0:8], t["gmk"][:, 8 * j2: 8 * j2 + 8],
                                    t["gsc"][:, 52 + j2: 53 + j2], None, ALU.mult)
                    b.fence(v, v.tensor_scalar(t["gsc"][:, 8:16], t["gmk"][:, 32 + 8 * j2: 40 + 8 * j2],
                                               t["gsc"][:, 56 + j2: 57 + j2], None, ALU.mult))
                    b.fence(v, v.tensor_tensor(t["gsc"][:, 0:8], t["gsc"][:, 0:8], t["gsc"][:, 8:16], ALU.add))
                    b.fence(v, v.tensor_tensor(t["gsc"][:, 0:8], t["gsc"][:, 0:8], t["eselS"][:, :], ALU.mult))
                    b.vred(v, "reduce_sum", t["ewtt"][:, j2: j2 + 1], t["gsc"][:, 0:8], AX.X)
                    b.inc(v.tensor_copy(t["ewtt"][:, 4 + j2: 5 + j2], t["ewtt"][:, j2: j2 + 1]), "dve2", 4)
                for m in range(32):
                    b.waitv(v, "pe", pe0 + 32 * u + m + 1)
                    if m == 0 and u >= 1:
                        b.waitv(v, "pe2", pe2_0 + 4 * u)
                    ph = gph[m % 2]
                    b.inc(v.tensor_scalar(t["HID"][:, 512 * m: 512 * (m + 1)], ph[:, :],
                                          t["b1effc"][:, m: m + 1], 0.0, ALU.add, ALU.max), "dve3", 1)
                for j2 in range(4):
                    b.waitv(v, "pe2", pe2_0 + 4 * u + j2 + 1)
                    if u >= 1 and j2 == 0:
                        b.waitv(v, "ldc", ldc0 + u * 16)
                    for cb in range(2):
                        i6 = v.tensor_scalar(t["MOEST"][:, 1024 * j2 + 512 * cb: 1024 * j2 + 512 * (cb + 1)],
                                             gpm[cb][:, :], t["ewtt"][:, j2: j2 + 1], None, ALU.mult)
                    b.inc(i6, "dve4", 1)

    @blk.gpsimd
    def _(g):
        for kp in range(4):
            b.waitv(g, "ldc", ldc0 + (8 * kp + 8) * 16)
            b.inc(g.collective_compute("ReduceScatter", ALU.add, replica_groups=GROUPS8,
                                       ins=[t["rs_in"][4096 * kp: 4096 * (kp + 1), :].opt()],
                                       outs=[t["rs_out"][512 * kp: 512 * (kp + 1), :].opt()]), "cc", 1)


def phase_H(b, blk):
    nc, t = b.nc, b.t
    ccH = b.cnt["cc"]
    dv0 = b.cnt["dve"]
    dv3_0 = b.cnt["dve3"]
    ac0 = b.cnt["act2"]
    lda0, ldb0 = b.cnt["lda"], b.cnt["ldb"]
    gp0, gp20 = b.cnt["gp"], b.cnt["gp2"]
    ldv0 = b.cnt["ldv"]
    st0 = b.cnt["st"]

    @blk.sync
    def _(s):
        b.waitv(s, "cc", ccH)
        for j in range(NJ):
            sem = "lda" if j % 2 == 0 else "ldb"
            if j >= 2:
                b.waitv(s, "dve", dv0 + 2 * (j - 2) + 1)
            b.inc(s.dma_start(out=(t["xc"] if j % 2 == 0 else t["msc"])[:, :],
                              in_=t["u2_pk"][:, 1024 * j: 1024 * (j + 1)]), sem, 16)
            if j >= 1:
                b.waitv(s, "dve", dv0 + 2 * (j - 1) + 1)
                if j >= 2:
                    b.waitv(s, "st", st0 + 16 * (j - 1))
                b.inc(s.dma_start(out=t["moef"][:, :], in_=t["x1_pk"][:, 1024 * (j - 1): 1024 * j]), "ldv", 16)
                b.waitv(s, "dve", dv0 + 2 * (j - 1) + 2)
                b.inc(s.dma_start(out=t["y"][128 * (j - 1): 128 * j, :], in_=t["moef"][:, :]), "st", 16)
        b.waitv(s, "dve", dv0 + 2 * (NJ - 1) + 1)
        b.waitv(s, "st", st0 + 16 * (NJ - 1))
        b.inc(s.dma_start(out=t["moef"][:, :], in_=t["x1_pk"][:, 1024 * (NJ - 1): 1024 * NJ]), "ldv", 16)
        b.waitv(s, "dve", dv0 + 2 * NJ)
        b.inc(s.dma_start(out=t["y"][128 * (NJ - 1): 128 * NJ, :], in_=t["moef"][:, :]), "st", 16)

    @blk.gpsimd
    def _(g):
        b.waitv(g, "cc", ccH)
        for j in range(NJ):
            if j >= 2:
                b.waitv(g, "dve", dv0 + 2 * (j - 2) + 1)
            b.inc(g.dma_start(out=(t["hsq"] if j % 2 == 0 else t["dsq"])[:, :],
                              in_=t["rs_out"][128 * j: 128 * (j + 1), :]),
                  "gp" if j % 2 == 0 else "gp2", 16)

    @blk.vector
    def _(v):
        for j in range(NJ):
            xb = t["xc"] if j % 2 == 0 else t["msc"]
            mb = t["hsq"] if j % 2 == 0 else t["dsq"]
            sem = "lda" if j % 2 == 0 else "ldb"
            b.waitv(v, sem, (lda0 if j % 2 == 0 else ldb0) + (j // 2 + 1) * 16)
            sem2 = "gp" if j % 2 == 0 else "gp2"
            b.waitv(v, sem2, (gp0 if j % 2 == 0 else gp20) + (j // 2 + 1) * 16)
            b.inc(v.tensor_tensor(xb[:, :], xb[:, :], mb[:, :], ALU.add), "dve", 1)
            st_ = t["stat"]
            b.vred(v, "reduce_sum", st_[:, j: j + 1], xb[:, :], AX.X)
            b.fence(v, v.tensor_scalar(st_[:, 16 + j: 17 + j], st_[:, j: j + 1], float(INV_D), None, ALU.mult))
            b.fence(v, v.tensor_scalar(t["sq"][:, :], xb[:, :], st_[:, 16 + j: 17 + j], None, ALU.subtract))
            b.fence(v, v.tensor_tensor(t["csq"][:, :], t["sq"][:, :], t["sq"][:, :], ALU.mult))
            b.vred(v, "reduce_sum", st_[:, j: j + 1], t["csq"][:, :], AX.X)
            b.inc(v.tensor_scalar(st_[:, j: j + 1], st_[:, j: j + 1], float(INV_D), float(EPS),
                                  ALU.mult, ALU.add), "dve3", 1)
            b.waitv(v, "act2", ac0 + j + 1)
            b.fence(v, v.tensor_scalar(t["sq"][:, :], t["sq"][:, :], t["rstdb"][:, j: j + 1], None, ALU.mult))
            b.waitv(v, "ldv", ldv0 + 16 * (j + 1))
            b.inc(v.tensor_tensor(t["moef"][:, :], t["moef"][:, :], t["sq"][:, :], ALU.add), "dve", 1)

    @blk.scalar
    def _(sc):
        for j in range(NJ):
            b.waitv(sc, "dve3", dv3_0 + j + 1)
            sc.activation(t["stat"][:, 16 + j: 17 + j], t["stat"][:, j: j + 1], AF.Ln)
            b.inc(sc.activation(t["rstdb"][:, j: j + 1], t["stat"][:, 16 + j: 17 + j], AF.Exp,
                                scale=np.float32(-0.5)), "act2", 1)


def emit_av_e(b, tt, u, g, EOP, PBe, ac0):
    t = b.t
    gg = 64 * u + g
    b.waitv(tt, "act", ac0 + gg + 1)
    r2, tile = g // 16, g % 16
    col = 2048 * r2 + 128 * tile
    tt.matmul(EOP[0][:, :], t["VHA"][:, col: col + 128],
              PBe[gg % 2][:, 0:512], start=(g == 0), stop=(g == 63))
    mm = tt.matmul(EOP[1][:, :], t["VHB"][:, col: col + 128],
                   PBe[gg % 2][:, 512:1024], start=(g == 0), stop=(g == 63))
    b.inc(mm, "pe2", 1)


def phase_E_end(b, blk):
    nc, t = b.nc, b.t
    dv3_all = b.cnt["dve3"]

    @blk.gpsimd
    def _(g):
        b.waitv(g, "dve3", dv3_all)
        b.inc(g.dma_start(out=t["WO2"].rearrange("p (k f) -> p k f", f=1024),
                          in_=t["wo2"].rearrange("(k p) f -> p k f", p=128)), "ldg", 16)
        b.inc(g.dma_start(out=t["bob"][0:1, :], in_=t["bo"]), "ldg", 16)
    ldg_e = b.cnt["ldg"]

    pe0 = b.cnt["pe"]
    pe2_0 = b.cnt["pe2"]
    pe3_0 = b.cnt["pe3"]
    ac0 = b.cnt["act2"]
    dv0 = b.cnt["dve"]
    lda0, ldb0 = b.cnt["lda"], b.cnt["ldb"]
    ldc0 = b.cnt["ldc"]
    ld30 = b.cnt["ld3"]

    with nc.psum_tensor("etr2", [128, 1024], BF16) as etr2, \
         nc.psum_tensor("exo0", [128, 512], F32) as exo0, \
         nc.psum_tensor("exo1", [128, 512], F32) as exo1, \
         nc.psum_tensor("etr3", [128, 1024], BF16) as etr3:
        exo = [exo0, exo1]

        @blk.sync
        def _(s):
            b.waitv(s, "dve3", dv3_all)
            for j in range(NJ):
                sem = "lda" if j % 2 == 0 else "ldb"
                if j >= 2:
                    b.waitv(s, "dve", dv0 + 4 * (j - 2) + 2)
                b.inc(s.dma_start(out=(t["xc"] if j % 2 == 0 else t["msc"])[:, :],
                                  in_=t["u_pk"][:, 1024 * j: 1024 * (j + 1)]), sem, 16)
                if j >= 1:
                    b.waitv(s, "dve", dv0 + 4 * (j - 1) + 3)
                    b.inc(s.dma_start(out=t["u2_pk"][:, 1024 * (j - 1): 1024 * j],
                                      in_=(t["xc"] if (j - 1) % 2 == 0 else t["msc"])[:, :]), "ldc", 16)
            b.waitv(s, "dve", dv0 + 4 * (NJ - 1) + 3)
            b.inc(s.dma_start(out=t["u2_pk"][:, 1024 * (NJ - 1): 1024 * NJ],
                              in_=(t["xc"] if (NJ - 1) % 2 == 0 else t["msc"])[:, :]), "ldc", 16)
            b.waitv(s, "dve", dv0 + 4 * NJ)
            for k in range(8):
                b.inc(s.dma_start(out=t["h2t_in"][128 * k: 128 * (k + 1), :],
                                  in_=t["H2TS"][:, 2048 * k: 2048 * (k + 1)]), "ld3", 16)

        @blk.tensor
        def _(tt):
            b.waitv(tt, "ldg", ldg_e)
            for j in range(NJ):
                if j >= 1:
                    b.waitv(tt, "dve", dv0 + 4 * (j - 1) + 1)
                for k in range(8):
                    mm = tt.transpose(etr2[:, 128 * k: 128 * (k + 1)],
                                      t["O2N"][:, 1024 * j + 128 * k: 1024 * j + 128 * (k + 1)],
                                      t["identb"][:, :])
                b.inc(mm, "pe", 1)
                b.waitv(tt, "dve", dv0 + 4 * j + 1)
                if j >= 1:
                    b.waitv(tt, "dve", dv0 + 4 * (j - 1) + 2)
                for cb in range(2):
                    pm = exo[cb]
                    for k in range(8):
                        tt.matmul(pm[:, :], t["o2lh"][:, 128 * k: 128 * (k + 1)],
                                  t["WO2"][:, 1024 * k + 512 * cb: 1024 * k + 512 * (cb + 1)],
                                  start=(k == 0), stop=False)
                    mm = tt.matmul(pm[:, :], t["ones_r"][0:1, 0:128], t["bob"][0:1, 512 * cb: 512 * (cb + 1)],
                                   start=False, stop=True)
                b.inc(mm, "pe2", 1)
                b.waitv(tt, "dve", dv0 + 4 * j + 3)
                for k in range(8):
                    mm = tt.transpose(etr3[:, 128 * k: 128 * (k + 1)], t["n2e"][:, 128 * k: 128 * (k + 1)],
                                      t["identb"][:, :])
                b.inc(mm, "pe3", 1)

        @blk.vector
        def _(v):
            for j in range(NJ):
                xb = t["xc"] if j % 2 == 0 else t["msc"]
                sem = "lda" if j % 2 == 0 else "ldb"
                b.waitv(v, sem, (lda0 if j % 2 == 0 else ldb0) + (j // 2 + 1) * 16)
                b.waitv(v, "pe", pe0 + j + 1)
                if j >= 1:
                    b.waitv(v, "pe2", pe2_0 + j)
                b.inc(v.tensor_copy(t["o2lh"][:, :], etr2[:, :]), "dve", 1)
                b.waitv(v, "pe2", pe2_0 + j + 1)
                v.tensor_tensor(xb[:, 0:512], xb[:, 0:512], exo[0][:, :], ALU.add)
                b.fence(v, v.tensor_tensor(xb[:, 512:1024], xb[:, 512:1024], exo[1][:, :], ALU.add))
                b.inc(v.tensor_copy(t["gsc"][:, 1:2], xb[:, 1023:1024]), "dve", 1)
                st_ = t["stat"]
                b.vred(v, "reduce_sum", st_[:, 4:5], xb[:, :], AX.X)
                b.fence(v, v.tensor_scalar(st_[:, 5:6], st_[:, 4:5], float(INV_D), None, ALU.mult))
                b.fence(v, v.tensor_scalar(t["sq"][:, :], xb[:, :], st_[:, 5:6], None, ALU.subtract))
                b.fence(v, v.tensor_tensor(t["esq"][:, :], t["sq"][:, :], t["sq"][:, :], ALU.mult))
                b.vred(v, "reduce_sum", st_[:, 4:5], t["esq"][:, :], AX.X)
                b.inc(v.tensor_scalar(st_[:, 4:5], st_[:, 4:5], float(INV_D), float(EPS),
                                      ALU.mult, ALU.add), "dve3", 1)
                b.waitv(v, "act2", ac0 + j + 1)
                if j >= 1:
                    b.waitv(v, "pe3", pe3_0 + j)
                b.inc(v.tensor_scalar(t["n2e"][:, :], t["sq"][:, :], t["rstdb"][:, 2:3], None, ALU.mult),
                      "dve", 1)
                b.waitv(v, "pe3", pe3_0 + j + 1)
                for k in range(8):
                    i5 = v.tensor_copy(t["H2TS"][:, 2048 * k + 128 * j: 2048 * k + 128 * (j + 1)],
                                       etr3[:, 128 * k: 128 * (k + 1)])
                b.inc(i5, "dve", 1)

        @blk.scalar
        def _(sc):
            dv3e = b.cnt["dve3"] - NJ
            for j in range(NJ):
                b.waitv(sc, "dve3", dv3e + j + 1)
                sc.activation(t["stat"][:, 5:6], t["stat"][:, 4:5], AF.Ln)
                b.inc(sc.activation(t["rstdb"][:, 2:3], t["stat"][:, 5:6], AF.Exp, scale=np.float32(-0.5)),
                      "act2", 1)

    @blk.gpsimd
    def _(g):
        b.wait(g, "ld3")
        b.inc(g.collective_compute("AllGather", ALU.bypass, replica_groups=GROUPS8,
                                   ins=[t["h2t_in"].opt()], outs=[t["h2t_ag"].opt()]), "cc", 1)


def emit_av(b, tt, units, u, g, bop, PB, ac0, dv0, gg_of_unit, mask_cnt_before):
    t = b.t
    bt, i, qb, nch, groups = units[u]
    base = gg_of_unit[u]
    gg = base + g
    grp = groups[g]
    b.waitv(tt, "act", ac0 + gg + 1)
    nmask_before_and_in = mask_cnt_before[u] + sum(
        1 for gp in range(g + 1) for n in groups[gp]
        if (n - 4 * qb) >= 0 and (n - 4 * qb) < 4)
    if nmask_before_and_in > 0:
        b.waitv(tt, "dve", b.cnt_dv0_B + nmask_before_and_in)
    for k, n in enumerate(grp):
        tile = 64 * bt + n
        mm = tt.matmul(bop[:, :], t["VP"][:, 130 * tile + 65 * i: 130 * tile + 65 * (i + 1)],
                       PB[gg % 2][:, 512 * k: 512 * (k + 1)],
                       start=(grp[0] == 0 and k == 0), stop=(n == nch - 1))
    b.inc(mm, "pe2", 1)


def kernel(**inputs):
    in_maps = make_in_maps(inputs)
    nc, _ = build_nc(debug=False)
    res = run_bass_kernel_spmd(nc, in_maps, core_ids=list(range(8)))
    return np.concatenate([res.results[cid]["y"] for cid in range(8)], 0).reshape(B, S, D)


# version 14 (final)
# speedup vs baseline: 1.0457x; 1.0112x over previous
import contextlib
import numpy as np
import ml_dtypes
import concourse.bass as bass
import concourse.mybir as mybir
from concourse.bass_utils import run_bass_kernel_spmd

F32 = mybir.dt.float32
BF16 = mybir.dt.bfloat16
AF = mybir.ActivationFunctionType
ALU = mybir.AluOpType
AX = mybir.AxisListType

B, S, D, H, DK, E, DFF = 2, 8192, 1024, 16, 64, 8, 4096
T = B * S
TLOC = T // 8
NJ = TLOC // 128
C0 = np.float32(-12.0)
INV8 = np.float32(0.125)
EPS = np.float32(1e-5)
INV_D = np.float32(1.0 / D)

GROUPS4 = [[0, 1, 2, 3], [4, 5, 6, 7]]
GROUPS8 = [list(range(8))]


def make_in_maps(inputs):
    x = np.asarray(inputs["L0_attn_x"], np.float32).reshape(T, D)
    Wqkv1 = np.asarray(inputs["L0_attn_W_qkv"], np.float32)
    Wo1 = np.asarray(inputs["L0_attn_W_o"], np.float32)
    perm = np.concatenate([np.arange(0, 64, 2), np.arange(1, 64, 2)])

    invf = (10000.0 ** (-np.arange(0, DK, 2, dtype=np.float32) / DK)).astype(np.float32)
    ang = np.arange(S, dtype=np.float32)[:, None] * invf[None, :]
    ropeC = np.concatenate([np.cos(ang), np.cos(ang)], 1).astype(np.float32)
    ropeS = np.concatenate([np.sin(ang), np.sin(ang)], 1).astype(np.float32)
    ropeC = np.ascontiguousarray(ropeC.reshape(64, 128, 64).transpose(1, 0, 2).reshape(128, 4096))
    ropeS = np.ascontiguousarray(ropeS.reshape(64, 128, 64).transpose(1, 0, 2).reshape(128, 4096))

    kvv = np.arange(128)[:, None]
    qv = np.arange(512)[None, :]
    maskc = np.concatenate([(128 * j + kvv <= qv) for j in range(4)], 1).astype(ml_dtypes.bfloat16)
    maskc = np.ascontiguousarray(maskc)

    common = {
        "ropeC": ropeC, "ropeS": ropeS, "maskc": maskc,

        "wqkv2": np.ascontiguousarray(np.asarray(inputs["L0_ffn_w_qkv"], np.float32)
                                      .reshape(8, 128, 3 * D).transpose(1, 0, 2).reshape(128, 24 * D)),
        "bqkv": np.asarray(inputs["L0_ffn_b_qkv"], np.float32).reshape(1, 3 * D),
        "wo2": np.ascontiguousarray(np.asarray(inputs["L0_ffn_w_o"], np.float32)),
        "bo": np.asarray(inputs["L0_ffn_b_o"], np.float32).reshape(1, D),
        "g1": np.asarray(inputs["L0_ffn_ln1_g"], np.float32).reshape(1, D),
        "b1ln": np.asarray(inputs["L0_ffn_ln1_b"], np.float32).reshape(1, D),
        "g2": np.asarray(inputs["L0_ffn_ln2_g"], np.float32).reshape(1, D),
        "b2ln": np.asarray(inputs["L0_ffn_ln2_b"], np.float32).reshape(1, D),
        "gatew": np.ascontiguousarray(np.asarray(inputs["L0_ffn_gate_w"], np.float32)
                                      .reshape(8, 128, E).transpose(1, 0, 2).reshape(128, 8 * E)),
        "gateb": np.asarray(inputs["L0_ffn_gate_b"], np.float32).reshape(1, E),
    }
    W1 = np.asarray(inputs["L0_ffn_W1"], np.float32)
    b1 = np.asarray(inputs["L0_ffn_b1"], np.float32)
    W2 = np.asarray(inputs["L0_ffn_W2"], np.float32)
    b2 = np.asarray(inputs["L0_ffn_b2"], np.float32)

    in_maps = []
    for c in range(8):
        h0 = 2 * c
        rows = []
        for blk in range(2):
            for h in (h0, h0 + 1):
                rows.append(blk * D + h * DK + perm)
        for h in (h0, h0 + 1):
            rows.append(2 * D + h * DK + np.arange(DK))
        rows = np.concatenate(rows)
        m = dict(common)
        xm = x[TLOC * c: TLOC * (c + 1)].reshape(NJ, 128, D).transpose(1, 0, 2)
        m["x_my"] = np.ascontiguousarray(xm.reshape(128, NJ * D))
        m["wo1c"] = np.ascontiguousarray(Wo1.T[128 * c: 128 * (c + 1)])
        wt = Wqkv1[rows].T.reshape(8, 128, 384).transpose(1, 0, 2)
        m["wqkv1T"] = np.ascontiguousarray(wt.reshape(128, 8 * 384))
        m["w1e"] = np.ascontiguousarray(W1[c].reshape(8, 128, DFF).transpose(1, 0, 2).reshape(128, 8 * DFF))
        m["b1e"] = b1[c].reshape(1, DFF)
        m["w2e"] = np.ascontiguousarray(W2[c].reshape(32, 128, D).transpose(1, 0, 2).reshape(128, 32 * D))
        m["b2e"] = b2[c].reshape(1, D)
        sel = np.zeros((128, E), np.float32)
        sel[:, c] = 1.0
        m["esel"] = sel
        in_maps.append(m)
    return in_maps


class Bld:
    def __init__(self, nc, debug):
        self.nc = nc
        self.debug = debug
        self.sem = {}
        self.cnt = {}
        self.t = {}
        self.stack = contextlib.ExitStack()
        self.dbg = {}

    def mksem(self, *names):
        for n in names:
            self.sem[n] = self.stack.enter_context(self.nc.semaphore(n))
            self.cnt[n] = 0

    def inc(self, inst, name, n=1):
        inst.then_inc(self.sem[name], n)
        self.cnt[name] += n
        return inst

    def wait(self, eng, name, val=None):
        eng.wait_ge(self.sem[name], self.cnt[name] if val is None else val)

    def waitv(self, eng, name, val):
        eng.wait_ge(self.sem[name], val)

    def fence(self, v, inst):
        self.inc(inst, "dvs", 1)
        self.wait(v, "dvs")
        return inst

    def vred(self, v, op, out, in_, axis):
        inst = getattr(v, op)(out, in_, axis)
        self.inc(inst, "dvs", 1)
        self.wait(v, "dvs")
        return inst


def build_nc(debug=False):
    nc = bass.Bass("TRN2", target_bir_lowering=False, detect_race_conditions=False)
    b = Bld(nc, debug)
    t = b.t

    def din(name, shape, dt=F32):
        t[name] = nc.dram_tensor(name, list(shape), dt, kind="ExternalInput").ap()

    def dint(name, shape, dt=BF16):
        t[name] = nc.dram_tensor(name, list(shape), dt, kind="Internal").ap()

    for name, shape, dt in [
        ("x_my", (128, NJ * D), F32), ("wqkv1T", (128, 8 * 384), F32),
        ("ropeC", (128, 4096), F32), ("ropeS", (128, 4096), F32), ("maskc", (128, 2048), BF16),
        ("wo1c", (128, D), F32), ("wqkv2", (128, 24 * D), F32), ("bqkv", (1, 3 * D), F32),
        ("wo2", (D, D), F32), ("bo", (1, D), F32),
        ("g1", (1, D), F32), ("b1ln", (1, D), F32), ("g2", (1, D), F32), ("b2ln", (1, D), F32),
        ("gatew", (128, 8 * E), F32), ("esel", (128, E), F32), ("gateb", (1, E), F32),
        ("w1e", (128, 8 * DFF), F32), ("b1e", (1, DFF), F32), ("w2e", (128, 32 * D), F32), ("b2e", (1, D), F32),
    ]:
        din(name, shape, dt)

    t["y"] = nc.dram_tensor("y", [TLOC, D], F32, kind="ExternalOutput").ap()

    for name, shape, dt in [
        ("n1t_in", (D, TLOC), BF16), ("n1t_ag", (8 * D, TLOC), BF16),
        ("og_in", (T, 128), BF16),
        ("xo_rs_in", (T, D), BF16), ("xo_rs_out", (TLOC, D), BF16),
        ("k2t_in", (D, TLOC), BF16), ("k2t_ag", (4 * D, TLOC), BF16),
        ("v2h_in", (H * TLOC, 128), BF16), ("v2h_ag", (4 * H * TLOC, 128), BF16),
        ("h2t_in", (D, TLOC), BF16), ("h2t_ag", (8 * D, TLOC), BF16),
        ("rs_in", (T, D), BF16), ("rs_out", (TLOC, D), BF16),
        ("x1_pk", (128, NJ * D), F32), ("u_pk", (128, NJ * D), F32), ("u2_pk", (128, NJ * D), F32),
    ]:
        dint(name, shape, dt)

    t["identf_d"] = nc.inline_tensor(np.eye(128, dtype=np.float32), "identf").ap()
    t["identb_d"] = nc.inline_tensor(np.eye(128, dtype=ml_dtypes.bfloat16), "identb").ap()

    if debug:
        def dout(name, shape, dt):
            b.dbg[name] = nc.dram_tensor("dbg_" + name, list(shape), dt, kind="ExternalOutput").ap()
        dout("og", (T, 128), BF16)
        dout("x1", (128, NJ * D), F32)
        dout("xo", (TLOC, D), BF16)
        dout("u", (128, NJ * D), F32)
        dout("u2", (128, NJ * D), F32)
        dout("h2t", (8 * D, TLOC), BF16)
        dout("rso", (TLOC, D), BF16)
        dout("ews", (128, 8), F32)
        dout("rsin", (T, D), BF16)
        dout("h2tin", (D, TLOC), BF16)
        dout("k2t", (4 * D, TLOC), BF16)
        dout("v2h", (4 * H * TLOC, 128), BF16)


    MB = 22496
    R1o, SCRo, CSTo, BIGo = MB, MB + 65536, MB + 77824, MB + 90112
    BIG_END = BIGo + 116736
    assert BIG_END <= 229344

    R1B = nc.alloc_sbuf_tensor_at("R1B", [128, 32768], BF16, offset=R1o).ap()
    SCRB = nc.alloc_sbuf_tensor_at("SCRB", [128, 6144], BF16, offset=SCRo).ap()
    CSTB = nc.alloc_sbuf_tensor_at("CSTB", [128, 6144], BF16, offset=CSTo).ap()
    BIGB = nc.alloc_sbuf_tensor_at("BIGB", [128, 58368], BF16, offset=BIGo).ap()

    def sb(name, shape, dt, off):
        for base, bo, bcols in ((R1B, R1o, 32768), (SCRB, SCRo, 6144), (CSTB, CSTo, 6144),
                                (BIGB, BIGo, 58368)):
            rel = off - bo
            if 0 <= rel < bcols * 2:
                break
        else:
            raise AssertionError(name)
        nb = int(np.prod(shape[1:])) * (4 if dt == F32 else 2)
        assert rel % 2 == 0 and rel + nb <= bcols * 2, (name, off)
        v = base[:, rel // 2: (rel + nb) // 2]
        if dt == F32:
            v = v.bitcast(F32)
        if shape[0] != 128:
            v = v[0: shape[0]]
        if len(shape) != 2:
            raise AssertionError(name)
        assert v.shape == tuple(shape), (name, v.shape, shape)
        t[name] = v

    sb("r1f", (128, 16384), F32, R1o)
    sb("ropeCt", (128, 4096), F32, R1o)
    sb("ropeSt", (128, 4096), F32, R1o + 16384)
    sb("slab0", (128, 8192), BF16, R1o + 32768)
    sb("slab1", (128, 8192), BF16, R1o + 49152)
    sb("Q2T", (128, 16384), BF16, R1o)
    sb("K2TS", (128, 16384), BF16, R1o + 32768)
    sb("W1G", (128, 32768), BF16, R1o)

    sb("xc", (128, 1024), F32, SCRo)
    sb("sq", (128, 1024), F32, SCRo + 4096)
    sb("msc", (128, 1024), F32, SCRo + 8192)
    sb("bfs", (128, 2048), BF16, SCRo + 4096)
    sb("mscb", (128, 2048), BF16, SCRo + 8192)
    sb("rowf0", (1, 2048), F32, SCRo)

    o = CSTo
    def cst(name, shape, dt, size):
        nonlocal o
        sb(name, shape, dt, o)
        o += size
    cst("identf", (128, 128), F32, 512)
    cst("identb", (128, 128), BF16, 256)
    cst("ones_r", (1, 128), BF16, 256)
    cst("vecc", (128, 32), F32, 128)
    cst("rstdb", (128, 16), F32, 64)
    cst("stat", (128, 32), F32, 128)
    cst("gwg", (128, 64), BF16, 128)
    cst("b1effc", (128, 32), F32, 128)
    cst("ewtt", (128, 32), F32, 128)
    cst("gsc", (128, 64), F32, 256)
    cst("gmk", (128, 64), BF16, 128)
    cst("lrec", (128, 8), F32, 32)
    cst("eselS", (128, 8), F32, 32)
    cst("beffq", (1, 3 * D), BF16, 6144)
    cst("beffg", (1, E), BF16, 64)
    assert o <= BIGo, o

    sb("xmy", (128, 16384), F32, BIGo)
    sb("n1tS", (128, 16384), BF16, BIGo + 65536)
    sb("QT", (128, 16384), BF16, BIGo)
    sb("KT", (128, 16384), BF16, BIGo + 32768)
    sb("VP", (128, 16640), BF16, BIGo + 65536)
    sb("WQ1", (128, 3072), BF16, BIGo + 98816)
    sb("PB0", (128, 1536), BF16, BIGo + 104960)
    sb("PB1", (128, 1536), BF16, BIGo + 108032)
    sb("obst", (128, 512), BF16, BIGo + 111104)
    sb("masks", (128, 2048), BF16, BIGo + 112128)

    sb("WO1", (128, 1024), BF16, BIGo)
    sb("oglh", (128, 128), BF16, BIGo + 2048)
    sb("ogld0", (128, 128), BF16, BIGo + 2304)
    sb("ogld1", (128, 128), BF16, BIGo + 2560)
    sb("cstg0", (128, 4096), BF16, BIGo + 4096)
    sb("cstg1", (128, 4096), BF16, BIGo + 12288)
    sb("csq", (128, 1024), F32, BIGo + 20480)

    sb("WQ2", (128, 24576), BF16, BIGo)
    sb("V2ST", (128, 2048), BF16, BIGo + 49152)
    sb("brow", (1, 4096), BF16, BIGo + 53248)
    sb("dsq", (128, 1024), F32, BIGo + 61440)
    sb("n1ib", (128, 1024), BF16, BIGo + 65536)
    sb("n1it", (128, 1024), BF16, BIGo + 67584)
    sb("qsb", (128, 512), BF16, BIGo + 69632)
    sb("csq2", (128, 1024), F32, BIGo + 70656)
    sb("b1c_bf", (128, 8), BF16, BIGo + 74752)

    sb("KB", (128, 8192), BF16, BIGo)
    sb("VHA", (128, 8192), BF16, BIGo + 16384)
    sb("VHB", (128, 8192), BF16, BIGo + 32768)
    sb("PB0e", (128, 1536), BF16, BIGo + 49152)
    sb("PB1e", (128, 1536), BF16, BIGo + 52224)
    sb("O2N", (128, 16384), BF16, BIGo + 57344)
    sb("WO2", (128, 8192), BF16, BIGo)
    sb("H2TS", (128, 16384), BF16, BIGo + 16384)
    sb("esq", (128, 1024), F32, BIGo + 49152)
    sb("o2lh", (128, 1024), BF16, BIGo + 53248)
    sb("bob", (1, D), BF16, BIGo + 55296)
    sb("n2e", (128, 1024), BF16, BIGo + 90112)

    sb("W2S", (128, 32768), BF16, BIGo)
    sb("HID", (128, 16384), BF16, BIGo + 65536)
    sb("H2TB", (128, 4096), BF16, BIGo + 98304)
    sb("MOEST", (128, 4096), BF16, BIGo + 106496)
    sb("b2eb", (1, D), BF16, BIGo + 114688)
    sb("b1effr", (1, DFF), F32, BIGo + 65536)
    sb("growb", (1, DFF), BF16, BIGo + 81920)
    sb("gbrow", (1, 64), BF16, BIGo + 92160)
    sb("b2c_bf", (128, 8), BF16, BIGo + 92288)
    sb("hsq", (128, 1024), F32, BIGo + 98304)
    sb("moef", (128, 1024), F32, BIGo + 102400)

    b.mksem("ld", "ldg", "ld2", "ld3", "ldv", "ldx", "lda", "ldb", "ldc", "cc", "dvs", "gp2", "dbg",
            "pe", "pe2", "pe3",
            "act", "act2",
            "dve", "dve2", "dve3", "dve4",
            "gp", "st")

    with b.stack, nc.Block() as blk:
        phase_P0(b, blk)
        phase_A0(b, blk)
        phase_A(b, blk)
        phase_B(b, blk)
        phase_C(b, blk)
        phase_D(b, blk)
        phase_E(b, blk)
        phase_G(b, blk)
        phase_H(b, blk)
        if debug:
            @blk.gpsimd
            def _(g):
                b.wait(g, "ldx")
                b.inc(g.dma_start(out=b.dbg["og"], in_=t["og_in"]), "dbg", 16)
                b.inc(g.dma_start(out=b.dbg["x1"], in_=t["x1_pk"]), "dbg", 16)
                b.inc(g.dma_start(out=b.dbg["xo"], in_=t["xo_rs_out"]), "dbg", 16)
                b.wait(g, "cc")
                b.wait(g, "ldc")
                b.inc(g.dma_start(out=b.dbg["u"], in_=t["u_pk"]), "dbg", 16)
                b.inc(g.dma_start(out=b.dbg["k2t"], in_=t["k2t_ag"]), "dbg", 16)
                b.inc(g.dma_start(out=b.dbg["v2h"], in_=t["v2h_ag"]), "dbg", 16)
                b.inc(g.dma_start(out=b.dbg["u2"], in_=t["u2_pk"]), "dbg", 16)
                b.inc(g.dma_start(out=b.dbg["h2t"], in_=t["h2t_ag"]), "dbg", 16)
                b.inc(g.dma_start(out=b.dbg["h2tin"], in_=t["h2t_in"]), "dbg", 16)
                b.inc(g.dma_start(out=b.dbg["rso"], in_=t["rs_out"]), "dbg", 16)
                b.inc(g.dma_start(out=b.dbg["ews"], in_=t["ewtt"][:, 0:8]), "dbg", 16)
                b.inc(g.dma_start(out=b.dbg["rsin"], in_=t["rs_in"]), "dbg", 16)
        final_wait(b, blk)
    return nc, b


def dump_debug(b, blk, items, wait):
    @blk.gpsimd
    def _(g):
        for s, v in wait:
            b.wait(g, s, v)
        for name, ap in items:
            b.inc(g.dma_start(out=b.dbg[name], in_=ap[:, :]), "dbg", 16)


def final_wait(b, blk):
    @blk.gpsimd
    def _(g):
        b.wait(g, "st")
        b.wait(g, "dbg")


def phase_P0(b, blk):
    nc, t = b.nc, b.t

    @blk.sync
    def _(s):
        b.inc(s.dma_start(out=t["identf"][:, :], in_=t["identf_d"]), "ld", 16)
        b.inc(s.dma_start(out=t["identb"][:, :], in_=t["identb_d"]), "ld", 16)
        b.inc(s.dma_start(out=t["masks"][:, :], in_=t["maskc"]), "ld", 16)
        b.inc(s.dma_start(out=t["ropeCt"][:, :], in_=t["ropeC"]), "ld", 16)
        b.inc(s.dma_start(out=t["ropeSt"][:, :], in_=t["ropeS"]), "ld", 16)
        b.inc(s.dma_start(out=t["rowf0"][0:1, 0:1024], in_=t["g1"]), "ld", 16)
        b.inc(s.dma_start(out=t["rowf0"][0:1, 1024:2048], in_=t["b1ln"]), "ld", 16)

    @blk.gpsimd
    def _(g):
        b.inc(g.dma_start(out=t["WQ1"][:, :], in_=t["wqkv1T"]), "ldg", 16)
        g.memset(t["ones_r"][:, :], 1.0)

    pe0, dv0 = b.cnt["pe"], b.cnt["dve"]
    ld_p1 = b.cnt["ld"]

    @blk.sync
    def _(s):
        b.waitv(s, "pe", pe0 + 16)
        b.inc(s.dma_start(out=t["rowf0"][0:1, 0:1024], in_=t["g2"]), "ldv", 16)
        b.inc(s.dma_start(out=t["rowf0"][0:1, 1024:2048], in_=t["b2ln"]), "ldv", 16)
    ld_p2 = b.cnt["ldv"]

    with nc.psum_tensor("pp0", [128, 128], F32) as pp0:
        @blk.tensor
        def _(tt):
            for i in range(32):
                v, k = i // 8, i % 8
                col = 1024 * (v % 2) + 128 * k
                if i < 16:
                    b.waitv(tt, "ld", ld_p1)
                else:
                    b.waitv(tt, "ldv", ld_p2)
                b.waitv(tt, "dve", dv0 + i)
                b.inc(tt.transpose(pp0[:, 0:1], t["rowf0"][0:1, col: col + 128], t["identf"][0:1, 0:1]), "pe", 1)

        @blk.vector
        def _(v):
            for i in range(32):
                b.waitv(v, "pe", pe0 + i + 1)
                b.inc(v.tensor_copy(t["vecc"][:, i: i + 1], pp0[:, 0:1]), "dve", 1)


def phase_A0(b, blk):
    nc, t = b.nc, b.t

    @blk.sync
    def _(s):
        b.inc(s.dma_start(out=t["xmy"][:, :], in_=t["x_my"]), "ldx", 16)

    pe0, dv0, ac0 = b.cnt["pe"], b.cnt["dve"], b.cnt["act"]
    pa0_cm = nc.psum_tensor("pa0", [128, 2048], BF16)
    pa0 = pa0_cm.__enter__()

    @blk.vector
    def _(v):
        b.wait(v, "ld")
        b.wait(v, "ldx")
        for j in range(NJ):
            xj = t["xmy"][:, 1024 * j: 1024 * (j + 1)]
            b.vred(v, "reduce_sum", t["stat"][:, j: j + 1], xj, AX.X)
            b.fence(v, v.tensor_scalar(t["stat"][:, 16 + j: 16 + j + 1], t["stat"][:, j: j + 1],
                                       float(INV_D), None, ALU.mult))
            b.fence(v, v.tensor_scalar(t["xc"][:, :], xj, t["stat"][:, 16 + j: 16 + j + 1], None, ALU.subtract))
            b.fence(v, v.tensor_tensor(t["sq"][:, :], t["xc"][:, :], t["xc"][:, :], ALU.mult))
            b.vred(v, "reduce_sum", t["stat"][:, j: j + 1], t["sq"][:, :], AX.X)
            b.inc(v.tensor_scalar(t["stat"][:, j: j + 1], t["stat"][:, j: j + 1], float(INV_D), float(EPS),
                                  ALU.mult, ALU.add), "dve", 1)
            b.waitv(v, "act", ac0 + j + 1)
            b.waitv(v, "pe", pe0 + 8 * j)
            b.inc(v.tensor_scalar(t["bfs"][:, 0:1024], t["xc"][:, :], t["rstdb"][:, j: j + 1], None, ALU.mult),
                  "dve", 1)
            b.waitv(v, "pe", pe0 + 8 * (j + 1))
            for k in range(8):
                b.inc(v.tensor_copy(t["n1tS"][:, 2048 * k + 128 * j: 2048 * k + 128 * (j + 1)],
                                    pa0[:, 128 * k: 128 * (k + 1)]), "dve", 1)

    @blk.scalar
    def _(sc):
        for j in range(NJ):
            b.waitv(sc, "dve", dv0 + j * 10 + 1)
            sc.activation(t["stat"][:, 16 + j: 16 + j + 1], t["stat"][:, j: j + 1], AF.Ln)
            b.inc(sc.activation(t["rstdb"][:, j: j + 1], t["stat"][:, 16 + j: 16 + j + 1], AF.Exp,
                                scale=np.float32(-0.5)), "act", 1)

    @blk.tensor
    def _(tt):
        for j in range(NJ):
            b.waitv(tt, "dve", dv0 + j * 10 + 2)
            if j >= 1:
                b.waitv(tt, "dve", dv0 + j * 10)
            for k in range(8):
                b.inc(tt.transpose(pa0[:, 128 * k: 128 * (k + 1)],
                                   t["bfs"][:, 128 * k: 128 * (k + 1)], t["identb"][:, :]), "pe", 1)

    pa0_cm.__exit__(None, None, None)

    @blk.sync
    def _(s):
        b.wait(s, "dve")
        for k in range(8):
            b.inc(s.dma_start(out=t["n1t_in"][128 * k: 128 * (k + 1), :],
                              in_=t["n1tS"][:, 2048 * k: 2048 * (k + 1)]), "ld2", 16)

    @blk.gpsimd
    def _(g):
        b.wait(g, "ld2")
        b.inc(g.collective_compute("AllGather", ALU.bypass, replica_groups=GROUPS8,
                                   ins=[t["n1t_in"].opt()], outs=[t["n1t_ag"].opt()]), "cc", 1)


def phase_A(b, blk):
    nc, t = b.nc, b.t
    slabs = [t["slab0"], t["slab1"]]
    NU = 128

    cc_need = b.cnt["cc"]
    ld2_0, ld3_0 = b.cnt["ld2"], b.cnt["ld3"]
    pe0, pe2_0 = b.cnt["pe"], b.cnt["pe2"]
    dv2_0, dv3_0, dv4_0 = b.cnt["dve2"], b.cnt["dve3"], b.cnt["dve4"]

    @blk.vector
    def _(v):
        b.wait(v, "ld2")
        v.memset(t["VP"][:, :], 0.0)
        b.inc(v.memset(t["VP"].rearrange("p (n f) -> p n f", f=65)[:, :, 64:65], 1.0), "dve2", 1)

    @blk.sync
    def _(s):
        b.waitv(s, "cc", cc_need)
        for g in range(16):
            r, half = g // 2, g % 2
            if g >= 2:
                b.waitv(s, "pe", pe0 + (g - 1) * 8)
            sem = "ld2" if g % 2 == 0 else "ld3"
            for k in range(8):
                b.inc(s.dma_start(out=slabs[g % 2][:, 1024 * k: 1024 * (k + 1)],
                                  in_=t["n1t_ag"][1024 * r + 128 * k: 1024 * r + 128 * (k + 1),
                                                  1024 * half: 1024 * (half + 1)]), sem, 16)

    pcms = [nc.psum_tensor("paq0", [128, 384], F32), nc.psum_tensor("paq1", [128, 384], F32),
            nc.psum_tensor("pat0", [128, 256], BF16), nc.psum_tensor("pat1", [128, 256], BF16)]
    hs = [cm.__enter__() for cm in pcms]
    pq, pt = hs[0:2], hs[2:4]

    def emit_transposes(tt, up):
        b.waitv(tt, "dve3", dv3_0 + up + 1)
        if up >= 2:
            b.waitv(tt, "dve4", dv4_0 + up - 1)
        ptu = pt[up % 2]
        sl = 512 * (up % 2)
        tt.transpose(ptu[:, 0:128], t["bfs"][:, sl: sl + 128], t["identb"][:, :])
        b.inc(tt.transpose(ptu[:, 128:256], t["bfs"][:, sl + 128: sl + 256], t["identb"][:, :]), "pe2", 1)

    @blk.tensor
    def _(tt):
        b.wait(tt, "ldg")
        b.waitv(tt, "dve2", dv2_0 + 1)
        for u in range(NU):
            g, j2 = u // 8, u % 8
            if g % 2 == 0:
                b.waitv(tt, "ld2", ld2_0 + (g // 2 + 1) * 128)
            else:
                b.waitv(tt, "ld3", ld3_0 + (g // 2 + 1) * 128)
            if u >= 2:
                b.waitv(tt, "dve2", dv2_0 + 1 + (u - 1))
            pqu = pq[u % 2]
            for k in range(8):
                mm = tt.matmul(pqu[:, :], slabs[g % 2][:, 1024 * k + 128 * j2: 1024 * k + 128 * (j2 + 1)],
                               t["WQ1"][:, 384 * k: 384 * (k + 1)], start=(k == 0), stop=(k == 7))
            b.inc(mm, "pe", 1)
            if u >= 1:
                emit_transposes(tt, u - 1)
        emit_transposes(tt, NU - 1)

    @blk.vector
    def _(v):
        for u in range(NU):
            g, j2 = u // 8, u % 8
            r, half = g // 2, g % 2
            tile = 16 * r + 8 * half + j2
            npos = tile % 64
            pqu = pq[u % 2]
            b.waitv(v, "pe", pe0 + u + 1)
            if u >= 2:
                b.waitv(v, "pe2", pe2_0 + u - 1)
            cosT = t["ropeCt"][:, 64 * npos: 64 * (npos + 1)]
            sinT = t["ropeSt"][:, 64 * npos: 64 * (npos + 1)]
            cos4 = cosT.unsqueeze(1).broadcast_to((128, 4, 64))
            sin32 = sinT[:, 0:32].unsqueeze(1).broadcast_to((128, 4, 32))
            qk4 = pqu.rearrange("p (a f) -> p a f", f=64)[:, 0:4, :]
            qk42 = pqu.rearrange("p (a c f) -> p a c f", c=2, f=32)[:, 0:4, :, :]
            mv = t["msc"].rearrange("p (a f) -> p a f", f=64)[:, 0:4, :]
            rv = t["xc"].rearrange("p (a c f) -> p a c f", c=2, f=32)[:, 0:4, :, :]
            b.fence(v, v.tensor_tensor(mv, qk4, cos4, ALU.mult))
            b.fence(v, v.scalar_tensor_tensor(rv[:, :, 0, :], qk42[:, :, 1, :], -1.0, sin32, ALU.mult, ALU.mult))
            b.fence(v, v.scalar_tensor_tensor(rv[:, :, 1, :], qk42[:, :, 0, :], 1.0, sin32, ALU.mult, ALU.mult))
            sl = 512 * (u % 2)
            b.inc(v.tensor_add(t["bfs"][:, sl: sl + 256], t["msc"][:, 0:256], t["xc"][:, 0:256]), "dve3", 1)
            b.inc(v.tensor_copy(t["VP"].rearrange("p (n f) -> p n f", f=65)[:, 2 * tile: 2 * tile + 2, 0:64],
                                pqu.rearrange("p (n f) -> p n f", f=64)[:, 4:6, :]), "dve2", 1)
            if u >= 1:
                copy_trans(b, v, pt, pe2_0, dv4_0, u - 1)
        copy_trans(b, v, pt, pe2_0, dv4_0, NU - 1)

    for cm in reversed(pcms):
        cm.__exit__(None, None, None)


def copy_trans(b, v, pt, pe2_0, dv4_0, up):
    t = b.t
    gp_, j2p = up // 8, up % 8
    rp, halfp = gp_ // 2, gp_ % 2
    tp = 16 * rp + 8 * halfp + j2p
    b.waitv(v, "pe2", pe2_0 + up + 1)
    ptu = pt[up % 2]
    v.tensor_copy(t["QT"][:, 128 * tp: 128 * (tp + 1)], ptu[:, 0:128])
    b.inc(v.tensor_copy(t["KT"][:, 128 * tp: 128 * (tp + 1)], ptu[:, 128:256]), "dve4", 1)


def phase_B(b, blk):
    nc, t = b.nc, b.t

    units = []
    for bt in range(2):
        for i in range(2):
            for qb in range(16):
                nch = 4 * (qb + 1)
                groups = [list(range(g, min(g + 3, nch))) for g in range(0, nch, 3)]
                units.append((bt, i, qb, nch, groups))
    NUB = len(units)
    gg_of_unit = []
    tot = 0
    for (_, _, _, nch, groups) in units:
        gg_of_unit.append(tot)
        tot += len(groups)
    NGG = tot

    pe0 = b.cnt["pe"]
    pe2_0 = b.cnt["pe2"]
    pe3_0 = b.cnt["pe3"]
    ac0 = b.cnt["act"]
    dv0 = b.cnt["dve"]
    b.cnt_dv0_B = dv0
    dv2_0 = b.cnt["dve2"]
    dv3_0 = b.cnt["dve3"]
    ld2_0 = b.cnt["ld2"]
    PB = [t["PB0"], t["PB1"]]

    def kv_ap(bt, n):
        return t["KT"][:, 8192 * bt + 128 * n: 8192 * bt + 128 * (n + 1)]

    def masked_j(qb, n):
        j = n - 4 * qb
        return j if 0 <= j < 4 else None

    mask_cnt_before = []
    mk = 0
    unit_mask_counts = []
    for (bt, i, qb, nch, groups) in units:
        mask_cnt_before.append(mk)
        nm = sum(1 for n in range(nch) if masked_j(qb, n) is not None)
        unit_mask_counts.append(nm)
        mk += nm

    with nc.psum_tensor("bsa", [128, 1536], F32) as bsa,          nc.psum_tensor("bsb", [128, 1536], F32) as bsb,          nc.psum_tensor("bop", [65, 512], F32) as bop,          nc.psum_tensor("btr", [128, 264], BF16) as btr:
        SBUFS = [bsa, bsb]

        @blk.tensor
        def _(tt):
            for u, (bt, i, qb, nch, groups) in enumerate(units):
                base = gg_of_unit[u]
                qcol = 8192 * bt + 512 * qb
                rows = slice(64 * i, 64 * i + 64)
                for g, grp in enumerate(groups):
                    gg = base + g
                    if gg >= 2:
                        b.waitv(tt, "act", ac0 + gg - 1)
                    sb_ = SBUFS[gg % 2]
                    for _ in range(16):
                        tt.matmul(sb_[:, 0:64], t["KT"][:, 0:128], t["QT"][:, 0:64],
                                  start=True, stop=True)
                    for k, n in enumerate(grp):
                        mm = tt.matmul(sb_[:, 512 * k: 512 * (k + 1)], kv_ap(bt, n)[rows, :].copy() if False else t["KT"][64 * i: 64 * i + 64, 8192 * bt + 128 * n: 8192 * bt + 128 * (n + 1)],
                                       t["QT"][64 * i: 64 * i + 64, qcol: qcol + 512],
                                       start=True, stop=True)
                    b.inc(mm, "pe", 1)
                    if g == 0:
                        b.waitv(tt, "dve2", dv2_0 + u)
                    if g >= 1:
                        emit_av(b, tt, units, u, g - 1, bop, PB, ac0, dv0, gg_of_unit, mask_cnt_before)
                emit_av(b, tt, units, u, len(groups) - 1, bop, PB, ac0, dv0, gg_of_unit, mask_cnt_before)
                for _ in range(8):
                    tt.matmul(SBUFS[0][:, 0:64], t["KT"][:, 0:128], t["QT"][:, 0:64],
                              start=True, stop=True)
                b.waitv(tt, "dve2", dv2_0 + u + 1)
                for tch in range(4):
                    mm = tt.transpose(btr[:, 66 * tch: 66 * tch + 65],
                                      t["mscb"][0:65, 128 * tch: 128 * (tch + 1)],
                                      t["identb"][0:65, 0:65])
                b.inc(mm, "pe3", 1)

        @blk.scalar
        def _(sc):
            for u, (bt, i, qb, nch, groups) in enumerate(units):
                base = gg_of_unit[u]
                for g, grp in enumerate(groups):
                    gg = base + g
                    b.waitv(sc, "pe", pe0 + gg + 1)
                    if gg >= 2:
                        b.waitv(sc, "pe2", pe2_0 + gg - 1)
                    sb_ = SBUFS[gg % 2]
                    b.inc(sc.activation(PB[gg % 2][:, 0: 512 * len(grp)], sb_[:, 0: 512 * len(grp)],
                                        AF.Exp, bias=C0, scale=INV8), "act", 1)

        @blk.vector
        def _(v):
            for u, (bt, i, qb, nch, groups) in enumerate(units):
                base = gg_of_unit[u]
                nm = 0
                for g, grp in enumerate(groups):
                    gg = base + g
                    if any(masked_j(qb, n) is not None for n in grp):
                        b.waitv(v, "act", ac0 + gg + 1)
                        for k, n in enumerate(grp):
                            j = masked_j(qb, n)
                            if j is not None:
                                sl = PB[gg % 2][:, 512 * k: 512 * (k + 1)]
                                b.inc(v.tensor_tensor(sl, sl, t["masks"][:, 512 * j: 512 * (j + 1)], ALU.mult),
                                      "dve", 1)
                                nm += 1
                assert nm == unit_mask_counts[u]
                b.waitv(v, "pe2", pe2_0 + gg_of_unit[u] + len(groups))
                b.waitv(v, "pe3", pe3_0 + u)
                b.inc(v.tensor_copy(t["mscb"][0:65, 0:512], bop[:, :]), "dve2", 1)
                b.waitv(v, "pe3", pe3_0 + u + 1)
                for tch in range(4):
                    b.fence(v, v.reciprocal(t["lrec"][:, tch: tch + 1], btr[:, 66 * tch + 64: 66 * tch + 65]))
                if u >= 2:
                    b.waitv(v, "ld2", ld2_0 + (u - 1) * 16)
                for tch in range(4):
                    i3 = v.tensor_scalar(t["obst"][:, 256 * (u % 2) + 64 * tch: 256 * (u % 2) + 64 * (tch + 1)],
                                         btr[:, 66 * tch: 66 * tch + 64], t["lrec"][:, tch: tch + 1], None, ALU.mult)
                b.inc(i3, "dve3", 1)

        @blk.sync
        def _(s):
            for u, (bt, i, qb, nch, groups) in enumerate(units):
                b.waitv(s, "dve3", dv3_0 + u + 1)
                dst = t["og_in"][8192 * bt + 512 * qb: 8192 * bt + 512 * (qb + 1), 64 * i: 64 * i + 64]
                src_ = t["obst"][:, 256 * (u % 2): 256 * (u % 2 + 1)].rearrange("p (tc f) -> p tc f", f=64)
                b.inc(s.dma_start(out=dst.rearrange("(tc p) f -> p tc f", p=128), in_=src_), "ld2", 16)



def phase_C(b, blk):
    nc, t = b.nc, b.t
    lda0 = b.cnt["lda"]
    ldb0 = b.cnt["ldb"]
    ldc0 = b.cnt["ldc"]
    ld2f = b.cnt["ld2"]
    pe0 = b.cnt["pe"]
    pe2_0 = b.cnt["pe2"]
    dv0 = b.cnt["dve"]
    dv2_0 = b.cnt["dve2"]
    cc0 = b.cnt["cc"]
    ogld = [t["ogld0"], t["ogld1"]]
    cstg = [t["cstg0"], t["cstg1"]]

    @blk.gpsimd
    def _(g):
        b.waitv(g, "ld2", ld2f)
        b.inc(g.dma_start(out=t["WO1"][:, :], in_=t["wo1c"]), "ldg", 16)
    ldg_w = b.cnt["ldg"]

    tiles = [(kp, cp, mm_) for kp in range(4) for cp in range(8) for mm_ in range(4)]

    with nc.psum_tensor("ctr0", [128, 128], BF16) as ctr0,          nc.psum_tensor("ctr1", [128, 128], BF16) as ctr1,          nc.psum_tensor("cxo0", [128, 1024], F32) as cxo0,          nc.psum_tensor("cxo1", [128, 1024], F32) as cxo1:
        ctr = [ctr0, ctr1]
        cxo = [cxo0, cxo1]

        @blk.sync
        def _(s):
            b.waitv(s, "ld2", ld2f)
            for u, (kp, cp, mm_) in enumerate(tiles):
                row = 2048 * cp + 512 * kp + 128 * mm_
                if u >= 2:
                    b.waitv(s, "pe", pe0 + u - 1)
                sem = "lda" if u % 2 == 0 else "ldb"
                b.inc(s.dma_start(out=ogld[u % 2][:, :], in_=t["og_in"][row: row + 128, :]), sem, 16)
                if u % 4 == 3:
                    grp = u // 4
                    kpg, cpg = grp // 8, grp % 8
                    b.waitv(s, "dve2", dv2_0 + (grp + 1) * 4)
                    dst = t["xo_rs_in"][4096 * kpg + 512 * cpg: 4096 * kpg + 512 * (cpg + 1), :]
                    b.inc(s.dma_start(out=dst.rearrange("(m p) f -> p m f", p=128),
                                      in_=cstg[grp % 2].rearrange("p (m f) -> p m f", f=1024)), "ldc", 16)

        @blk.tensor
        def _(tt):
            b.waitv(tt, "ldg", ldg_w)
            for u, (kp, cp, mm_) in enumerate(tiles):
                if u % 2 == 0:
                    b.waitv(tt, "lda", lda0 + (u // 2 + 1) * 16)
                else:
                    b.waitv(tt, "ldb", ldb0 + (u // 2 + 1) * 16)
                if u >= 2:
                    b.waitv(tt, "dve", dv0 + u - 1)
                b.inc(tt.transpose(ctr[u % 2][:, :], ogld[u % 2][:, :], t["identb"][:, :]), "pe", 1)
                b.waitv(tt, "dve", dv0 + u + 1)
                if u >= 2:
                    b.waitv(tt, "dve2", dv2_0 + u - 1)
                mm = tt.matmul(cxo[u % 2][:, 0:512], t["oglh"][:, :], t["WO1"][:, 0:512], start=True, stop=True)
                mm = tt.matmul(cxo[u % 2][:, 512:1024], t["oglh"][:, :], t["WO1"][:, 512:1024], start=True, stop=True)
                b.inc(mm, "pe2", 1)

        @blk.vector
        def _(v):
            for u, (kp, cp, mm_) in enumerate(tiles):
                b.waitv(v, "pe", pe0 + u + 1)
                if u >= 1:
                    b.waitv(v, "pe2", pe2_0 + u)
                b.inc(v.tensor_copy(t["oglh"][:, :], ctr[u % 2][:, :]), "dve", 1)
                b.waitv(v, "pe2", pe2_0 + u + 1)
                gpar = (u // 4) % 2
                if u >= 8:
                    b.waitv(v, "ldc", ldc0 + (u // 4 - 1) * 16)
                b.inc(v.tensor_copy(cstg[gpar][:, 1024 * mm_: 1024 * (mm_ + 1)], cxo[u % 2][:, :]), "dve2", 1)


    @blk.gpsimd
    def _(g):
        for kp in range(4):
            b.waitv(g, "ldc", ldc0 + (8 * kp + 8) * 16)
            b.inc(g.collective_compute("ReduceScatter", ALU.add, replica_groups=GROUPS8,
                                       ins=[t["xo_rs_in"][4096 * kp: 4096 * (kp + 1), :].opt()],
                                       outs=[t["xo_rs_out"][512 * kp: 512 * (kp + 1), :].opt()]), "cc", 1)

    @blk.sync
    def _(s):
        b.inc(s.dma_start(out=t["r1f"][:, :], in_=t["x_my"]), "ldx", 16)
    ldx_f = b.cnt["ldx"]

    dvf0 = b.cnt["dve3"]
    acf0 = b.cnt["act2"]

    @blk.gpsimd
    def _(g):
        for j in range(NJ):
            kp = j // 4
            b.waitv(g, "cc", cc0 + kp + 1)
            if j >= 2:
                b.waitv(g, "dve3", dvf0 + (j - 1) * 2)
            xb = t["xc"] if j % 2 == 0 else t["msc"]
            row = 128 * j
            b.inc(g.dma_start(out=xb[:, :], in_=t["xo_rs_out"][row: row + 128, :]),
                  "gp" if j % 2 == 0 else "gp2", 16)

    @blk.vector
    def _(v):
        b.waitv(v, "ldx", ldx_f)
        for j in range(NJ):
            xb = t["xc"] if j % 2 == 0 else t["msc"]
            sem = "gp" if j % 2 == 0 else "gp2"
            b.waitv(v, sem, b.cnt[sem] - 16 * (NJ // 2 - 1 - j // 2))
            b.vred(v, "reduce_sum", t["stat"][:, j: j + 1], xb[:, :], AX.X)
            b.fence(v, v.tensor_scalar(t["stat"][:, 16 + j: 16 + j + 1], t["stat"][:, j: j + 1],
                                       float(INV_D), None, ALU.mult))
            b.fence(v, v.tensor_scalar(t["sq"][:, :], xb[:, :], t["stat"][:, 16 + j: 16 + j + 1],
                                       None, ALU.subtract))
            b.fence(v, v.tensor_tensor(t["csq"][:, :], t["sq"][:, :], t["sq"][:, :], ALU.mult))
            b.vred(v, "reduce_sum", t["stat"][:, j: j + 1], t["csq"][:, :], AX.X)
            b.inc(v.tensor_scalar(t["stat"][:, j: j + 1], t["stat"][:, j: j + 1], float(INV_D), float(EPS),
                                  ALU.mult, ALU.add), "dve3", 1)
            b.waitv(v, "act2", acf0 + j + 1)
            b.fence(v, v.tensor_scalar(t["sq"][:, :], t["sq"][:, :], t["rstdb"][:, j: j + 1], None, ALU.mult))
            b.inc(v.tensor_tensor(t["r1f"][:, 1024 * j: 1024 * (j + 1)], t["r1f"][:, 1024 * j: 1024 * (j + 1)],
                                  t["sq"][:, :], ALU.add), "dve3", 1)

    @blk.scalar
    def _(sc):
        for j in range(NJ):
            b.waitv(sc, "dve3", dvf0 + 2 * j + 1)
            sc.activation(t["stat"][:, 16 + j: 16 + j + 1], t["stat"][:, j: j + 1], AF.Ln)
            b.inc(sc.activation(t["rstdb"][:, j: j + 1], t["stat"][:, 16 + j: 16 + j + 1], AF.Exp,
                                scale=np.float32(-0.5)), "act2", 1)

    @blk.sync
    def _(s):
        b.wait(s, "dve3")
        b.inc(s.dma_start(out=t["x1_pk"], in_=t["r1f"][:, :]), "ldx", 16)


def phase_D(b, blk):
    nc, t = b.nc, b.t
    dvf = b.cnt["dve3"]
    ldxf = b.cnt["ldx"]

    @blk.gpsimd
    def _(g):
        b.waitv(g, "dve3", dvf)
        b.inc(g.dma_start(out=t["WQ2"][:, :], in_=t["wqkv2"]), "ldg", 16)
        b.inc(g.dma_start(out=t["brow"][0:1, 0:3072], in_=t["bqkv"]), "ldg", 16)
    ldg_d = b.cnt["ldg"]

    pe_be0 = b.cnt["pe"]
    dv_be0 = b.cnt["dve"]
    with nc.psum_tensor("dbe", [1, 512], F32) as dbe:
        @blk.vector
        def _(v):
            b.inc(v.tensor_copy(t["b1c_bf"][:, 0:8], t["vecc"][:, 8:16]), "dve", 1)

        @blk.tensor
        def _(tt):
            b.waitv(tt, "ldg", ldg_d)
            b.waitv(tt, "dve", dv_be0 + 1)
            for cb in range(6):
                if cb >= 1:
                    b.waitv(tt, "dve", dv_be0 + 1 + cb)
                for k in range(8):
                    tt.matmul(dbe[:, :], t["b1c_bf"][:, k: k + 1],
                              t["WQ2"][:, 3072 * k + 512 * cb: 3072 * k + 512 * (cb + 1)],
                              start=(k == 0), stop=False)
                mm = tt.matmul(dbe[:, :], t["ones_r"][0:1, 0:1], t["brow"][0:1, 512 * cb: 512 * (cb + 1)],
                               start=False, stop=True)
                b.inc(mm, "pe", 1)

        @blk.vector
        def _(v):
            for cb in range(6):
                b.waitv(v, "pe", pe_be0 + cb + 1)
                b.inc(v.tensor_copy(t["beffq"][0:1, 512 * cb: 512 * (cb + 1)], dbe[0:1, :]), "dve", 1)
            for k in range(8):
                i2 = v.tensor_scalar(t["WQ2"][:, 3072 * k: 3072 * (k + 1)], t["WQ2"][:, 3072 * k: 3072 * (k + 1)],
                                     t["vecc"][:, k: k + 1], None, ALU.mult)
            b.inc(i2, "dve2", 1)
            v.memset(t["V2ST"][:, :], 0.0)
            b.inc(v.memset(t["V2ST"].rearrange("p (h f) -> p h f", f=128)[:, :, 64:65], 1.0), "dve2", 1)
    dv2_w = b.cnt["dve2"]

    pe0 = b.cnt["pe"]
    pe2_0 = b.cnt["pe2"]
    pe3_0 = b.cnt["pe3"]
    ac0 = b.cnt["act"]
    ac2_0 = b.cnt["act2"]
    dv0 = b.cnt["dve"]
    dv3_0 = b.cnt["dve3"]
    dv4_0 = b.cnt["dve4"]
    lda0, ldb0 = b.cnt["lda"], b.cnt["ldb"]
    ldc0 = b.cnt["ldc"]
    ld2_0, ld3_0 = b.cnt["ld2"], b.cnt["ld3"]

    with nc.psum_tensor("pdtr", [128, 1024], BF16) as pdtr, \
         nc.psum_tensor("pdm0", [128, 512], F32) as pdm0, \
         nc.psum_tensor("pdm1", [128, 512], F32) as pdm1, \
         nc.psum_tensor("pdq0", [128, 512], BF16) as pdq0, \
         nc.psum_tensor("pdq1", [128, 512], BF16) as pdq1:
        pdm = [pdm0, pdm1]
        pdq = [pdq0, pdq1]

        @blk.sync
        def _(s):
            b.waitv(s, "ldx", ldxf)
            for j in range(NJ):
                sem = "lda" if j % 2 == 0 else "ldb"
                if j >= 2:
                    b.waitv(s, "dve", dv0 + (j - 2) * 9 + 1)
                b.inc(s.dma_start(out=(t["xc"] if j % 2 == 0 else t["dsq"])[:, :],
                                  in_=t["x1_pk"][:, 1024 * j: 1024 * (j + 1)]), sem, 16)
                if j >= 1:
                    b.waitv(s, "dve", dv0 + (j - 1) * 9 + 2)
                    b.inc(s.dma_start(out=t["u_pk"][:, 1024 * (j - 1): 1024 * j], in_=t["msc"][:, :]), "ldc", 16)
                    b.waitv(s, "dve", dv0 + 9 * (j - 1) + 9)
                    dstv = t["v2h_in"].rearrange("(h jj p) f -> p h jj f", h=H, p=128)[:, :, j - 1, :]
                    b.inc(s.dma_start(out=dstv, in_=t["V2ST"].rearrange("p (h f) -> p h f", f=128)), "ld2", 16)
            b.waitv(s, "dve", dv0 + (NJ - 1) * 9 + 2)
            b.inc(s.dma_start(out=t["u_pk"][:, 1024 * (NJ - 1): 1024 * NJ], in_=t["msc"][:, :]), "ldc", 16)
            b.waitv(s, "dve", dv0 + 9 * NJ)
            dstv = t["v2h_in"].rearrange("(h jj p) f -> p h jj f", h=H, p=128)[:, :, NJ - 1, :]
            b.inc(s.dma_start(out=dstv, in_=t["V2ST"].rearrange("p (h f) -> p h f", f=128)), "ld2", 16)
            b.waitv(s, "dve4", dv4_0 + 4 * NJ)
            for k in range(8):
                b.inc(s.dma_start(out=t["k2t_in"][128 * k: 128 * (k + 1), :],
                                  in_=t["K2TS"][:, 2048 * k: 2048 * (k + 1)]), "ld3", 16)

        @blk.vector
        def _(v):
            for j in range(NJ):
                xb = t["xc"] if j % 2 == 0 else t["dsq"]
                sqs = t["sq"]
                st_ = t["stat"]
                sem = "lda" if j % 2 == 0 else "ldb"
                b.waitv(v, sem, (lda0 if j % 2 == 0 else ldb0) + (j // 2 + 1) * 16)
                if j >= 1:
                    b.waitv(v, "ldc", ldc0 + j * 16)
                b.vred(v, "reduce_sum", st_[:, 0:1], xb[:, :], AX.X)
                b.fence(v, v.tensor_scalar(st_[:, 1:2], st_[:, 0:1], float(INV_D), None, ALU.mult))
                b.fence(v, v.tensor_scalar(sqs[:, :], xb[:, :], st_[:, 1:2], None, ALU.subtract))
                b.fence(v, v.tensor_tensor(t["csq2"][:, :], sqs[:, :], sqs[:, :], ALU.mult))
                b.vred(v, "reduce_sum", st_[:, 0:1], t["csq2"][:, :], AX.X)
                b.inc(v.tensor_scalar(st_[:, 0:1], st_[:, 0:1], float(INV_D), float(EPS),
                                      ALU.mult, ALU.add), "dve3", 1)
                b.waitv(v, "act", ac0 + j + 1)
                b.inc(v.tensor_scalar(t["msc"][:, :], sqs[:, :], t["rstdb"][:, 0:1], None, ALU.mult), "dve", 1)
                b.fence(v, v.tensor_copy(t["gsc"][:, 0:1], t["msc"][:, 1023:1024]))
                b.vred(v, "reduce_sum", st_[:, 2:3], t["msc"][:, :], AX.X)
                b.fence(v, v.tensor_scalar(st_[:, 3:4], st_[:, 2:3], float(INV_D), None, ALU.mult))
                b.fence(v, v.tensor_scalar(sqs[:, :], t["msc"][:, :], st_[:, 3:4], None, ALU.subtract))
                b.fence(v, v.tensor_tensor(t["csq2"][:, :], sqs[:, :], sqs[:, :], ALU.mult))
                b.vred(v, "reduce_sum", st_[:, 2:3], t["csq2"][:, :], AX.X)
                b.inc(v.tensor_scalar(st_[:, 2:3], st_[:, 2:3], float(INV_D), float(EPS),
                                      ALU.mult, ALU.add), "dve3", 1)
                b.waitv(v, "act2", ac2_0 + j + 1)
                if j >= 1:
                    b.waitv(v, "pe", pe0 + j)
                b.inc(v.tensor_scalar(t["n1ib"][:, :], sqs[:, :], t["rstdb"][:, 1:2], None, ALU.mult), "dve", 1)
                b.waitv(v, "pe", pe0 + j + 1)
                if j >= 1:
                    b.waitv(v, "pe2", pe2_0 + 6 * j)
                b.inc(v.tensor_copy(t["n1it"][:, :], pdtr[:, :]), "dve", 1)
                def qkcopy(cb2):
                    b.waitv(v, "pe3", pe3_0 + j * 4 + cb2 + 1)
                    pq = pdq[cb2 % 2]
                    dstb = t["Q2T"] if cb2 < 2 else t["K2TS"]
                    for s4 in range(4):
                        kp = 4 * (cb2 % 2) + s4
                        i5 = v.tensor_copy(dstb[:, 2048 * kp + 128 * j: 2048 * kp + 128 * (j + 1)],
                                           pq[:, 128 * s4: 128 * (s4 + 1)])
                    b.inc(i5, "dve4", 1)
                for cb in range(6):
                    b.waitv(v, "pe2", pe2_0 + 6 * j + cb + 1)
                    pm = pdm[cb % 2]
                    if cb < 4:
                        nprev = j * 4 + cb
                        if nprev >= 1:
                            b.waitv(v, "pe3", pe3_0 + nprev)
                        b.inc(v.tensor_copy(t["qsb"][:, :], pm[:, :]), "dve", 1)
                    else:
                        hb = 8 * (cb - 4)
                        if cb == 4 and j >= 1:
                            b.waitv(v, "ld2", ld2_0 + 16 * j)
                        b.inc(v.tensor_copy(
                            t["V2ST"].rearrange("p (h f) -> p h f", f=128)[:, hb: hb + 8, 0:64],
                            pm.rearrange("p (h f) -> p h f", f=64)[:, 0:8, :]), "dve", 1)
                    if 1 <= cb <= 4:
                        qkcopy(cb - 1)

        @blk.scalar
        def _(sc):
            for j in range(NJ):
                b.waitv(sc, "dve3", dv3_0 + 2 * j + 1)
                sc.activation(t["stat"][:, 1:2], t["stat"][:, 0:1], AF.Ln)
                b.inc(sc.activation(t["rstdb"][:, 0:1], t["stat"][:, 1:2], AF.Exp, scale=np.float32(-0.5)),
                      "act", 1)
                b.waitv(sc, "dve3", dv3_0 + 2 * j + 2)
                sc.activation(t["stat"][:, 3:4], t["stat"][:, 2:3], AF.Ln)
                b.inc(sc.activation(t["rstdb"][:, 1:2], t["stat"][:, 3:4], AF.Exp, scale=np.float32(-0.5)),
                      "act2", 1)

        @blk.tensor
        def _(tt):
            b.waitv(tt, "dve2", dv2_w)
            b.waitv(tt, "ldx", ldxf)
            for j in range(NJ):
                b.waitv(tt, "dve", dv0 + 9 * j + 2)
                for k in range(8):
                    mm = tt.transpose(pdtr[:, 128 * k: 128 * (k + 1)], t["n1ib"][:, 128 * k: 128 * (k + 1)],
                                      t["identb"][:, :])
                b.inc(mm, "pe", 1)
                b.waitv(tt, "dve", dv0 + 9 * j + 3)
                for cb in range(6):
                    pm = pdm[cb % 2]
                    nprv = j * 6 + cb
                    if nprv >= 2:
                        pj, pcb = (nprv - 2) // 6, (nprv - 2) % 6
                        b.waitv(tt, "dve", dv0 + 9 * pj + 4 + pcb)
                    for k in range(8):
                        tt.matmul(pm[:, :], t["n1it"][:, 128 * k: 128 * (k + 1)],
                                  t["WQ2"][:, 3072 * k + 512 * cb: 3072 * k + 512 * (cb + 1)],
                                  start=(k == 0), stop=False)
                    mm = tt.matmul(pm[:, :], t["ones_r"][0:1, 0:128], t["beffq"][0:1, 512 * cb: 512 * (cb + 1)],
                                   start=False, stop=True)
                    b.inc(mm, "pe2", 1)
                    if cb < 4:
                        b.waitv(tt, "dve", dv0 + 9 * j + 4 + cb)
                        pq = pdq[cb % 2]
                        ntr = j * 4 + cb
                        if ntr >= 2:
                            b.waitv(tt, "dve4", dv4_0 + ntr - 1)
                        for s4 in range(4):
                            mm2 = tt.transpose(pq[:, 128 * s4: 128 * (s4 + 1)],
                                               t["qsb"][:, 128 * s4: 128 * (s4 + 1)], t["identb"][:, :])
                        b.inc(mm2, "pe3", 1)

    @blk.gpsimd
    def _(g):
        b.wait(g, "ld3")
        b.inc(g.collective_compute("AllGather", ALU.bypass, replica_groups=GROUPS4,
                                   ins=[t["k2t_in"].opt()], outs=[t["k2t_ag"].opt()]), "cc", 1)
        b.wait(g, "ld2")
        b.inc(g.collective_compute("AllGather", ALU.bypass, replica_groups=GROUPS4,
                                   ins=[t["v2h_in"].opt()], outs=[t["v2h_ag"].opt()]), "cc", 1)


def phase_E(b, blk):
    nc, t = b.nc, b.t
    ccD = b.cnt["cc"]
    PBe = [t["PB0e"], t["PB1e"]]

    units = [(hp, qb) for hp in range(8) for qb in range(4)]
    NGE = 64

    pe0 = b.cnt["pe"]
    pe2_0 = b.cnt["pe2"]
    pe3_0 = b.cnt["pe3"]
    ac0 = b.cnt["act"]
    dv2_0 = b.cnt["dve2"]
    dv3_0 = b.cnt["dve3"]
    lda0, ldb0 = b.cnt["lda"], b.cnt["ldb"]

    @blk.sync
    def _(s):
        b.waitv(s, "cc", ccD)
        for hp in range(8):
            sem = "lda" if hp % 2 == 0 else "ldb"
            if hp >= 1:
                b.waitv(s, "pe2", pe2_0 + NGE * 4 * hp)
            for h in (2 * hp, 2 * hp + 1):
                for r2 in range(4):
                    b.inc(s.dma_start(out=t["KB"][64 * (h % 2): 64 * (h % 2) + 64, 2048 * r2: 2048 * (r2 + 1)],
                                      in_=t["k2t_ag"][1024 * r2 + 64 * h: 1024 * r2 + 64 * (h + 1), :]), sem, 16)
                vh = t["VHA"] if h % 2 == 0 else t["VHB"]
                for r2 in range(4):
                    b.inc(s.dma_start(
                        out=vh.rearrange("p (rb tt f) -> p rb tt f", rb=4, f=128)[:, r2, :, :],
                        in_=t["v2h_ag"][32768 * r2 + 2048 * h: 32768 * r2 + 2048 * (h + 1), :]
                            .rearrange("(tt p) f -> p tt f", p=128)), sem, 16)

    with nc.psum_tensor("spa", [128, 1024], F32) as spa, \
         nc.psum_tensor("spb", [128, 1024], F32) as spb, \
         nc.psum_tensor("eop0", [128, 512], F32) as eop0, \
         nc.psum_tensor("eop1", [128, 512], F32) as eop1, \
         nc.psum_tensor("etr", [128, 264], BF16) as etr:
        SB2 = [spa, spb]
        EOP = [eop0, eop1]

        @blk.tensor
        def _(tt):
            for u, (hp, qb) in enumerate(units):
                qcol = 2048 * hp + 512 * qb
                if qb == 0:
                    sem = "lda" if hp % 2 == 0 else "ldb"
                    b.waitv(tt, sem, (lda0 if hp % 2 == 0 else ldb0) + (hp // 2 + 1) * 16 * 16)
                for g in range(NGE):
                    gg = NGE * u + g
                    if gg >= 2:
                        b.waitv(tt, "act", ac0 + gg - 1)
                    sb_ = SB2[gg % 2]
                    tt.matmul(sb_[:, 0:512],
                              t["KB"][0:64, 128 * g: 128 * (g + 1)],
                              t["Q2T"][0:64, qcol: qcol + 512],
                              start=True, stop=True)
                    mm = tt.matmul(sb_[:, 512:1024],
                                   t["KB"][64:128, 128 * g: 128 * (g + 1)],
                                   t["Q2T"][64:128, qcol: qcol + 512],
                                   start=True, stop=True)
                    b.inc(mm, "pe", 1)
                    if g == 0:
                        b.waitv(tt, "dve2", dv2_0 + 2 * u)
                    if g >= 1:
                        emit_av_e(b, tt, u, g - 1, EOP, PBe, ac0)
                emit_av_e(b, tt, u, NGE - 1, EOP, PBe, ac0)
                for _ in range(8):
                    tt.matmul(spa[:, 0:64], t["KB"][:, 0:128], t["Q2T"][:, 0:64],
                              start=True, stop=True)
                for p in range(2):
                    b.waitv(tt, "dve2", dv2_0 + 2 * u + p + 1)
                    for tch in range(4):
                        mm = tt.transpose(etr[:, 66 * tch: 66 * tch + 65],
                                          t["mscb"][0:65, 128 * tch: 128 * (tch + 1)],
                                          t["identb"][0:65, 0:65])
                    b.inc(mm, "pe3", 1)

        @blk.scalar
        def _(sc):
            for u in range(len(units)):
                for g in range(NGE):
                    gg = NGE * u + g
                    b.waitv(sc, "pe", pe0 + gg + 1)
                    if gg >= 2:
                        b.waitv(sc, "pe2", pe2_0 + gg - 1)
                    b.inc(sc.activation(PBe[gg % 2][:, 0:1024], SB2[gg % 2][:, 0:1024],
                                        AF.Exp, bias=C0, scale=INV8), "act", 1)

        @blk.vector
        def _(v):
            for u, (hp, qb) in enumerate(units):
                b.waitv(v, "pe2", pe2_0 + NGE * (u + 1))
                for p in range(2):
                    h = 2 * hp + p
                    b.waitv(v, "pe3", pe3_0 + 2 * u + p)
                    b.inc(v.tensor_copy(t["mscb"][0:65, 0:512], EOP[p][0:65, :]), "dve2", 1)
                    b.waitv(v, "pe3", pe3_0 + 2 * u + p + 1)
                    for tch in range(4):
                        b.fence(v, v.reciprocal(t["lrec"][:, tch: tch + 1], etr[:, 66 * tch + 64: 66 * tch + 65]))
                    for tch in range(4):
                        j = 4 * qb + tch
                        i3 = v.tensor_scalar(t["O2N"][:, 1024 * j + 64 * h: 1024 * j + 64 * (h + 1)],
                                             etr[:, 66 * tch: 66 * tch + 64], t["lrec"][:, tch: tch + 1],
                                             None, ALU.mult)
                b.inc(i3, "dve3", 1)

    phase_E_end(b, blk)


def phase_G(b, blk):
    nc, t = b.nc, b.t
    ccE = b.cnt["cc"]
    dvEe = b.cnt["dve"]
    ldcE = b.cnt["ldc"]
    ld3E = b.cnt["ld3"]

    @blk.gpsimd
    def _(g):
        b.waitv(g, "dve", dvEe)
        b.waitv(g, "ldc", ldcE)
        b.waitv(g, "ld3", ld3E)
        b.inc(g.dma_start(out=t["W1G"][:, :], in_=t["w1e"]), "ldg", 16)
        b.inc(g.dma_start(out=t["W2S"][:, :], in_=t["w2e"]), "ldg", 16)
        b.inc(g.dma_start(out=t["growb"][0:1, :], in_=t["b1e"]), "ldg", 16)
        b.inc(g.dma_start(out=t["gbrow"][0:1, 0:E], in_=t["gateb"]), "ldg", 16)
        b.inc(g.dma_start(out=t["b2eb"][0:1, :], in_=t["b2e"]), "ldg", 16)
        b.inc(g.dma_start(out=t["gwg"][:, :], in_=t["gatew"]), "ldg", 16)
        b.inc(g.dma_start(out=t["eselS"][:, :], in_=t["esel"]), "ldg", 16)
    ldg_g = b.cnt["ldg"]

    pe_p0 = b.cnt["pe"]
    dv_p0 = b.cnt["dve"]
    with nc.psum_tensor("gbe", [1, 512], F32) as gbe, \
         nc.psum_tensor("gbc", [128, 128], F32) as gbc:
        @blk.vector
        def _(v):
            b.waitv(v, "ldg", ldg_g)
            b.inc(v.tensor_copy(t["b2c_bf"][:, 0:8], t["vecc"][:, 24:32]), "dve", 1)

        @blk.tensor
        def _(tt):
            b.waitv(tt, "ldg", ldg_g)
            b.waitv(tt, "dve", dv_p0 + 1)
            for fb in range(8):
                if fb >= 1:
                    b.waitv(tt, "dve", dv_p0 + 1 + fb)
                for k in range(8):
                    tt.matmul(gbe[:, :], t["b2c_bf"][:, k: k + 1],
                              t["W1G"][:, 4096 * k + 512 * fb: 4096 * k + 512 * (fb + 1)],
                              start=(k == 0), stop=False)
                mm = tt.matmul(gbe[:, :], t["ones_r"][0:1, 0:1], t["growb"][0:1, 512 * fb: 512 * (fb + 1)],
                               start=False, stop=True)
                b.inc(mm, "pe", 1)
            b.waitv(tt, "dve", dv_p0 + 9)
            for k in range(8):
                tt.matmul(gbe[:, 0:E], t["b2c_bf"][:, k: k + 1], t["gwg"][:, 8 * k: 8 * (k + 1)],
                          start=(k == 0), stop=False)
            mm = tt.matmul(gbe[:, 0:E], t["ones_r"][0:1, 0:1], t["gbrow"][0:1, 0:E],
                           start=False, stop=True)
            b.inc(mm, "pe", 1)
            for m in range(32):
                b.waitv(tt, "dve", dv_p0 + 11 + m)
                b.inc(tt.transpose(gbc[:, 0:1], t["b1effr"][0:1, 128 * m: 128 * (m + 1)],
                                   t["identf"][0:1, 0:1]), "pe", 1)

        @blk.vector
        def _(v):
            for fb in range(8):
                b.waitv(v, "pe", pe_p0 + fb + 1)
                b.inc(v.tensor_copy(t["b1effr"][0:1, 512 * fb: 512 * (fb + 1)], gbe[0:1, :]), "dve", 1)
            b.waitv(v, "pe", pe_p0 + 9)
            b.inc(v.tensor_copy(t["beffg"][0:1, 0:E], gbe[0:1, 0:E]), "dve", 1)
            for k in range(8):
                v.tensor_scalar(t["W1G"][:, 4096 * k: 4096 * (k + 1)], t["W1G"][:, 4096 * k: 4096 * (k + 1)],
                                t["vecc"][:, 16 + k: 17 + k], None, ALU.mult)
            i2 = None
            for k in range(8):
                i2 = v.tensor_scalar(t["gwg"][:, 8 * k: 8 * (k + 1)], t["gwg"][:, 8 * k: 8 * (k + 1)],
                                     t["vecc"][:, 16 + k: 17 + k], None, ALU.mult)
            b.inc(i2, "dve", 1)
            for m in range(32):
                b.waitv(v, "pe", pe_p0 + 10 + m)
                b.inc(v.tensor_copy(t["b1effc"][:, m: m + 1], gbc[:, 0:1]), "dve", 1)
    dv_pw = b.cnt["dve"]

    pe0 = b.cnt["pe"]
    pe2_0 = b.cnt["pe2"]
    pe3_0 = b.cnt["pe3"]
    ac0 = b.cnt["act"]
    dv0 = b.cnt["dve2"]
    dv3_0 = b.cnt["dve3"]
    dv4_0 = b.cnt["dve4"]
    lda0, ldb0 = b.cnt["lda"], b.cnt["ldb"]
    ldc0 = b.cnt["ldc"]
    cc0 = b.cnt["cc"]
    order = [(kp, cp) for kp in range(4) for cp in range(8)]

    with nc.psum_tensor("glg0", [128, 16], F32) as glg0, \
         nc.psum_tensor("glg1", [128, 16], F32) as glg1, \
         nc.psum_tensor("gph0", [128, 512], F32) as gph0, \
         nc.psum_tensor("gph1", [128, 512], F32) as gph1, \
         nc.psum_tensor("gpm0", [128, 512], F32) as gpm0, \
         nc.psum_tensor("gpm1", [128, 512], F32) as gpm1:
        glg = [glg0, glg1]
        gph = [gph0, gph1]
        gpm = [gpm0, gpm1]

        @blk.sync
        def _(s):
            b.waitv(s, "cc", ccE)
            for u, (kp, cp) in enumerate(order):
                TT = 4 * cp + kp
                rr, colb = TT // 4, TT % 4
                sem = "lda" if u % 2 == 0 else "ldb"
                if u >= 1:
                    b.waitv(s, "pe", pe0 + 32 * u)
                for k in range(8):
                    b.inc(s.dma_start(out=t["H2TB"][:, 512 * k: 512 * (k + 1)],
                                      in_=t["h2t_ag"][1024 * rr + 128 * k: 1024 * rr + 128 * (k + 1),
                                                      512 * colb: 512 * (colb + 1)]), sem, 16)
                if u >= 1:
                    pkp, pcp = order[u - 1]
                    pTT = 4 * pcp + pkp
                    b.waitv(s, "dve4", dv4_0 + 4 * u)
                    dst = t["rs_in"][4096 * pkp + 512 * pcp: 4096 * pkp + 512 * (pcp + 1), :]
                    b.inc(s.dma_start(out=dst.rearrange("(m p) f -> p m f", p=128),
                                      in_=t["MOEST"].rearrange("p (m f) -> p m f", f=1024)), "ldc", 16)
            pkp, pcp = order[-1]
            b.waitv(s, "dve4", dv4_0 + 4 * 32)
            dst = t["rs_in"][4096 * pkp + 512 * pcp: 4096 * pkp + 512 * (pcp + 1), :]
            b.inc(s.dma_start(out=dst.rearrange("(m p) f -> p m f", p=128),
                              in_=t["MOEST"].rearrange("p (m f) -> p m f", f=1024)), "ldc", 16)

        @blk.tensor
        def _(tt):
            b.waitv(tt, "dve", dv_pw)
            b.waitv(tt, "cc", ccE)
            for u, (kp, cp) in enumerate(order):
                sem = "lda" if u % 2 == 0 else "ldb"
                b.waitv(tt, sem, (lda0 if u % 2 == 0 else ldb0) + (u // 2 + 1) * 8 * 16)
                for j2 in range(4):
                    if 4 * u + j2 >= 2:
                        if j2 >= 2:
                            b.waitv(tt, "dve2", dv0 + 36 * u + 5 * (j2 - 2) + 1)
                        else:
                            b.waitv(tt, "dve2", dv0 + 36 * (u - 1) + 5 * (j2 + 2) + 1)
                    pl = glg[j2 % 2]
                    for k in range(8):
                        tt.matmul(pl[:, 0:E], t["H2TB"][:, 512 * k + 128 * j2: 512 * k + 128 * (j2 + 1)],
                                  t["gwg"][:, 8 * k: 8 * (k + 1)], start=(k == 0), stop=False)
                    mm = tt.matmul(pl[:, 0:E], t["ones_r"][0:1, 0:128], t["beffg"][0:1, 0:E],
                                   start=False, stop=True)
                    b.inc(mm, "pe3", 1)
                for m in range(32):
                    ph = gph[m % 2]
                    if 32 * u + m >= 2:
                        b.waitv(tt, "dve3", dv3_0 + 32 * u + m - 1)
                    for k in range(8):
                        mm = tt.matmul(ph[:, :], t["W1G"][:, 4096 * k + 128 * m: 4096 * k + 128 * (m + 1)],
                                       t["H2TB"][:, 512 * k: 512 * (k + 1)], start=(k == 0), stop=(k == 7))
                    b.inc(mm, "pe", 1)
                for j2 in range(4):
                    b.waitv(tt, "dve3", dv3_0 + 32 * (u + 1))
                    if 4 * u + j2 >= 1:
                        b.waitv(tt, "dve4", dv4_0 + 4 * u + j2)
                    b.waitv(tt, "dve2", dv0 + 36 * u + 20 + 4 * (j2 + 1))
                    for cb in range(2):
                        pm = gpm[cb]
                        for m in range(32):
                            tt.matmul(pm[:, :], t["HID"][:, 512 * m + 128 * j2: 512 * m + 128 * (j2 + 1)],
                                      t["W2S"][:, 1024 * m + 512 * cb: 1024 * m + 512 * (cb + 1)],
                                      start=(m == 0), stop=False)
                        mm = tt.matmul(pm[:, :], t["ones_r"][0:1, 0:128],
                                       t["b2eb"][0:1, 512 * cb: 512 * (cb + 1)], start=False, stop=True)
                    b.inc(mm, "pe2", 1)

        @blk.scalar
        def _(sc):
            for u in range(32):
                b.waitv(sc, "dve2", dv0 + 36 * u + 20)
                b.inc(sc.activation(t["gsc"][:, 44: 48], t["gsc"][:, 40: 44], AF.Exp), "act", 1)

        @blk.vector
        def _(v):
            for u, (kp, cp) in enumerate(order):
                for j2 in range(4):
                    gid = 4 * u + j2
                    b.waitv(v, "pe3", pe3_0 + gid + 1)
                    pl = glg[j2 % 2]
                    g8 = t["gsc"][:, 0:8]
                    b.fence(v, v.tensor_copy(g8, pl[:, 0:E]))
                    b.inc(v.tensor_copy(t["gsc"][:, 8:9], pl[:, 0:1]), "dve2", 1)
                    b.vred(v, "reduce_max", t["gsc"][:, 16 + j2: 17 + j2], g8, AX.X)
                    b.fence(v, v.tensor_scalar(t["gmk"][:, 8 * j2: 8 * j2 + 8], g8,
                                               t["gsc"][:, 16 + j2: 17 + j2], None, ALU.is_equal))
                    b.fence(v, v.scalar_tensor_tensor(t["gsc"][:, 24: 32], t["gmk"][:, 8 * j2: 8 * j2 + 8],
                                                      -1e30, g8, ALU.mult, ALU.add))
                    b.vred(v, "reduce_max", t["gsc"][:, 20 + j2: 21 + j2], t["gsc"][:, 24:32], AX.X)
                    b.fence(v, v.tensor_scalar(t["gmk"][:, 32 + 8 * j2: 40 + 8 * j2], t["gsc"][:, 24:32],
                                               t["gsc"][:, 20 + j2: 21 + j2], None, ALU.is_equal))
                    b.inc(v.tensor_tensor(t["gsc"][:, 40 + j2: 41 + j2], t["gsc"][:, 20 + j2: 21 + j2],
                                          t["gsc"][:, 16 + j2: 17 + j2], ALU.subtract), "dve2", 4)
                b.waitv(v, "act", ac0 + u + 1)
                for j2 in range(4):
                    gid = 4 * u + j2
                    b.fence(v, v.tensor_scalar(t["gsc"][:, 48 + j2: 49 + j2], t["gsc"][:, 44 + j2: 45 + j2],
                                               1.0, None, ALU.add))
                    b.fence(v, v.reciprocal(t["gsc"][:, 52 + j2: 53 + j2], t["gsc"][:, 48 + j2: 49 + j2]))
                    b.fence(v, v.tensor_tensor(t["gsc"][:, 56 + j2: 57 + j2], t["gsc"][:, 44 + j2: 45 + j2],
                                               t["gsc"][:, 52 + j2: 53 + j2], ALU.mult))
                    v.tensor_scalar(t["gsc"][:, 0:8], t["gmk"][:, 8 * j2: 8 * j2 + 8],
                                    t["gsc"][:, 52 + j2: 53 + j2], None, ALU.mult)
                    b.fence(v, v.tensor_scalar(t["gsc"][:, 8:16], t["gmk"][:, 32 + 8 * j2: 40 + 8 * j2],
                                               t["gsc"][:, 56 + j2: 57 + j2], None, ALU.mult))
                    b.fence(v, v.tensor_tensor(t["gsc"][:, 0:8], t["gsc"][:, 0:8], t["gsc"][:, 8:16], ALU.add))
                    b.fence(v, v.tensor_tensor(t["gsc"][:, 0:8], t["gsc"][:, 0:8], t["eselS"][:, :], ALU.mult))
                    b.vred(v, "reduce_sum", t["ewtt"][:, j2: j2 + 1], t["gsc"][:, 0:8], AX.X)
                    b.inc(v.tensor_copy(t["ewtt"][:, 4 + j2: 5 + j2], t["ewtt"][:, j2: j2 + 1]), "dve2", 4)
                for m in range(32):
                    b.waitv(v, "pe", pe0 + 32 * u + m + 1)
                    if m == 0 and u >= 1:
                        b.waitv(v, "pe2", pe2_0 + 4 * u)
                    ph = gph[m % 2]
                    b.inc(v.tensor_scalar(t["HID"][:, 512 * m: 512 * (m + 1)], ph[:, :],
                                          t["b1effc"][:, m: m + 1], 0.0, ALU.add, ALU.max), "dve3", 1)
                for j2 in range(4):
                    b.waitv(v, "pe2", pe2_0 + 4 * u + j2 + 1)
                    if u >= 1 and j2 == 0:
                        b.waitv(v, "ldc", ldc0 + u * 16)
                    for cb in range(2):
                        i6 = v.tensor_scalar(t["MOEST"][:, 1024 * j2 + 512 * cb: 1024 * j2 + 512 * (cb + 1)],
                                             gpm[cb][:, :], t["ewtt"][:, j2: j2 + 1], None, ALU.mult)
                    b.inc(i6, "dve4", 1)

    @blk.gpsimd
    def _(g):
        for kp in range(4):
            b.waitv(g, "ldc", ldc0 + (8 * kp + 8) * 16)
            b.inc(g.collective_compute("ReduceScatter", ALU.add, replica_groups=GROUPS8,
                                       ins=[t["rs_in"][4096 * kp: 4096 * (kp + 1), :].opt()],
                                       outs=[t["rs_out"][512 * kp: 512 * (kp + 1), :].opt()]), "cc", 1)


def phase_H(b, blk):
    nc, t = b.nc, b.t
    ccH = b.cnt["cc"]
    dv0 = b.cnt["dve"]
    dv3_0 = b.cnt["dve3"]
    ac0 = b.cnt["act2"]
    lda0, ldb0 = b.cnt["lda"], b.cnt["ldb"]
    gp0, gp20 = b.cnt["gp"], b.cnt["gp2"]
    ldv0 = b.cnt["ldv"]
    st0 = b.cnt["st"]

    @blk.sync
    def _(s):
        b.waitv(s, "cc", ccH)
        for j in range(NJ):
            sem = "lda" if j % 2 == 0 else "ldb"
            if j >= 2:
                b.waitv(s, "dve", dv0 + 2 * (j - 2) + 1)
            b.inc(s.dma_start(out=(t["xc"] if j % 2 == 0 else t["msc"])[:, :],
                              in_=t["u2_pk"][:, 1024 * j: 1024 * (j + 1)]), sem, 16)
            if j >= 1:
                b.waitv(s, "dve", dv0 + 2 * (j - 1) + 1)
                if j >= 2:
                    b.waitv(s, "st", st0 + 16 * (j - 1))
                b.inc(s.dma_start(out=t["moef"][:, :], in_=t["x1_pk"][:, 1024 * (j - 1): 1024 * j]), "ldv", 16)
                b.waitv(s, "dve", dv0 + 2 * (j - 1) + 2)
                b.inc(s.dma_start(out=t["y"][128 * (j - 1): 128 * j, :], in_=t["moef"][:, :]), "st", 16)
        b.waitv(s, "dve", dv0 + 2 * (NJ - 1) + 1)
        b.waitv(s, "st", st0 + 16 * (NJ - 1))
        b.inc(s.dma_start(out=t["moef"][:, :], in_=t["x1_pk"][:, 1024 * (NJ - 1): 1024 * NJ]), "ldv", 16)
        b.waitv(s, "dve", dv0 + 2 * NJ)
        b.inc(s.dma_start(out=t["y"][128 * (NJ - 1): 128 * NJ, :], in_=t["moef"][:, :]), "st", 16)

    @blk.gpsimd
    def _(g):
        b.waitv(g, "cc", ccH)
        for j in range(NJ):
            if j >= 2:
                b.waitv(g, "dve", dv0 + 2 * (j - 2) + 1)
            b.inc(g.dma_start(out=(t["hsq"] if j % 2 == 0 else t["dsq"])[:, :],
                              in_=t["rs_out"][128 * j: 128 * (j + 1), :]),
                  "gp" if j % 2 == 0 else "gp2", 16)

    @blk.vector
    def _(v):
        for j in range(NJ):
            xb = t["xc"] if j % 2 == 0 else t["msc"]
            mb = t["hsq"] if j % 2 == 0 else t["dsq"]
            sem = "lda" if j % 2 == 0 else "ldb"
            b.waitv(v, sem, (lda0 if j % 2 == 0 else ldb0) + (j // 2 + 1) * 16)
            sem2 = "gp" if j % 2 == 0 else "gp2"
            b.waitv(v, sem2, (gp0 if j % 2 == 0 else gp20) + (j // 2 + 1) * 16)
            b.inc(v.tensor_tensor(xb[:, :], xb[:, :], mb[:, :], ALU.add), "dve", 1)
            st_ = t["stat"]
            b.vred(v, "reduce_sum", st_[:, j: j + 1], xb[:, :], AX.X)
            b.fence(v, v.tensor_scalar(st_[:, 16 + j: 17 + j], st_[:, j: j + 1], float(INV_D), None, ALU.mult))
            b.fence(v, v.tensor_scalar(t["sq"][:, :], xb[:, :], st_[:, 16 + j: 17 + j], None, ALU.subtract))
            b.fence(v, v.tensor_tensor(t["csq"][:, :], t["sq"][:, :], t["sq"][:, :], ALU.mult))
            b.vred(v, "reduce_sum", st_[:, j: j + 1], t["csq"][:, :], AX.X)
            b.inc(v.tensor_scalar(st_[:, j: j + 1], st_[:, j: j + 1], float(INV_D), float(EPS),
                                  ALU.mult, ALU.add), "dve3", 1)
            b.waitv(v, "act2", ac0 + j + 1)
            b.fence(v, v.tensor_scalar(t["sq"][:, :], t["sq"][:, :], t["rstdb"][:, j: j + 1], None, ALU.mult))
            b.waitv(v, "ldv", ldv0 + 16 * (j + 1))
            b.inc(v.tensor_tensor(t["moef"][:, :], t["moef"][:, :], t["sq"][:, :], ALU.add), "dve", 1)

    @blk.scalar
    def _(sc):
        for j in range(NJ):
            b.waitv(sc, "dve3", dv3_0 + j + 1)
            sc.activation(t["stat"][:, 16 + j: 17 + j], t["stat"][:, j: j + 1], AF.Ln)
            b.inc(sc.activation(t["rstdb"][:, j: j + 1], t["stat"][:, 16 + j: 17 + j], AF.Exp,
                                scale=np.float32(-0.5)), "act2", 1)


def emit_av_e(b, tt, u, g, EOP, PBe, ac0):
    t = b.t
    gg = 64 * u + g
    b.waitv(tt, "act", ac0 + gg + 1)
    r2, tile = g // 16, g % 16
    col = 2048 * r2 + 128 * tile
    tt.matmul(EOP[0][:, :], t["VHA"][:, col: col + 128],
              PBe[gg % 2][:, 0:512], start=(g == 0), stop=(g == 63))
    mm = tt.matmul(EOP[1][:, :], t["VHB"][:, col: col + 128],
                   PBe[gg % 2][:, 512:1024], start=(g == 0), stop=(g == 63))
    b.inc(mm, "pe2", 1)


def phase_E_end(b, blk):
    nc, t = b.nc, b.t
    dv3_all = b.cnt["dve3"]

    @blk.gpsimd
    def _(g):
        b.waitv(g, "dve3", dv3_all)
        b.inc(g.dma_start(out=t["WO2"].rearrange("p (k f) -> p k f", f=1024),
                          in_=t["wo2"].rearrange("(k p) f -> p k f", p=128)), "ldg", 16)
        b.inc(g.dma_start(out=t["bob"][0:1, :], in_=t["bo"]), "ldg", 16)
    ldg_e = b.cnt["ldg"]

    pe0 = b.cnt["pe"]
    pe2_0 = b.cnt["pe2"]
    pe3_0 = b.cnt["pe3"]
    ac0 = b.cnt["act2"]
    dv0 = b.cnt["dve"]
    lda0, ldb0 = b.cnt["lda"], b.cnt["ldb"]
    ldc0 = b.cnt["ldc"]
    ld30 = b.cnt["ld3"]

    with nc.psum_tensor("etr2", [128, 1024], BF16) as etr2, \
         nc.psum_tensor("exo0", [128, 512], F32) as exo0, \
         nc.psum_tensor("exo1", [128, 512], F32) as exo1, \
         nc.psum_tensor("etr3", [128, 1024], BF16) as etr3:
        exo = [exo0, exo1]

        @blk.sync
        def _(s):
            b.waitv(s, "dve3", dv3_all)
            for j in range(NJ):
                sem = "lda" if j % 2 == 0 else "ldb"
                if j >= 2:
                    b.waitv(s, "dve", dv0 + 4 * (j - 2) + 2)
                b.inc(s.dma_start(out=(t["xc"] if j % 2 == 0 else t["msc"])[:, :],
                                  in_=t["u_pk"][:, 1024 * j: 1024 * (j + 1)]), sem, 16)
                if j >= 1:
                    b.waitv(s, "dve", dv0 + 4 * (j - 1) + 3)
                    b.inc(s.dma_start(out=t["u2_pk"][:, 1024 * (j - 1): 1024 * j],
                                      in_=(t["xc"] if (j - 1) % 2 == 0 else t["msc"])[:, :]), "ldc", 16)
            b.waitv(s, "dve", dv0 + 4 * (NJ - 1) + 3)
            b.inc(s.dma_start(out=t["u2_pk"][:, 1024 * (NJ - 1): 1024 * NJ],
                              in_=(t["xc"] if (NJ - 1) % 2 == 0 else t["msc"])[:, :]), "ldc", 16)
            b.waitv(s, "dve", dv0 + 4 * NJ)
            for k in range(8):
                b.inc(s.dma_start(out=t["h2t_in"][128 * k: 128 * (k + 1), :],
                                  in_=t["H2TS"][:, 2048 * k: 2048 * (k + 1)]), "ld3", 16)

        @blk.tensor
        def _(tt):
            b.waitv(tt, "ldg", ldg_e)
            for j in range(NJ):
                if j >= 1:
                    b.waitv(tt, "dve", dv0 + 4 * (j - 1) + 1)
                for k in range(8):
                    mm = tt.transpose(etr2[:, 128 * k: 128 * (k + 1)],
                                      t["O2N"][:, 1024 * j + 128 * k: 1024 * j + 128 * (k + 1)],
                                      t["identb"][:, :])
                b.inc(mm, "pe", 1)
                b.waitv(tt, "dve", dv0 + 4 * j + 1)
                if j >= 1:
                    b.waitv(tt, "dve", dv0 + 4 * (j - 1) + 2)
                for cb in range(2):
                    pm = exo[cb]
                    for k in range(8):
                        tt.matmul(pm[:, :], t["o2lh"][:, 128 * k: 128 * (k + 1)],
                                  t["WO2"][:, 1024 * k + 512 * cb: 1024 * k + 512 * (cb + 1)],
                                  start=(k == 0), stop=False)
                    mm = tt.matmul(pm[:, :], t["ones_r"][0:1, 0:128], t["bob"][0:1, 512 * cb: 512 * (cb + 1)],
                                   start=False, stop=True)
                b.inc(mm, "pe2", 1)
                b.waitv(tt, "dve", dv0 + 4 * j + 3)
                for k in range(8):
                    mm = tt.transpose(etr3[:, 128 * k: 128 * (k + 1)], t["n2e"][:, 128 * k: 128 * (k + 1)],
                                      t["identb"][:, :])
                b.inc(mm, "pe3", 1)

        @blk.vector
        def _(v):
            for j in range(NJ):
                xb = t["xc"] if j % 2 == 0 else t["msc"]
                sem = "lda" if j % 2 == 0 else "ldb"
                b.waitv(v, sem, (lda0 if j % 2 == 0 else ldb0) + (j // 2 + 1) * 16)
                b.waitv(v, "pe", pe0 + j + 1)
                if j >= 1:
                    b.waitv(v, "pe2", pe2_0 + j)
                b.inc(v.tensor_copy(t["o2lh"][:, :], etr2[:, :]), "dve", 1)
                b.waitv(v, "pe2", pe2_0 + j + 1)
                v.tensor_tensor(xb[:, 0:512], xb[:, 0:512], exo[0][:, :], ALU.add)
                b.fence(v, v.tensor_tensor(xb[:, 512:1024], xb[:, 512:1024], exo[1][:, :], ALU.add))
                b.inc(v.tensor_copy(t["gsc"][:, 1:2], xb[:, 1023:1024]), "dve", 1)
                st_ = t["stat"]
                b.vred(v, "reduce_sum", st_[:, 4:5], xb[:, :], AX.X)
                b.fence(v, v.tensor_scalar(st_[:, 5:6], st_[:, 4:5], float(INV_D), None, ALU.mult))
                b.fence(v, v.tensor_scalar(t["sq"][:, :], xb[:, :], st_[:, 5:6], None, ALU.subtract))
                b.fence(v, v.tensor_tensor(t["esq"][:, :], t["sq"][:, :], t["sq"][:, :], ALU.mult))
                b.vred(v, "reduce_sum", st_[:, 4:5], t["esq"][:, :], AX.X)
                b.inc(v.tensor_scalar(st_[:, 4:5], st_[:, 4:5], float(INV_D), float(EPS),
                                      ALU.mult, ALU.add), "dve3", 1)
                b.waitv(v, "act2", ac0 + j + 1)
                if j >= 1:
                    b.waitv(v, "pe3", pe3_0 + j)
                b.inc(v.tensor_scalar(t["n2e"][:, :], t["sq"][:, :], t["rstdb"][:, 2:3], None, ALU.mult),
                      "dve", 1)
                b.waitv(v, "pe3", pe3_0 + j + 1)
                for k in range(8):
                    i5 = v.tensor_copy(t["H2TS"][:, 2048 * k + 128 * j: 2048 * k + 128 * (j + 1)],
                                       etr3[:, 128 * k: 128 * (k + 1)])
                b.inc(i5, "dve", 1)

        @blk.scalar
        def _(sc):
            dv3e = b.cnt["dve3"] - NJ
            for j in range(NJ):
                b.waitv(sc, "dve3", dv3e + j + 1)
                sc.activation(t["stat"][:, 5:6], t["stat"][:, 4:5], AF.Ln)
                b.inc(sc.activation(t["rstdb"][:, 2:3], t["stat"][:, 5:6], AF.Exp, scale=np.float32(-0.5)),
                      "act2", 1)

    @blk.gpsimd
    def _(g):
        b.wait(g, "ld3")
        b.inc(g.collective_compute("AllGather", ALU.bypass, replica_groups=GROUPS8,
                                   ins=[t["h2t_in"].opt()], outs=[t["h2t_ag"].opt()]), "cc", 1)


def emit_av(b, tt, units, u, g, bop, PB, ac0, dv0, gg_of_unit, mask_cnt_before):
    t = b.t
    bt, i, qb, nch, groups = units[u]
    base = gg_of_unit[u]
    gg = base + g
    grp = groups[g]
    b.waitv(tt, "act", ac0 + gg + 1)
    nmask_before_and_in = mask_cnt_before[u] + sum(
        1 for gp in range(g + 1) for n in groups[gp]
        if (n - 4 * qb) >= 0 and (n - 4 * qb) < 4)
    if nmask_before_and_in > 0:
        b.waitv(tt, "dve", b.cnt_dv0_B + nmask_before_and_in)
    for k, n in enumerate(grp):
        tile = 64 * bt + n
        mm = tt.matmul(bop[:, :], t["VP"][:, 130 * tile + 65 * i: 130 * tile + 65 * (i + 1)],
                       PB[gg % 2][:, 512 * k: 512 * (k + 1)],
                       start=(grp[0] == 0 and k == 0), stop=(n == nch - 1))
    b.inc(mm, "pe2", 1)


def kernel(**inputs):
    in_maps = make_in_maps(inputs)
    nc, _ = build_nc(debug=False)
    res = run_bass_kernel_spmd(nc, in_maps, core_ids=list(range(8)))
    return np.concatenate([res.results[cid]["y"] for cid in range(8)], 0).reshape(B, S, D)


# version 15
# speedup vs baseline: 1.0463x; 1.0006x over previous
import contextlib
import numpy as np
import ml_dtypes
import concourse.bass as bass
import concourse.mybir as mybir
from concourse.bass_utils import run_bass_kernel_spmd

F32 = mybir.dt.float32
BF16 = mybir.dt.bfloat16
AF = mybir.ActivationFunctionType
ALU = mybir.AluOpType
AX = mybir.AxisListType

B, S, D, H, DK, E, DFF = 2, 8192, 1024, 16, 64, 8, 4096
T = B * S
TLOC = T // 8
NJ = TLOC // 128
C0 = np.float32(-12.0)
INV8 = np.float32(0.125)
EPS = np.float32(1e-5)
INV_D = np.float32(1.0 / D)

GROUPS4 = [[0, 1, 2, 3], [4, 5, 6, 7]]
GROUPS8 = [list(range(8))]


def make_in_maps(inputs):
    x = np.asarray(inputs["L0_attn_x"], np.float32).reshape(T, D)
    Wqkv1 = np.asarray(inputs["L0_attn_W_qkv"], np.float32)
    Wo1 = np.asarray(inputs["L0_attn_W_o"], np.float32)
    perm = np.concatenate([np.arange(0, 64, 2), np.arange(1, 64, 2)])

    invf = (10000.0 ** (-np.arange(0, DK, 2, dtype=np.float32) / DK)).astype(np.float32)
    ang = np.arange(S, dtype=np.float32)[:, None] * invf[None, :]
    ropeC = np.concatenate([np.cos(ang), np.cos(ang)], 1).astype(np.float32)
    ropeS = np.concatenate([np.sin(ang), np.sin(ang)], 1).astype(np.float32)
    ropeC = np.ascontiguousarray(ropeC.reshape(64, 128, 64).transpose(1, 0, 2).reshape(128, 4096))
    ropeS = np.ascontiguousarray(ropeS.reshape(64, 128, 64).transpose(1, 0, 2).reshape(128, 4096))

    kvv = np.arange(128)[:, None]
    qv = np.arange(512)[None, :]
    maskc = np.concatenate([(128 * j + kvv <= qv) for j in range(4)], 1).astype(ml_dtypes.bfloat16)
    maskc = np.ascontiguousarray(maskc)

    common = {
        "ropeC": ropeC, "ropeS": ropeS, "maskc": maskc,

        "wqkv2": np.ascontiguousarray(np.asarray(inputs["L0_ffn_w_qkv"], np.float32)
                                      .reshape(8, 128, 3 * D).transpose(1, 0, 2).reshape(128, 24 * D)),
        "bqkv": np.asarray(inputs["L0_ffn_b_qkv"], np.float32).reshape(1, 3 * D),
        "wo2": np.ascontiguousarray(np.asarray(inputs["L0_ffn_w_o"], np.float32)),
        "bo": np.asarray(inputs["L0_ffn_b_o"], np.float32).reshape(1, D),
        "g1": np.asarray(inputs["L0_ffn_ln1_g"], np.float32).reshape(1, D),
        "b1ln": np.asarray(inputs["L0_ffn_ln1_b"], np.float32).reshape(1, D),
        "g2": np.asarray(inputs["L0_ffn_ln2_g"], np.float32).reshape(1, D),
        "b2ln": np.asarray(inputs["L0_ffn_ln2_b"], np.float32).reshape(1, D),
        "gatew": np.ascontiguousarray(np.asarray(inputs["L0_ffn_gate_w"], np.float32)
                                      .reshape(8, 128, E).transpose(1, 0, 2).reshape(128, 8 * E)),
        "gateb": np.asarray(inputs["L0_ffn_gate_b"], np.float32).reshape(1, E),
    }
    W1 = np.asarray(inputs["L0_ffn_W1"], np.float32)
    b1 = np.asarray(inputs["L0_ffn_b1"], np.float32)
    W2 = np.asarray(inputs["L0_ffn_W2"], np.float32)
    b2 = np.asarray(inputs["L0_ffn_b2"], np.float32)

    in_maps = []
    for c in range(8):
        h0 = 2 * c
        rows = []
        for blk in range(2):
            for h in (h0, h0 + 1):
                rows.append(blk * D + h * DK + perm)
        for h in (h0, h0 + 1):
            rows.append(2 * D + h * DK + np.arange(DK))
        rows = np.concatenate(rows)
        m = dict(common)
        xm = x[TLOC * c: TLOC * (c + 1)].reshape(NJ, 128, D).transpose(1, 0, 2)
        m["x_my"] = np.ascontiguousarray(xm.reshape(128, NJ * D))
        m["wo1c"] = np.ascontiguousarray(Wo1.T[128 * c: 128 * (c + 1)])
        wt = Wqkv1[rows].T.reshape(8, 128, 384).transpose(1, 0, 2)
        m["wqkv1T"] = np.ascontiguousarray(wt.reshape(128, 8 * 384))
        m["w1e"] = np.ascontiguousarray(W1[c].reshape(8, 128, DFF).transpose(1, 0, 2).reshape(128, 8 * DFF))
        m["b1e"] = b1[c].reshape(1, DFF)
        m["w2e"] = np.ascontiguousarray(W2[c].reshape(32, 128, D).transpose(1, 0, 2).reshape(128, 32 * D))
        m["b2e"] = b2[c].reshape(1, D)
        sel = np.zeros((128, E), np.float32)
        sel[:, c] = 1.0
        m["esel"] = sel
        in_maps.append(m)
    return in_maps


class Bld:
    def __init__(self, nc, debug):
        self.nc = nc
        self.debug = debug
        self.sem = {}
        self.cnt = {}
        self.t = {}
        self.stack = contextlib.ExitStack()
        self.dbg = {}

    def mksem(self, *names):
        for n in names:
            self.sem[n] = self.stack.enter_context(self.nc.semaphore(n))
            self.cnt[n] = 0

    def inc(self, inst, name, n=1):
        inst.then_inc(self.sem[name], n)
        self.cnt[name] += n
        return inst

    def wait(self, eng, name, val=None):
        eng.wait_ge(self.sem[name], self.cnt[name] if val is None else val)

    def waitv(self, eng, name, val):
        eng.wait_ge(self.sem[name], val)

    def fence(self, v, inst):
        self.inc(inst, "dvs", 1)
        self.wait(v, "dvs")
        return inst

    def vred(self, v, op, out, in_, axis):
        inst = getattr(v, op)(out, in_, axis)
        self.inc(inst, "dvs", 1)
        self.wait(v, "dvs")
        return inst


def build_nc(debug=False):
    nc = bass.Bass("TRN2", target_bir_lowering=False, detect_race_conditions=False)
    b = Bld(nc, debug)
    t = b.t

    def din(name, shape, dt=F32):
        t[name] = nc.dram_tensor(name, list(shape), dt, kind="ExternalInput").ap()

    def dint(name, shape, dt=BF16):
        t[name] = nc.dram_tensor(name, list(shape), dt, kind="Internal").ap()

    for name, shape, dt in [
        ("x_my", (128, NJ * D), F32), ("wqkv1T", (128, 8 * 384), F32),
        ("ropeC", (128, 4096), F32), ("ropeS", (128, 4096), F32), ("maskc", (128, 2048), BF16),
        ("wo1c", (128, D), F32), ("wqkv2", (128, 24 * D), F32), ("bqkv", (1, 3 * D), F32),
        ("wo2", (D, D), F32), ("bo", (1, D), F32),
        ("g1", (1, D), F32), ("b1ln", (1, D), F32), ("g2", (1, D), F32), ("b2ln", (1, D), F32),
        ("gatew", (128, 8 * E), F32), ("esel", (128, E), F32), ("gateb", (1, E), F32),
        ("w1e", (128, 8 * DFF), F32), ("b1e", (1, DFF), F32), ("w2e", (128, 32 * D), F32), ("b2e", (1, D), F32),
    ]:
        din(name, shape, dt)

    t["y"] = nc.dram_tensor("y", [TLOC, D], F32, kind="ExternalOutput").ap()

    for name, shape, dt in [
        ("n1t_in", (D, TLOC), BF16), ("n1t_ag", (8 * D, TLOC), BF16),
        ("og_in", (T, 128), BF16),
        ("xo_rs_in", (T, D), BF16), ("xo_rs_out", (TLOC, D), BF16),
        ("k2t_in", (D, TLOC), BF16), ("k2t_ag", (4 * D, TLOC), BF16),
        ("v2h_in", (H * TLOC, 128), BF16), ("v2h_ag", (4 * H * TLOC, 128), BF16),
        ("h2t_in", (D, TLOC), BF16), ("h2t_ag", (8 * D, TLOC), BF16),
        ("rs_in", (T, D), BF16), ("rs_out", (TLOC, D), BF16),
        ("x1_pk", (128, NJ * D), F32), ("u_pk", (128, NJ * D), F32), ("u2_pk", (128, NJ * D), F32),
    ]:
        dint(name, shape, dt)

    t["identf_d"] = nc.inline_tensor(np.eye(128, dtype=np.float32), "identf").ap()
    t["identb_d"] = nc.inline_tensor(np.eye(128, dtype=ml_dtypes.bfloat16), "identb").ap()

    if debug:
        def dout(name, shape, dt):
            b.dbg[name] = nc.dram_tensor("dbg_" + name, list(shape), dt, kind="ExternalOutput").ap()
        dout("og", (T, 128), BF16)
        dout("x1", (128, NJ * D), F32)
        dout("xo", (TLOC, D), BF16)
        dout("u", (128, NJ * D), F32)
        dout("u2", (128, NJ * D), F32)
        dout("h2t", (8 * D, TLOC), BF16)
        dout("rso", (TLOC, D), BF16)
        dout("ews", (128, 8), F32)
        dout("rsin", (T, D), BF16)
        dout("h2tin", (D, TLOC), BF16)
        dout("k2t", (4 * D, TLOC), BF16)
        dout("v2h", (4 * H * TLOC, 128), BF16)


    MB = 22496
    R1o, SCRo, CSTo, BIGo = MB, MB + 65536, MB + 77824, MB + 90112
    BIG_END = BIGo + 116736
    assert BIG_END <= 229344

    R1B = nc.alloc_sbuf_tensor_at("R1B", [128, 32768], BF16, offset=R1o).ap()
    SCRB = nc.alloc_sbuf_tensor_at("SCRB", [128, 6144], BF16, offset=SCRo).ap()
    CSTB = nc.alloc_sbuf_tensor_at("CSTB", [128, 6144], BF16, offset=CSTo).ap()
    BIGB = nc.alloc_sbuf_tensor_at("BIGB", [128, 58368], BF16, offset=BIGo).ap()

    def sb(name, shape, dt, off):
        for base, bo, bcols in ((R1B, R1o, 32768), (SCRB, SCRo, 6144), (CSTB, CSTo, 6144),
                                (BIGB, BIGo, 58368)):
            rel = off - bo
            if 0 <= rel < bcols * 2:
                break
        else:
            raise AssertionError(name)
        nb = int(np.prod(shape[1:])) * (4 if dt == F32 else 2)
        assert rel % 2 == 0 and rel + nb <= bcols * 2, (name, off)
        v = base[:, rel // 2: (rel + nb) // 2]
        if dt == F32:
            v = v.bitcast(F32)
        if shape[0] != 128:
            v = v[0: shape[0]]
        if len(shape) != 2:
            raise AssertionError(name)
        assert v.shape == tuple(shape), (name, v.shape, shape)
        t[name] = v

    sb("r1f", (128, 16384), F32, R1o)
    sb("ropeCt", (128, 4096), F32, R1o)
    sb("ropeSt", (128, 4096), F32, R1o + 16384)
    sb("slab0", (128, 8192), BF16, R1o + 32768)
    sb("slab1", (128, 8192), BF16, R1o + 49152)
    sb("Q2T", (128, 16384), BF16, R1o)
    sb("K2TS", (128, 16384), BF16, R1o + 32768)
    sb("W1G", (128, 32768), BF16, R1o)

    sb("xc", (128, 1024), F32, SCRo)
    sb("sq", (128, 1024), F32, SCRo + 4096)
    sb("msc", (128, 1024), F32, SCRo + 8192)
    sb("bfs", (128, 2048), BF16, SCRo + 4096)
    sb("mscb", (128, 2048), BF16, SCRo + 8192)
    sb("rowf0", (1, 2048), F32, SCRo)

    o = CSTo
    def cst(name, shape, dt, size):
        nonlocal o
        sb(name, shape, dt, o)
        o += size
    cst("identf", (128, 128), F32, 512)
    cst("identb", (128, 128), BF16, 256)
    cst("ones_r", (1, 128), BF16, 256)
    cst("vecc", (128, 32), F32, 128)
    cst("rstdb", (128, 16), F32, 64)
    cst("stat", (128, 32), F32, 128)
    cst("gwg", (128, 64), BF16, 128)
    cst("b1effc", (128, 32), F32, 128)
    cst("ewtt", (128, 32), F32, 128)
    cst("gsc", (128, 64), F32, 256)
    cst("gmk", (128, 64), BF16, 128)
    cst("lrec", (128, 8), F32, 32)
    cst("eselS", (128, 8), F32, 32)
    cst("beffq", (1, 3 * D), BF16, 6144)
    cst("beffg", (1, E), BF16, 64)
    assert o <= BIGo, o

    sb("xmy", (128, 16384), F32, BIGo)
    sb("n1tS", (128, 16384), BF16, BIGo + 65536)
    sb("QT", (128, 16384), BF16, BIGo)
    sb("KT", (128, 16384), BF16, BIGo + 32768)
    sb("VP", (128, 16640), BF16, BIGo + 65536)
    sb("WQ1", (128, 3072), BF16, BIGo + 98816)
    sb("PB0", (128, 1536), BF16, BIGo + 104960)
    sb("PB1", (128, 1536), BF16, BIGo + 108032)
    sb("obst", (128, 512), BF16, BIGo + 111104)
    sb("masks", (128, 2048), BF16, BIGo + 112128)

    sb("WO1", (128, 1024), BF16, BIGo)
    sb("oglh", (128, 128), BF16, BIGo + 2048)
    sb("ogld0", (128, 128), BF16, BIGo + 2304)
    sb("ogld1", (128, 128), BF16, BIGo + 2560)
    sb("cstg0", (128, 4096), BF16, BIGo + 4096)
    sb("cstg1", (128, 4096), BF16, BIGo + 12288)
    sb("csq", (128, 1024), F32, BIGo + 20480)

    sb("WQ2", (128, 24576), BF16, BIGo)
    sb("V2ST", (128, 2048), BF16, BIGo + 49152)
    sb("brow", (1, 4096), BF16, BIGo + 53248)
    sb("dsq", (128, 1024), F32, BIGo + 61440)
    sb("n1ib", (128, 1024), BF16, BIGo + 65536)
    sb("n1it", (128, 1024), BF16, BIGo + 67584)
    sb("qsb", (128, 512), BF16, BIGo + 69632)
    sb("csq2", (128, 1024), F32, BIGo + 70656)
    sb("b1c_bf", (128, 8), BF16, BIGo + 74752)

    sb("KB", (128, 8192), BF16, BIGo)
    sb("VHA", (128, 8192), BF16, BIGo + 16384)
    sb("VHB", (128, 8192), BF16, BIGo + 32768)
    sb("PB0e", (128, 1536), BF16, BIGo + 49152)
    sb("PB1e", (128, 1536), BF16, BIGo + 52224)
    sb("O2N", (128, 16384), BF16, BIGo + 57344)
    sb("WO2", (128, 8192), BF16, BIGo)
    sb("H2TS", (128, 16384), BF16, BIGo + 16384)
    sb("esq", (128, 1024), F32, BIGo + 49152)
    sb("o2lh", (128, 1024), BF16, BIGo + 53248)
    sb("bob", (1, D), BF16, BIGo + 55296)
    sb("n2e", (128, 1024), BF16, BIGo + 90112)

    sb("W2S", (128, 32768), BF16, BIGo)
    sb("HID", (128, 16384), BF16, BIGo + 65536)
    sb("H2TB", (128, 4096), BF16, BIGo + 98304)
    sb("MOEST", (128, 4096), BF16, BIGo + 106496)
    sb("b2eb", (1, D), BF16, BIGo + 114688)
    sb("b1effr", (1, DFF), F32, BIGo + 65536)
    sb("growb", (1, DFF), BF16, BIGo + 81920)
    sb("gbrow", (1, 64), BF16, BIGo + 92160)
    sb("b2c_bf", (128, 8), BF16, BIGo + 92288)
    sb("hsq", (128, 1024), F32, BIGo + 98304)
    sb("moef", (128, 1024), F32, BIGo + 102400)

    b.mksem("ld", "ldg", "ld2", "ld3", "ldv", "ldx", "lda", "ldb", "ldc", "cc", "dvs", "gp2", "dbg",
            "pe", "pe2", "pe3",
            "act", "act2",
            "dve", "dve2", "dve3", "dve4",
            "gp", "st")

    with b.stack, nc.Block() as blk:
        phase_P0(b, blk)
        phase_A0(b, blk)
        phase_A(b, blk)
        phase_B(b, blk)
        phase_C(b, blk)
        phase_D(b, blk)
        phase_E(b, blk)
        phase_G(b, blk)
        phase_H(b, blk)
        if debug:
            @blk.gpsimd
            def _(g):
                b.wait(g, "ldx")
                b.inc(g.dma_start(out=b.dbg["og"], in_=t["og_in"]), "dbg", 16)
                b.inc(g.dma_start(out=b.dbg["x1"], in_=t["x1_pk"]), "dbg", 16)
                b.inc(g.dma_start(out=b.dbg["xo"], in_=t["xo_rs_out"]), "dbg", 16)
                b.wait(g, "cc")
                b.wait(g, "ldc")
                b.inc(g.dma_start(out=b.dbg["u"], in_=t["u_pk"]), "dbg", 16)
                b.inc(g.dma_start(out=b.dbg["k2t"], in_=t["k2t_ag"]), "dbg", 16)
                b.inc(g.dma_start(out=b.dbg["v2h"], in_=t["v2h_ag"]), "dbg", 16)
                b.inc(g.dma_start(out=b.dbg["u2"], in_=t["u2_pk"]), "dbg", 16)
                b.inc(g.dma_start(out=b.dbg["h2t"], in_=t["h2t_ag"]), "dbg", 16)
                b.inc(g.dma_start(out=b.dbg["h2tin"], in_=t["h2t_in"]), "dbg", 16)
                b.inc(g.dma_start(out=b.dbg["rso"], in_=t["rs_out"]), "dbg", 16)
                b.inc(g.dma_start(out=b.dbg["ews"], in_=t["ewtt"][:, 0:8]), "dbg", 16)
                b.inc(g.dma_start(out=b.dbg["rsin"], in_=t["rs_in"]), "dbg", 16)
        final_wait(b, blk)
    return nc, b


def dump_debug(b, blk, items, wait):
    @blk.gpsimd
    def _(g):
        for s, v in wait:
            b.wait(g, s, v)
        for name, ap in items:
            b.inc(g.dma_start(out=b.dbg[name], in_=ap[:, :]), "dbg", 16)


def final_wait(b, blk):
    @blk.gpsimd
    def _(g):
        b.wait(g, "st")
        b.wait(g, "dbg")


def phase_P0(b, blk):
    nc, t = b.nc, b.t

    @blk.sync
    def _(s):
        b.inc(s.dma_start(out=t["identf"][:, :], in_=t["identf_d"]), "ld", 16)
        b.inc(s.dma_start(out=t["identb"][:, :], in_=t["identb_d"]), "ld", 16)
        b.inc(s.dma_start(out=t["masks"][:, :], in_=t["maskc"]), "ld", 16)
        b.inc(s.dma_start(out=t["ropeCt"][:, :], in_=t["ropeC"]), "ld", 16)
        b.inc(s.dma_start(out=t["ropeSt"][:, :], in_=t["ropeS"]), "ld", 16)
        b.inc(s.dma_start(out=t["rowf0"][0:1, 0:1024], in_=t["g1"]), "ld", 16)
        b.inc(s.dma_start(out=t["rowf0"][0:1, 1024:2048], in_=t["b1ln"]), "ld", 16)

    @blk.gpsimd
    def _(g):
        b.inc(g.dma_start(out=t["WQ1"][:, :], in_=t["wqkv1T"]), "ldg", 16)
        g.memset(t["ones_r"][:, :], 1.0)

    pe0, dv0 = b.cnt["pe"], b.cnt["dve"]
    ld_p1 = b.cnt["ld"]

    @blk.sync
    def _(s):
        b.waitv(s, "pe", pe0 + 16)
        b.inc(s.dma_start(out=t["rowf0"][0:1, 0:1024], in_=t["g2"]), "ldv", 16)
        b.inc(s.dma_start(out=t["rowf0"][0:1, 1024:2048], in_=t["b2ln"]), "ldv", 16)
    ld_p2 = b.cnt["ldv"]

    with nc.psum_tensor("pp0", [128, 128], F32) as pp0:
        @blk.tensor
        def _(tt):
            for i in range(32):
                v, k = i // 8, i % 8
                col = 1024 * (v % 2) + 128 * k
                if i < 16:
                    b.waitv(tt, "ld", ld_p1)
                else:
                    b.waitv(tt, "ldv", ld_p2)
                b.waitv(tt, "dve", dv0 + i)
                b.inc(tt.transpose(pp0[:, 0:1], t["rowf0"][0:1, col: col + 128], t["identf"][0:1, 0:1]), "pe", 1)

        @blk.vector
        def _(v):
            for i in range(32):
                b.waitv(v, "pe", pe0 + i + 1)
                b.inc(v.tensor_copy(t["vecc"][:, i: i + 1], pp0[:, 0:1]), "dve", 1)


def phase_A0(b, blk):
    nc, t = b.nc, b.t

    @blk.sync
    def _(s):
        b.inc(s.dma_start(out=t["xmy"][:, :], in_=t["x_my"]), "ldx", 16)

    pe0, dv0, ac0 = b.cnt["pe"], b.cnt["dve"], b.cnt["act"]
    pa0_cm = nc.psum_tensor("pa0", [128, 2048], BF16)
    pa0 = pa0_cm.__enter__()

    @blk.vector
    def _(v):
        b.wait(v, "ld")
        b.wait(v, "ldx")
        for j in range(NJ):
            xj = t["xmy"][:, 1024 * j: 1024 * (j + 1)]
            b.vred(v, "reduce_sum", t["stat"][:, j: j + 1], xj, AX.X)
            b.fence(v, v.tensor_scalar(t["stat"][:, 16 + j: 16 + j + 1], t["stat"][:, j: j + 1],
                                       float(INV_D), None, ALU.mult))
            b.fence(v, v.tensor_scalar(t["xc"][:, :], xj, t["stat"][:, 16 + j: 16 + j + 1], None, ALU.subtract))
            b.fence(v, v.tensor_tensor(t["sq"][:, :], t["xc"][:, :], t["xc"][:, :], ALU.mult))
            b.vred(v, "reduce_sum", t["stat"][:, j: j + 1], t["sq"][:, :], AX.X)
            b.inc(v.tensor_scalar(t["stat"][:, j: j + 1], t["stat"][:, j: j + 1], float(INV_D), float(EPS),
                                  ALU.mult, ALU.add), "dve", 1)
            b.waitv(v, "act", ac0 + j + 1)
            b.waitv(v, "pe", pe0 + 8 * j)
            b.inc(v.tensor_scalar(t["bfs"][:, 0:1024], t["xc"][:, :], t["rstdb"][:, j: j + 1], None, ALU.mult),
                  "dve", 1)
            b.waitv(v, "pe", pe0 + 8 * (j + 1))
            for k in range(8):
                b.inc(v.tensor_copy(t["n1tS"][:, 2048 * k + 128 * j: 2048 * k + 128 * (j + 1)],
                                    pa0[:, 128 * k: 128 * (k + 1)]), "dve", 1)

    @blk.scalar
    def _(sc):
        for j in range(NJ):
            b.waitv(sc, "dve", dv0 + j * 10 + 1)
            sc.activation(t["stat"][:, 16 + j: 16 + j + 1], t["stat"][:, j: j + 1], AF.Ln)
            b.inc(sc.activation(t["rstdb"][:, j: j + 1], t["stat"][:, 16 + j: 16 + j + 1], AF.Exp,
                                scale=np.float32(-0.5)), "act", 1)

    @blk.tensor
    def _(tt):
        for j in range(NJ):
            b.waitv(tt, "dve", dv0 + j * 10 + 2)
            if j >= 1:
                b.waitv(tt, "dve", dv0 + j * 10)
            for k in range(8):
                b.inc(tt.transpose(pa0[:, 128 * k: 128 * (k + 1)],
                                   t["bfs"][:, 128 * k: 128 * (k + 1)], t["identb"][:, :]), "pe", 1)

    pa0_cm.__exit__(None, None, None)

    @blk.sync
    def _(s):
        b.wait(s, "dve")
        for k in range(8):
            b.inc(s.dma_start(out=t["n1t_in"][128 * k: 128 * (k + 1), :],
                              in_=t["n1tS"][:, 2048 * k: 2048 * (k + 1)]), "ld2", 16)

    @blk.gpsimd
    def _(g):
        b.wait(g, "ld2")
        b.inc(g.collective_compute("AllGather", ALU.bypass, replica_groups=GROUPS8,
                                   ins=[t["n1t_in"].opt()], outs=[t["n1t_ag"].opt()]), "cc", 1)


def phase_A(b, blk):
    nc, t = b.nc, b.t
    slabs = [t["slab0"], t["slab1"]]
    NU = 128

    cc_need = b.cnt["cc"]
    ld2_0, ld3_0 = b.cnt["ld2"], b.cnt["ld3"]
    pe0, pe2_0 = b.cnt["pe"], b.cnt["pe2"]
    dv2_0, dv3_0, dv4_0 = b.cnt["dve2"], b.cnt["dve3"], b.cnt["dve4"]

    @blk.vector
    def _(v):
        b.wait(v, "ld2")
        v.memset(t["VP"][:, :], 0.0)
        b.inc(v.memset(t["VP"].rearrange("p (n f) -> p n f", f=65)[:, :, 64:65], 1.0), "dve2", 1)

    @blk.sync
    def _(s):
        b.waitv(s, "cc", cc_need)
        for g in range(16):
            r, half = g // 2, g % 2
            if g >= 2:
                b.waitv(s, "pe", pe0 + (g - 1) * 8)
            sem = "ld2" if g % 2 == 0 else "ld3"
            for k in range(8):
                b.inc(s.dma_start(out=slabs[g % 2][:, 1024 * k: 1024 * (k + 1)],
                                  in_=t["n1t_ag"][1024 * r + 128 * k: 1024 * r + 128 * (k + 1),
                                                  1024 * half: 1024 * (half + 1)]), sem, 16)

    pcms = [nc.psum_tensor("paq0", [128, 384], F32), nc.psum_tensor("paq1", [128, 384], F32),
            nc.psum_tensor("pat0", [128, 256], BF16), nc.psum_tensor("pat1", [128, 256], BF16)]
    hs = [cm.__enter__() for cm in pcms]
    pq, pt = hs[0:2], hs[2:4]

    def emit_transposes(tt, up):
        b.waitv(tt, "dve3", dv3_0 + up + 1)
        if up >= 2:
            b.waitv(tt, "dve4", dv4_0 + up - 1)
        ptu = pt[up % 2]
        sl = 512 * (up % 2)
        tt.transpose(ptu[:, 0:128], t["bfs"][:, sl: sl + 128], t["identb"][:, :])
        b.inc(tt.transpose(ptu[:, 128:256], t["bfs"][:, sl + 128: sl + 256], t["identb"][:, :]), "pe2", 1)

    @blk.tensor
    def _(tt):
        b.wait(tt, "ldg")
        b.waitv(tt, "dve2", dv2_0 + 1)
        for u in range(NU):
            g, j2 = u // 8, u % 8
            if g % 2 == 0:
                b.waitv(tt, "ld2", ld2_0 + (g // 2 + 1) * 128)
            else:
                b.waitv(tt, "ld3", ld3_0 + (g // 2 + 1) * 128)
            if u >= 2:
                b.waitv(tt, "dve2", dv2_0 + 1 + (u - 1))
            pqu = pq[u % 2]
            for k in range(8):
                mm = tt.matmul(pqu[:, :], slabs[g % 2][:, 1024 * k + 128 * j2: 1024 * k + 128 * (j2 + 1)],
                               t["WQ1"][:, 384 * k: 384 * (k + 1)], start=(k == 0), stop=(k == 7))
            b.inc(mm, "pe", 1)
            if u >= 1:
                emit_transposes(tt, u - 1)
        emit_transposes(tt, NU - 1)

    @blk.vector
    def _(v):
        for u in range(NU):
            g, j2 = u // 8, u % 8
            r, half = g // 2, g % 2
            tile = 16 * r + 8 * half + j2
            npos = tile % 64
            pqu = pq[u % 2]
            b.waitv(v, "pe", pe0 + u + 1)
            if u >= 2:
                b.waitv(v, "pe2", pe2_0 + u - 1)
            cosT = t["ropeCt"][:, 64 * npos: 64 * (npos + 1)]
            sinT = t["ropeSt"][:, 64 * npos: 64 * (npos + 1)]
            cos4 = cosT.unsqueeze(1).broadcast_to((128, 4, 64))
            sin32 = sinT[:, 0:32].unsqueeze(1).broadcast_to((128, 4, 32))
            qk4 = pqu.rearrange("p (a f) -> p a f", f=64)[:, 0:4, :]
            qk42 = pqu.rearrange("p (a c f) -> p a c f", c=2, f=32)[:, 0:4, :, :]
            mv = t["msc"].rearrange("p (a f) -> p a f", f=64)[:, 0:4, :]
            rv = t["xc"].rearrange("p (a c f) -> p a c f", c=2, f=32)[:, 0:4, :, :]
            b.fence(v, v.tensor_tensor(mv, qk4, cos4, ALU.mult))
            b.fence(v, v.scalar_tensor_tensor(rv[:, :, 0, :], qk42[:, :, 1, :], -1.0, sin32, ALU.mult, ALU.mult))
            b.fence(v, v.scalar_tensor_tensor(rv[:, :, 1, :], qk42[:, :, 0, :], 1.0, sin32, ALU.mult, ALU.mult))
            sl = 512 * (u % 2)
            b.inc(v.tensor_add(t["bfs"][:, sl: sl + 256], t["msc"][:, 0:256], t["xc"][:, 0:256]), "dve3", 1)
            b.inc(v.tensor_copy(t["VP"].rearrange("p (n f) -> p n f", f=65)[:, 2 * tile: 2 * tile + 2, 0:64],
                                pqu.rearrange("p (n f) -> p n f", f=64)[:, 4:6, :]), "dve2", 1)
            if u >= 1:
                copy_trans(b, v, pt, pe2_0, dv4_0, u - 1)
        copy_trans(b, v, pt, pe2_0, dv4_0, NU - 1)

    for cm in reversed(pcms):
        cm.__exit__(None, None, None)


def copy_trans(b, v, pt, pe2_0, dv4_0, up):
    t = b.t
    gp_, j2p = up // 8, up % 8
    rp, halfp = gp_ // 2, gp_ % 2
    tp = 16 * rp + 8 * halfp + j2p
    b.waitv(v, "pe2", pe2_0 + up + 1)
    ptu = pt[up % 2]
    v.tensor_copy(t["QT"][:, 128 * tp: 128 * (tp + 1)], ptu[:, 0:128])
    b.inc(v.tensor_copy(t["KT"][:, 128 * tp: 128 * (tp + 1)], ptu[:, 128:256]), "dve4", 1)


def phase_B(b, blk):
    nc, t = b.nc, b.t

    units = []
    for bt in range(2):
        for i in range(2):
            for qb in range(16):
                nch = 4 * (qb + 1)
                groups = [list(range(g, min(g + 3, nch))) for g in range(0, nch, 3)]
                units.append((bt, i, qb, nch, groups))
    NUB = len(units)
    gg_of_unit = []
    tot = 0
    for (_, _, _, nch, groups) in units:
        gg_of_unit.append(tot)
        tot += len(groups)
    NGG = tot

    pe0 = b.cnt["pe"]
    pe2_0 = b.cnt["pe2"]
    pe3_0 = b.cnt["pe3"]
    ac0 = b.cnt["act"]
    dv0 = b.cnt["dve"]
    b.cnt_dv0_B = dv0
    dv2_0 = b.cnt["dve2"]
    dv3_0 = b.cnt["dve3"]
    ld2_0 = b.cnt["ld2"]
    PB = [t["PB0"], t["PB1"]]

    def kv_ap(bt, n):
        return t["KT"][:, 8192 * bt + 128 * n: 8192 * bt + 128 * (n + 1)]

    def masked_j(qb, n):
        j = n - 4 * qb
        return j if 0 <= j < 4 else None

    mask_cnt_before = []
    mk = 0
    unit_mask_counts = []
    for (bt, i, qb, nch, groups) in units:
        mask_cnt_before.append(mk)
        nm = sum(1 for n in range(nch) if masked_j(qb, n) is not None)
        unit_mask_counts.append(nm)
        mk += nm

    with nc.psum_tensor("bsa", [128, 1536], F32) as bsa,          nc.psum_tensor("bsb", [128, 1536], F32) as bsb,          nc.psum_tensor("bop", [65, 512], F32) as bop,          nc.psum_tensor("btr", [128, 264], BF16) as btr:
        SBUFS = [bsa, bsb]

        @blk.tensor
        def _(tt):
            for u, (bt, i, qb, nch, groups) in enumerate(units):
                base = gg_of_unit[u]
                qcol = 8192 * bt + 512 * qb
                rows = slice(64 * i, 64 * i + 64)
                for g, grp in enumerate(groups):
                    gg = base + g
                    if gg >= 2:
                        b.waitv(tt, "act", ac0 + gg - 1)
                    sb_ = SBUFS[gg % 2]
                    for _ in range(16):
                        tt.matmul(sb_[:, 0:64], t["KT"][:, 0:128], t["QT"][:, 0:64],
                                  start=True, stop=True)
                    for k, n in enumerate(grp):
                        mm = tt.matmul(sb_[:, 512 * k: 512 * (k + 1)], kv_ap(bt, n)[rows, :].copy() if False else t["KT"][64 * i: 64 * i + 64, 8192 * bt + 128 * n: 8192 * bt + 128 * (n + 1)],
                                       t["QT"][64 * i: 64 * i + 64, qcol: qcol + 512],
                                       start=True, stop=True)
                    b.inc(mm, "pe", 1)
                    if g == 0:
                        b.waitv(tt, "dve2", dv2_0 + u)
                    if g >= 1:
                        emit_av(b, tt, units, u, g - 1, bop, PB, ac0, dv0, gg_of_unit, mask_cnt_before)
                emit_av(b, tt, units, u, len(groups) - 1, bop, PB, ac0, dv0, gg_of_unit, mask_cnt_before)
                for _ in range(12):
                    tt.matmul(SBUFS[0][:, 0:64], t["KT"][:, 0:128], t["QT"][:, 0:64],
                              start=True, stop=True)
                b.waitv(tt, "dve2", dv2_0 + u + 1)
                for tch in range(4):
                    mm = tt.transpose(btr[:, 66 * tch: 66 * tch + 65],
                                      t["mscb"][0:65, 128 * tch: 128 * (tch + 1)],
                                      t["identb"][0:65, 0:65])
                b.inc(mm, "pe3", 1)

        @blk.scalar
        def _(sc):
            for u, (bt, i, qb, nch, groups) in enumerate(units):
                base = gg_of_unit[u]
                for g, grp in enumerate(groups):
                    gg = base + g
                    b.waitv(sc, "pe", pe0 + gg + 1)
                    if gg >= 2:
                        b.waitv(sc, "pe2", pe2_0 + gg - 1)
                    sb_ = SBUFS[gg % 2]
                    b.inc(sc.activation(PB[gg % 2][:, 0: 512 * len(grp)], sb_[:, 0: 512 * len(grp)],
                                        AF.Exp, bias=C0, scale=INV8), "act", 1)

        @blk.vector
        def _(v):
            for u, (bt, i, qb, nch, groups) in enumerate(units):
                base = gg_of_unit[u]
                nm = 0
                for g, grp in enumerate(groups):
                    gg = base + g
                    if any(masked_j(qb, n) is not None for n in grp):
                        b.waitv(v, "act", ac0 + gg + 1)
                        for k, n in enumerate(grp):
                            j = masked_j(qb, n)
                            if j is not None:
                                sl = PB[gg % 2][:, 512 * k: 512 * (k + 1)]
                                b.inc(v.tensor_tensor(sl, sl, t["masks"][:, 512 * j: 512 * (j + 1)], ALU.mult),
                                      "dve", 1)
                                nm += 1
                assert nm == unit_mask_counts[u]
                b.waitv(v, "pe2", pe2_0 + gg_of_unit[u] + len(groups))
                b.waitv(v, "pe3", pe3_0 + u)
                b.inc(v.tensor_copy(t["mscb"][0:65, 0:512], bop[:, :]), "dve2", 1)
                b.waitv(v, "pe3", pe3_0 + u + 1)
                for tch in range(4):
                    b.fence(v, v.reciprocal(t["lrec"][:, tch: tch + 1], btr[:, 66 * tch + 64: 66 * tch + 65]))
                if u >= 2:
                    b.waitv(v, "ld2", ld2_0 + (u - 1) * 16)
                for tch in range(4):
                    i3 = v.tensor_scalar(t["obst"][:, 256 * (u % 2) + 64 * tch: 256 * (u % 2) + 64 * (tch + 1)],
                                         btr[:, 66 * tch: 66 * tch + 64], t["lrec"][:, tch: tch + 1], None, ALU.mult)
                b.inc(i3, "dve3", 1)

        @blk.sync
        def _(s):
            for u, (bt, i, qb, nch, groups) in enumerate(units):
                b.waitv(s, "dve3", dv3_0 + u + 1)
                dst = t["og_in"][8192 * bt + 512 * qb: 8192 * bt + 512 * (qb + 1), 64 * i: 64 * i + 64]
                src_ = t["obst"][:, 256 * (u % 2): 256 * (u % 2 + 1)].rearrange("p (tc f) -> p tc f", f=64)
                b.inc(s.dma_start(out=dst.rearrange("(tc p) f -> p tc f", p=128), in_=src_), "ld2", 16)



def phase_C(b, blk):
    nc, t = b.nc, b.t
    lda0 = b.cnt["lda"]
    ldb0 = b.cnt["ldb"]
    ldc0 = b.cnt["ldc"]
    ld2f = b.cnt["ld2"]
    pe0 = b.cnt["pe"]
    pe2_0 = b.cnt["pe2"]
    dv0 = b.cnt["dve"]
    dv2_0 = b.cnt["dve2"]
    cc0 = b.cnt["cc"]
    ogld = [t["ogld0"], t["ogld1"]]
    cstg = [t["cstg0"], t["cstg1"]]

    @blk.gpsimd
    def _(g):
        b.waitv(g, "ld2", ld2f)
        b.inc(g.dma_start(out=t["WO1"][:, :], in_=t["wo1c"]), "ldg", 16)
    ldg_w = b.cnt["ldg"]

    tiles = [(kp, cp, mm_) for kp in range(4) for cp in range(8) for mm_ in range(4)]

    with nc.psum_tensor("ctr0", [128, 128], BF16) as ctr0,          nc.psum_tensor("ctr1", [128, 128], BF16) as ctr1,          nc.psum_tensor("cxo0", [128, 1024], F32) as cxo0,          nc.psum_tensor("cxo1", [128, 1024], F32) as cxo1:
        ctr = [ctr0, ctr1]
        cxo = [cxo0, cxo1]

        @blk.sync
        def _(s):
            b.waitv(s, "ld2", ld2f)
            for u, (kp, cp, mm_) in enumerate(tiles):
                row = 2048 * cp + 512 * kp + 128 * mm_
                if u >= 2:
                    b.waitv(s, "pe", pe0 + u - 1)
                sem = "lda" if u % 2 == 0 else "ldb"
                b.inc(s.dma_start(out=ogld[u % 2][:, :], in_=t["og_in"][row: row + 128, :]), sem, 16)
                if u % 4 == 3:
                    grp = u // 4
                    kpg, cpg = grp // 8, grp % 8
                    b.waitv(s, "dve2", dv2_0 + (grp + 1) * 4)
                    dst = t["xo_rs_in"][4096 * kpg + 512 * cpg: 4096 * kpg + 512 * (cpg + 1), :]
                    b.inc(s.dma_start(out=dst.rearrange("(m p) f -> p m f", p=128),
                                      in_=cstg[grp % 2].rearrange("p (m f) -> p m f", f=1024)), "ldc", 16)

        @blk.tensor
        def _(tt):
            b.waitv(tt, "ldg", ldg_w)
            for u, (kp, cp, mm_) in enumerate(tiles):
                if u % 2 == 0:
                    b.waitv(tt, "lda", lda0 + (u // 2 + 1) * 16)
                else:
                    b.waitv(tt, "ldb", ldb0 + (u // 2 + 1) * 16)
                if u >= 2:
                    b.waitv(tt, "dve", dv0 + u - 1)
                b.inc(tt.transpose(ctr[u % 2][:, :], ogld[u % 2][:, :], t["identb"][:, :]), "pe", 1)
                b.waitv(tt, "dve", dv0 + u + 1)
                if u >= 2:
                    b.waitv(tt, "dve2", dv2_0 + u - 1)
                mm = tt.matmul(cxo[u % 2][:, 0:512], t["oglh"][:, :], t["WO1"][:, 0:512], start=True, stop=True)
                mm = tt.matmul(cxo[u % 2][:, 512:1024], t["oglh"][:, :], t["WO1"][:, 512:1024], start=True, stop=True)
                b.inc(mm, "pe2", 1)

        @blk.vector
        def _(v):
            for u, (kp, cp, mm_) in enumerate(tiles):
                b.waitv(v, "pe", pe0 + u + 1)
                if u >= 1:
                    b.waitv(v, "pe2", pe2_0 + u)
                b.inc(v.tensor_copy(t["oglh"][:, :], ctr[u % 2][:, :]), "dve", 1)
                b.waitv(v, "pe2", pe2_0 + u + 1)
                gpar = (u // 4) % 2
                if u >= 8:
                    b.waitv(v, "ldc", ldc0 + (u // 4 - 1) * 16)
                b.inc(v.tensor_copy(cstg[gpar][:, 1024 * mm_: 1024 * (mm_ + 1)], cxo[u % 2][:, :]), "dve2", 1)


    @blk.gpsimd
    def _(g):
        for kp in range(4):
            b.waitv(g, "ldc", ldc0 + (8 * kp + 8) * 16)
            b.inc(g.collective_compute("ReduceScatter", ALU.add, replica_groups=GROUPS8,
                                       ins=[t["xo_rs_in"][4096 * kp: 4096 * (kp + 1), :].opt()],
                                       outs=[t["xo_rs_out"][512 * kp: 512 * (kp + 1), :].opt()]), "cc", 1)

    @blk.sync
    def _(s):
        b.inc(s.dma_start(out=t["r1f"][:, :], in_=t["x_my"]), "ldx", 16)
    ldx_f = b.cnt["ldx"]

    dvf0 = b.cnt["dve3"]
    acf0 = b.cnt["act2"]

    @blk.gpsimd
    def _(g):
        for j in range(NJ):
            kp = j // 4
            b.waitv(g, "cc", cc0 + kp + 1)
            if j >= 2:
                b.waitv(g, "dve3", dvf0 + (j - 1) * 2)
            xb = t["xc"] if j % 2 == 0 else t["msc"]
            row = 128 * j
            b.inc(g.dma_start(out=xb[:, :], in_=t["xo_rs_out"][row: row + 128, :]),
                  "gp" if j % 2 == 0 else "gp2", 16)

    @blk.vector
    def _(v):
        b.waitv(v, "ldx", ldx_f)
        for j in range(NJ):
            xb = t["xc"] if j % 2 == 0 else t["msc"]
            sem = "gp" if j % 2 == 0 else "gp2"
            b.waitv(v, sem, b.cnt[sem] - 16 * (NJ // 2 - 1 - j // 2))
            b.vred(v, "reduce_sum", t["stat"][:, j: j + 1], xb[:, :], AX.X)
            b.fence(v, v.tensor_scalar(t["stat"][:, 16 + j: 16 + j + 1], t["stat"][:, j: j + 1],
                                       float(INV_D), None, ALU.mult))
            b.fence(v, v.tensor_scalar(t["sq"][:, :], xb[:, :], t["stat"][:, 16 + j: 16 + j + 1],
                                       None, ALU.subtract))
            b.fence(v, v.tensor_tensor(t["csq"][:, :], t["sq"][:, :], t["sq"][:, :], ALU.mult))
            b.vred(v, "reduce_sum", t["stat"][:, j: j + 1], t["csq"][:, :], AX.X)
            b.inc(v.tensor_scalar(t["stat"][:, j: j + 1], t["stat"][:, j: j + 1], float(INV_D), float(EPS),
                                  ALU.mult, ALU.add), "dve3", 1)
            b.waitv(v, "act2", acf0 + j + 1)
            b.fence(v, v.tensor_scalar(t["sq"][:, :], t["sq"][:, :], t["rstdb"][:, j: j + 1], None, ALU.mult))
            b.inc(v.tensor_tensor(t["r1f"][:, 1024 * j: 1024 * (j + 1)], t["r1f"][:, 1024 * j: 1024 * (j + 1)],
                                  t["sq"][:, :], ALU.add), "dve3", 1)

    @blk.scalar
    def _(sc):
        for j in range(NJ):
            b.waitv(sc, "dve3", dvf0 + 2 * j + 1)
            sc.activation(t["stat"][:, 16 + j: 16 + j + 1], t["stat"][:, j: j + 1], AF.Ln)
            b.inc(sc.activation(t["rstdb"][:, j: j + 1], t["stat"][:, 16 + j: 16 + j + 1], AF.Exp,
                                scale=np.float32(-0.5)), "act2", 1)

    @blk.sync
    def _(s):
        b.wait(s, "dve3")
        b.inc(s.dma_start(out=t["x1_pk"], in_=t["r1f"][:, :]), "ldx", 16)


def phase_D(b, blk):
    nc, t = b.nc, b.t
    dvf = b.cnt["dve3"]
    ldxf = b.cnt["ldx"]

    @blk.gpsimd
    def _(g):
        b.waitv(g, "dve3", dvf)
        b.inc(g.dma_start(out=t["WQ2"][:, :], in_=t["wqkv2"]), "ldg", 16)
        b.inc(g.dma_start(out=t["brow"][0:1, 0:3072], in_=t["bqkv"]), "ldg", 16)
    ldg_d = b.cnt["ldg"]

    pe_be0 = b.cnt["pe"]
    dv_be0 = b.cnt["dve"]
    with nc.psum_tensor("dbe", [1, 512], F32) as dbe:
        @blk.vector
        def _(v):
            b.inc(v.tensor_copy(t["b1c_bf"][:, 0:8], t["vecc"][:, 8:16]), "dve", 1)

        @blk.tensor
        def _(tt):
            b.waitv(tt, "ldg", ldg_d)
            b.waitv(tt, "dve", dv_be0 + 1)
            for cb in range(6):
                if cb >= 1:
                    b.waitv(tt, "dve", dv_be0 + 1 + cb)
                for k in range(8):
                    tt.matmul(dbe[:, :], t["b1c_bf"][:, k: k + 1],
                              t["WQ2"][:, 3072 * k + 512 * cb: 3072 * k + 512 * (cb + 1)],
                              start=(k == 0), stop=False)
                mm = tt.matmul(dbe[:, :], t["ones_r"][0:1, 0:1], t["brow"][0:1, 512 * cb: 512 * (cb + 1)],
                               start=False, stop=True)
                b.inc(mm, "pe", 1)

        @blk.vector
        def _(v):
            for cb in range(6):
                b.waitv(v, "pe", pe_be0 + cb + 1)
                b.inc(v.tensor_copy(t["beffq"][0:1, 512 * cb: 512 * (cb + 1)], dbe[0:1, :]), "dve", 1)
            for k in range(8):
                i2 = v.tensor_scalar(t["WQ2"][:, 3072 * k: 3072 * (k + 1)], t["WQ2"][:, 3072 * k: 3072 * (k + 1)],
                                     t["vecc"][:, k: k + 1], None, ALU.mult)
            b.inc(i2, "dve2", 1)
            v.memset(t["V2ST"][:, :], 0.0)
            b.inc(v.memset(t["V2ST"].rearrange("p (h f) -> p h f", f=128)[:, :, 64:65], 1.0), "dve2", 1)
    dv2_w = b.cnt["dve2"]

    pe0 = b.cnt["pe"]
    pe2_0 = b.cnt["pe2"]
    pe3_0 = b.cnt["pe3"]
    ac0 = b.cnt["act"]
    ac2_0 = b.cnt["act2"]
    dv0 = b.cnt["dve"]
    dv3_0 = b.cnt["dve3"]
    dv4_0 = b.cnt["dve4"]
    lda0, ldb0 = b.cnt["lda"], b.cnt["ldb"]
    ldc0 = b.cnt["ldc"]
    ld2_0, ld3_0 = b.cnt["ld2"], b.cnt["ld3"]

    with nc.psum_tensor("pdtr", [128, 1024], BF16) as pdtr, \
         nc.psum_tensor("pdm0", [128, 512], F32) as pdm0, \
         nc.psum_tensor("pdm1", [128, 512], F32) as pdm1, \
         nc.psum_tensor("pdq0", [128, 512], BF16) as pdq0, \
         nc.psum_tensor("pdq1", [128, 512], BF16) as pdq1:
        pdm = [pdm0, pdm1]
        pdq = [pdq0, pdq1]

        @blk.sync
        def _(s):
            b.waitv(s, "ldx", ldxf)
            for j in range(NJ):
                sem = "lda" if j % 2 == 0 else "ldb"
                if j >= 2:
                    b.waitv(s, "dve", dv0 + (j - 2) * 9 + 1)
                b.inc(s.dma_start(out=(t["xc"] if j % 2 == 0 else t["dsq"])[:, :],
                                  in_=t["x1_pk"][:, 1024 * j: 1024 * (j + 1)]), sem, 16)
                if j >= 1:
                    b.waitv(s, "dve", dv0 + (j - 1) * 9 + 2)
                    b.inc(s.dma_start(out=t["u_pk"][:, 1024 * (j - 1): 1024 * j], in_=t["msc"][:, :]), "ldc", 16)
                    b.waitv(s, "dve", dv0 + 9 * (j - 1) + 9)
                    dstv = t["v2h_in"].rearrange("(h jj p) f -> p h jj f", h=H, p=128)[:, :, j - 1, :]
                    b.inc(s.dma_start(out=dstv, in_=t["V2ST"].rearrange("p (h f) -> p h f", f=128)), "ld2", 16)
            b.waitv(s, "dve", dv0 + (NJ - 1) * 9 + 2)
            b.inc(s.dma_start(out=t["u_pk"][:, 1024 * (NJ - 1): 1024 * NJ], in_=t["msc"][:, :]), "ldc", 16)
            b.waitv(s, "dve", dv0 + 9 * NJ)
            dstv = t["v2h_in"].rearrange("(h jj p) f -> p h jj f", h=H, p=128)[:, :, NJ - 1, :]
            b.inc(s.dma_start(out=dstv, in_=t["V2ST"].rearrange("p (h f) -> p h f", f=128)), "ld2", 16)
            b.waitv(s, "dve4", dv4_0 + 4 * NJ)
            for k in range(8):
                b.inc(s.dma_start(out=t["k2t_in"][128 * k: 128 * (k + 1), :],
                                  in_=t["K2TS"][:, 2048 * k: 2048 * (k + 1)]), "ld3", 16)

        @blk.vector
        def _(v):
            for j in range(NJ):
                xb = t["xc"] if j % 2 == 0 else t["dsq"]
                sqs = t["sq"]
                st_ = t["stat"]
                sem = "lda" if j % 2 == 0 else "ldb"
                b.waitv(v, sem, (lda0 if j % 2 == 0 else ldb0) + (j // 2 + 1) * 16)
                if j >= 1:
                    b.waitv(v, "ldc", ldc0 + j * 16)
                b.vred(v, "reduce_sum", st_[:, 0:1], xb[:, :], AX.X)
                b.fence(v, v.tensor_scalar(st_[:, 1:2], st_[:, 0:1], float(INV_D), None, ALU.mult))
                b.fence(v, v.tensor_scalar(sqs[:, :], xb[:, :], st_[:, 1:2], None, ALU.subtract))
                b.fence(v, v.tensor_tensor(t["csq2"][:, :], sqs[:, :], sqs[:, :], ALU.mult))
                b.vred(v, "reduce_sum", st_[:, 0:1], t["csq2"][:, :], AX.X)
                b.inc(v.tensor_scalar(st_[:, 0:1], st_[:, 0:1], float(INV_D), float(EPS),
                                      ALU.mult, ALU.add), "dve3", 1)
                b.waitv(v, "act", ac0 + j + 1)
                b.inc(v.tensor_scalar(t["msc"][:, :], sqs[:, :], t["rstdb"][:, 0:1], None, ALU.mult), "dve", 1)
                b.fence(v, v.tensor_copy(t["gsc"][:, 0:1], t["msc"][:, 1023:1024]))
                b.vred(v, "reduce_sum", st_[:, 2:3], t["msc"][:, :], AX.X)
                b.fence(v, v.tensor_scalar(st_[:, 3:4], st_[:, 2:3], float(INV_D), None, ALU.mult))
                b.fence(v, v.tensor_scalar(sqs[:, :], t["msc"][:, :], st_[:, 3:4], None, ALU.subtract))
                b.fence(v, v.tensor_tensor(t["csq2"][:, :], sqs[:, :], sqs[:, :], ALU.mult))
                b.vred(v, "reduce_sum", st_[:, 2:3], t["csq2"][:, :], AX.X)
                b.inc(v.tensor_scalar(st_[:, 2:3], st_[:, 2:3], float(INV_D), float(EPS),
                                      ALU.mult, ALU.add), "dve3", 1)
                b.waitv(v, "act2", ac2_0 + j + 1)
                if j >= 1:
                    b.waitv(v, "pe", pe0 + j)
                b.inc(v.tensor_scalar(t["n1ib"][:, :], sqs[:, :], t["rstdb"][:, 1:2], None, ALU.mult), "dve", 1)
                b.waitv(v, "pe", pe0 + j + 1)
                if j >= 1:
                    b.waitv(v, "pe2", pe2_0 + 6 * j)
                b.inc(v.tensor_copy(t["n1it"][:, :], pdtr[:, :]), "dve", 1)
                def qkcopy(cb2):
                    b.waitv(v, "pe3", pe3_0 + j * 4 + cb2 + 1)
                    pq = pdq[cb2 % 2]
                    dstb = t["Q2T"] if cb2 < 2 else t["K2TS"]
                    for s4 in range(4):
                        kp = 4 * (cb2 % 2) + s4
                        i5 = v.tensor_copy(dstb[:, 2048 * kp + 128 * j: 2048 * kp + 128 * (j + 1)],
                                           pq[:, 128 * s4: 128 * (s4 + 1)])
                    b.inc(i5, "dve4", 1)
                for cb in range(6):
                    b.waitv(v, "pe2", pe2_0 + 6 * j + cb + 1)
                    pm = pdm[cb % 2]
                    if cb < 4:
                        nprev = j * 4 + cb
                        if nprev >= 1:
                            b.waitv(v, "pe3", pe3_0 + nprev)
                        b.inc(v.tensor_copy(t["qsb"][:, :], pm[:, :]), "dve", 1)
                    else:
                        hb = 8 * (cb - 4)
                        if cb == 4 and j >= 1:
                            b.waitv(v, "ld2", ld2_0 + 16 * j)
                        b.inc(v.tensor_copy(
                            t["V2ST"].rearrange("p (h f) -> p h f", f=128)[:, hb: hb + 8, 0:64],
                            pm.rearrange("p (h f) -> p h f", f=64)[:, 0:8, :]), "dve", 1)
                    if 1 <= cb <= 4:
                        qkcopy(cb - 1)

        @blk.scalar
        def _(sc):
            for j in range(NJ):
                b.waitv(sc, "dve3", dv3_0 + 2 * j + 1)
                sc.activation(t["stat"][:, 1:2], t["stat"][:, 0:1], AF.Ln)
                b.inc(sc.activation(t["rstdb"][:, 0:1], t["stat"][:, 1:2], AF.Exp, scale=np.float32(-0.5)),
                      "act", 1)
                b.waitv(sc, "dve3", dv3_0 + 2 * j + 2)
                sc.activation(t["stat"][:, 3:4], t["stat"][:, 2:3], AF.Ln)
                b.inc(sc.activation(t["rstdb"][:, 1:2], t["stat"][:, 3:4], AF.Exp, scale=np.float32(-0.5)),
                      "act2", 1)

        @blk.tensor
        def _(tt):
            b.waitv(tt, "dve2", dv2_w)
            b.waitv(tt, "ldx", ldxf)
            for j in range(NJ):
                b.waitv(tt, "dve", dv0 + 9 * j + 2)
                for k in range(8):
                    mm = tt.transpose(pdtr[:, 128 * k: 128 * (k + 1)], t["n1ib"][:, 128 * k: 128 * (k + 1)],
                                      t["identb"][:, :])
                b.inc(mm, "pe", 1)
                b.waitv(tt, "dve", dv0 + 9 * j + 3)
                for cb in range(6):
                    pm = pdm[cb % 2]
                    nprv = j * 6 + cb
                    if nprv >= 2:
                        pj, pcb = (nprv - 2) // 6, (nprv - 2) % 6
                        b.waitv(tt, "dve", dv0 + 9 * pj + 4 + pcb)
                    for k in range(8):
                        tt.matmul(pm[:, :], t["n1it"][:, 128 * k: 128 * (k + 1)],
                                  t["WQ2"][:, 3072 * k + 512 * cb: 3072 * k + 512 * (cb + 1)],
                                  start=(k == 0), stop=False)
                    mm = tt.matmul(pm[:, :], t["ones_r"][0:1, 0:128], t["beffq"][0:1, 512 * cb: 512 * (cb + 1)],
                                   start=False, stop=True)
                    b.inc(mm, "pe2", 1)
                    if cb < 4:
                        b.waitv(tt, "dve", dv0 + 9 * j + 4 + cb)
                        pq = pdq[cb % 2]
                        ntr = j * 4 + cb
                        if ntr >= 2:
                            b.waitv(tt, "dve4", dv4_0 + ntr - 1)
                        for s4 in range(4):
                            mm2 = tt.transpose(pq[:, 128 * s4: 128 * (s4 + 1)],
                                               t["qsb"][:, 128 * s4: 128 * (s4 + 1)], t["identb"][:, :])
                        b.inc(mm2, "pe3", 1)

    @blk.gpsimd
    def _(g):
        b.wait(g, "ld3")
        b.inc(g.collective_compute("AllGather", ALU.bypass, replica_groups=GROUPS4,
                                   ins=[t["k2t_in"].opt()], outs=[t["k2t_ag"].opt()]), "cc", 1)
        b.wait(g, "ld2")
        b.inc(g.collective_compute("AllGather", ALU.bypass, replica_groups=GROUPS4,
                                   ins=[t["v2h_in"].opt()], outs=[t["v2h_ag"].opt()]), "cc", 1)


def phase_E(b, blk):
    nc, t = b.nc, b.t
    ccD = b.cnt["cc"]
    PBe = [t["PB0e"], t["PB1e"]]

    units = [(hp, qb) for hp in range(8) for qb in range(4)]
    NGE = 64

    pe0 = b.cnt["pe"]
    pe2_0 = b.cnt["pe2"]
    pe3_0 = b.cnt["pe3"]
    ac0 = b.cnt["act"]
    dv2_0 = b.cnt["dve2"]
    dv3_0 = b.cnt["dve3"]
    lda0, ldb0 = b.cnt["lda"], b.cnt["ldb"]

    @blk.sync
    def _(s):
        b.waitv(s, "cc", ccD)
        for hp in range(8):
            sem = "lda" if hp % 2 == 0 else "ldb"
            if hp >= 1:
                b.waitv(s, "pe2", pe2_0 + NGE * 4 * hp)
            for h in (2 * hp, 2 * hp + 1):
                for r2 in range(4):
                    b.inc(s.dma_start(out=t["KB"][64 * (h % 2): 64 * (h % 2) + 64, 2048 * r2: 2048 * (r2 + 1)],
                                      in_=t["k2t_ag"][1024 * r2 + 64 * h: 1024 * r2 + 64 * (h + 1), :]), sem, 16)
                vh = t["VHA"] if h % 2 == 0 else t["VHB"]
                for r2 in range(4):
                    b.inc(s.dma_start(
                        out=vh.rearrange("p (rb tt f) -> p rb tt f", rb=4, f=128)[:, r2, :, :],
                        in_=t["v2h_ag"][32768 * r2 + 2048 * h: 32768 * r2 + 2048 * (h + 1), :]
                            .rearrange("(tt p) f -> p tt f", p=128)), sem, 16)

    with nc.psum_tensor("spa", [128, 1024], F32) as spa, \
         nc.psum_tensor("spb", [128, 1024], F32) as spb, \
         nc.psum_tensor("eop0", [128, 512], F32) as eop0, \
         nc.psum_tensor("eop1", [128, 512], F32) as eop1, \
         nc.psum_tensor("etr", [128, 264], BF16) as etr:
        SB2 = [spa, spb]
        EOP = [eop0, eop1]

        @blk.tensor
        def _(tt):
            for u, (hp, qb) in enumerate(units):
                qcol = 2048 * hp + 512 * qb
                if qb == 0:
                    sem = "lda" if hp % 2 == 0 else "ldb"
                    b.waitv(tt, sem, (lda0 if hp % 2 == 0 else ldb0) + (hp // 2 + 1) * 16 * 16)
                for g in range(NGE):
                    gg = NGE * u + g
                    if gg >= 2:
                        b.waitv(tt, "act", ac0 + gg - 1)
                    sb_ = SB2[gg % 2]
                    tt.matmul(sb_[:, 0:512],
                              t["KB"][0:64, 128 * g: 128 * (g + 1)],
                              t["Q2T"][0:64, qcol: qcol + 512],
                              start=True, stop=True)
                    mm = tt.matmul(sb_[:, 512:1024],
                                   t["KB"][64:128, 128 * g: 128 * (g + 1)],
                                   t["Q2T"][64:128, qcol: qcol + 512],
                                   start=True, stop=True)
                    b.inc(mm, "pe", 1)
                    if g == 0:
                        b.waitv(tt, "dve2", dv2_0 + 2 * u)
                    if g >= 1:
                        emit_av_e(b, tt, u, g - 1, EOP, PBe, ac0)
                emit_av_e(b, tt, u, NGE - 1, EOP, PBe, ac0)
                for _ in range(12):
                    tt.matmul(spa[:, 0:64], t["KB"][:, 0:128], t["Q2T"][:, 0:64],
                              start=True, stop=True)
                for p in range(2):
                    b.waitv(tt, "dve2", dv2_0 + 2 * u + p + 1)
                    for tch in range(4):
                        mm = tt.transpose(etr[:, 66 * tch: 66 * tch + 65],
                                          t["mscb"][0:65, 128 * tch: 128 * (tch + 1)],
                                          t["identb"][0:65, 0:65])
                    b.inc(mm, "pe3", 1)

        @blk.scalar
        def _(sc):
            for u in range(len(units)):
                for g in range(NGE):
                    gg = NGE * u + g
                    b.waitv(sc, "pe", pe0 + gg + 1)
                    if gg >= 2:
                        b.waitv(sc, "pe2", pe2_0 + gg - 1)
                    b.inc(sc.activation(PBe[gg % 2][:, 0:1024], SB2[gg % 2][:, 0:1024],
                                        AF.Exp, bias=C0, scale=INV8), "act", 1)

        @blk.vector
        def _(v):
            for u, (hp, qb) in enumerate(units):
                b.waitv(v, "pe2", pe2_0 + NGE * (u + 1))
                for p in range(2):
                    h = 2 * hp + p
                    b.waitv(v, "pe3", pe3_0 + 2 * u + p)
                    b.inc(v.tensor_copy(t["mscb"][0:65, 0:512], EOP[p][0:65, :]), "dve2", 1)
                    b.waitv(v, "pe3", pe3_0 + 2 * u + p + 1)
                    for tch in range(4):
                        b.fence(v, v.reciprocal(t["lrec"][:, tch: tch + 1], etr[:, 66 * tch + 64: 66 * tch + 65]))
                    for tch in range(4):
                        j = 4 * qb + tch
                        i3 = v.tensor_scalar(t["O2N"][:, 1024 * j + 64 * h: 1024 * j + 64 * (h + 1)],
                                             etr[:, 66 * tch: 66 * tch + 64], t["lrec"][:, tch: tch + 1],
                                             None, ALU.mult)
                b.inc(i3, "dve3", 1)

    phase_E_end(b, blk)


def phase_G(b, blk):
    nc, t = b.nc, b.t
    ccE = b.cnt["cc"]
    dvEe = b.cnt["dve"]
    ldcE = b.cnt["ldc"]
    ld3E = b.cnt["ld3"]

    @blk.gpsimd
    def _(g):
        b.waitv(g, "dve", dvEe)
        b.waitv(g, "ldc", ldcE)
        b.waitv(g, "ld3", ld3E)
        b.inc(g.dma_start(out=t["W1G"][:, :], in_=t["w1e"]), "ldg", 16)
        b.inc(g.dma_start(out=t["W2S"][:, :], in_=t["w2e"]), "ldg", 16)
        b.inc(g.dma_start(out=t["growb"][0:1, :], in_=t["b1e"]), "ldg", 16)
        b.inc(g.dma_start(out=t["gbrow"][0:1, 0:E], in_=t["gateb"]), "ldg", 16)
        b.inc(g.dma_start(out=t["b2eb"][0:1, :], in_=t["b2e"]), "ldg", 16)
        b.inc(g.dma_start(out=t["gwg"][:, :], in_=t["gatew"]), "ldg", 16)
        b.inc(g.dma_start(out=t["eselS"][:, :], in_=t["esel"]), "ldg", 16)
    ldg_g = b.cnt["ldg"]

    pe_p0 = b.cnt["pe"]
    dv_p0 = b.cnt["dve"]
    with nc.psum_tensor("gbe", [1, 512], F32) as gbe, \
         nc.psum_tensor("gbc", [128, 128], F32) as gbc:
        @blk.vector
        def _(v):
            b.waitv(v, "ldg", ldg_g)
            b.inc(v.tensor_copy(t["b2c_bf"][:, 0:8], t["vecc"][:, 24:32]), "dve", 1)

        @blk.tensor
        def _(tt):
            b.waitv(tt, "ldg", ldg_g)
            b.waitv(tt, "dve", dv_p0 + 1)
            for fb in range(8):
                if fb >= 1:
                    b.waitv(tt, "dve", dv_p0 + 1 + fb)
                for k in range(8):
                    tt.matmul(gbe[:, :], t["b2c_bf"][:, k: k + 1],
                              t["W1G"][:, 4096 * k + 512 * fb: 4096 * k + 512 * (fb + 1)],
                              start=(k == 0), stop=False)
                mm = tt.matmul(gbe[:, :], t["ones_r"][0:1, 0:1], t["growb"][0:1, 512 * fb: 512 * (fb + 1)],
                               start=False, stop=True)
                b.inc(mm, "pe", 1)
            b.waitv(tt, "dve", dv_p0 + 9)
            for k in range(8):
                tt.matmul(gbe[:, 0:E], t["b2c_bf"][:, k: k + 1], t["gwg"][:, 8 * k: 8 * (k + 1)],
                          start=(k == 0), stop=False)
            mm = tt.matmul(gbe[:, 0:E], t["ones_r"][0:1, 0:1], t["gbrow"][0:1, 0:E],
                           start=False, stop=True)
            b.inc(mm, "pe", 1)
            for m in range(32):
                b.waitv(tt, "dve", dv_p0 + 11 + m)
                b.inc(tt.transpose(gbc[:, 0:1], t["b1effr"][0:1, 128 * m: 128 * (m + 1)],
                                   t["identf"][0:1, 0:1]), "pe", 1)

        @blk.vector
        def _(v):
            for fb in range(8):
                b.waitv(v, "pe", pe_p0 + fb + 1)
                b.inc(v.tensor_copy(t["b1effr"][0:1, 512 * fb: 512 * (fb + 1)], gbe[0:1, :]), "dve", 1)
            b.waitv(v, "pe", pe_p0 + 9)
            b.inc(v.tensor_copy(t["beffg"][0:1, 0:E], gbe[0:1, 0:E]), "dve", 1)
            for k in range(8):
                v.tensor_scalar(t["W1G"][:, 4096 * k: 4096 * (k + 1)], t["W1G"][:, 4096 * k: 4096 * (k + 1)],
                                t["vecc"][:, 16 + k: 17 + k], None, ALU.mult)
            i2 = None
            for k in range(8):
                i2 = v.tensor_scalar(t["gwg"][:, 8 * k: 8 * (k + 1)], t["gwg"][:, 8 * k: 8 * (k + 1)],
                                     t["vecc"][:, 16 + k: 17 + k], None, ALU.mult)
            b.inc(i2, "dve", 1)
            for m in range(32):
                b.waitv(v, "pe", pe_p0 + 10 + m)
                b.inc(v.tensor_copy(t["b1effc"][:, m: m + 1], gbc[:, 0:1]), "dve", 1)
    dv_pw = b.cnt["dve"]

    pe0 = b.cnt["pe"]
    pe2_0 = b.cnt["pe2"]
    pe3_0 = b.cnt["pe3"]
    ac0 = b.cnt["act"]
    dv0 = b.cnt["dve2"]
    dv3_0 = b.cnt["dve3"]
    dv4_0 = b.cnt["dve4"]
    lda0, ldb0 = b.cnt["lda"], b.cnt["ldb"]
    ldc0 = b.cnt["ldc"]
    cc0 = b.cnt["cc"]
    order = [(kp, cp) for kp in range(4) for cp in range(8)]

    with nc.psum_tensor("glg0", [128, 16], F32) as glg0, \
         nc.psum_tensor("glg1", [128, 16], F32) as glg1, \
         nc.psum_tensor("gph0", [128, 512], F32) as gph0, \
         nc.psum_tensor("gph1", [128, 512], F32) as gph1, \
         nc.psum_tensor("gpm0", [128, 512], F32) as gpm0, \
         nc.psum_tensor("gpm1", [128, 512], F32) as gpm1:
        glg = [glg0, glg1]
        gph = [gph0, gph1]
        gpm = [gpm0, gpm1]

        @blk.sync
        def _(s):
            b.waitv(s, "cc", ccE)
            for u, (kp, cp) in enumerate(order):
                TT = 4 * cp + kp
                rr, colb = TT // 4, TT % 4
                sem = "lda" if u % 2 == 0 else "ldb"
                if u >= 1:
                    b.waitv(s, "pe", pe0 + 32 * u)
                for k in range(8):
                    b.inc(s.dma_start(out=t["H2TB"][:, 512 * k: 512 * (k + 1)],
                                      in_=t["h2t_ag"][1024 * rr + 128 * k: 1024 * rr + 128 * (k + 1),
                                                      512 * colb: 512 * (colb + 1)]), sem, 16)
                if u >= 1:
                    pkp, pcp = order[u - 1]
                    pTT = 4 * pcp + pkp
                    b.waitv(s, "dve4", dv4_0 + 4 * u)
                    dst = t["rs_in"][4096 * pkp + 512 * pcp: 4096 * pkp + 512 * (pcp + 1), :]
                    b.inc(s.dma_start(out=dst.rearrange("(m p) f -> p m f", p=128),
                                      in_=t["MOEST"].rearrange("p (m f) -> p m f", f=1024)), "ldc", 16)
            pkp, pcp = order[-1]
            b.waitv(s, "dve4", dv4_0 + 4 * 32)
            dst = t["rs_in"][4096 * pkp + 512 * pcp: 4096 * pkp + 512 * (pcp + 1), :]
            b.inc(s.dma_start(out=dst.rearrange("(m p) f -> p m f", p=128),
                              in_=t["MOEST"].rearrange("p (m f) -> p m f", f=1024)), "ldc", 16)

        @blk.tensor
        def _(tt):
            b.waitv(tt, "dve", dv_pw)
            b.waitv(tt, "cc", ccE)
            for u, (kp, cp) in enumerate(order):
                sem = "lda" if u % 2 == 0 else "ldb"
                b.waitv(tt, sem, (lda0 if u % 2 == 0 else ldb0) + (u // 2 + 1) * 8 * 16)
                for j2 in range(4):
                    if 4 * u + j2 >= 2:
                        if j2 >= 2:
                            b.waitv(tt, "dve2", dv0 + 36 * u + 5 * (j2 - 2) + 1)
                        else:
                            b.waitv(tt, "dve2", dv0 + 36 * (u - 1) + 5 * (j2 + 2) + 1)
                    pl = glg[j2 % 2]
                    for k in range(8):
                        tt.matmul(pl[:, 0:E], t["H2TB"][:, 512 * k + 128 * j2: 512 * k + 128 * (j2 + 1)],
                                  t["gwg"][:, 8 * k: 8 * (k + 1)], start=(k == 0), stop=False)
                    mm = tt.matmul(pl[:, 0:E], t["ones_r"][0:1, 0:128], t["beffg"][0:1, 0:E],
                                   start=False, stop=True)
                    b.inc(mm, "pe3", 1)
                for m in range(32):
                    ph = gph[m % 2]
                    if 32 * u + m >= 2:
                        b.waitv(tt, "dve3", dv3_0 + 32 * u + m - 1)
                    for k in range(8):
                        mm = tt.matmul(ph[:, :], t["W1G"][:, 4096 * k + 128 * m: 4096 * k + 128 * (m + 1)],
                                       t["H2TB"][:, 512 * k: 512 * (k + 1)], start=(k == 0), stop=(k == 7))
                    b.inc(mm, "pe", 1)
                for j2 in range(4):
                    b.waitv(tt, "dve3", dv3_0 + 32 * (u + 1))
                    if 4 * u + j2 >= 1:
                        b.waitv(tt, "dve4", dv4_0 + 4 * u + j2)
                    b.waitv(tt, "dve2", dv0 + 36 * u + 20 + 4 * (j2 + 1))
                    for cb in range(2):
                        pm = gpm[cb]
                        for m in range(32):
                            tt.matmul(pm[:, :], t["HID"][:, 512 * m + 128 * j2: 512 * m + 128 * (j2 + 1)],
                                      t["W2S"][:, 1024 * m + 512 * cb: 1024 * m + 512 * (cb + 1)],
                                      start=(m == 0), stop=False)
                        mm = tt.matmul(pm[:, :], t["ones_r"][0:1, 0:128],
                                       t["b2eb"][0:1, 512 * cb: 512 * (cb + 1)], start=False, stop=True)
                    b.inc(mm, "pe2", 1)

        @blk.scalar
        def _(sc):
            for u in range(32):
                b.waitv(sc, "dve2", dv0 + 36 * u + 20)
                b.inc(sc.activation(t["gsc"][:, 44: 48], t["gsc"][:, 40: 44], AF.Exp), "act", 1)

        @blk.vector
        def _(v):
            for u, (kp, cp) in enumerate(order):
                for j2 in range(4):
                    gid = 4 * u + j2
                    b.waitv(v, "pe3", pe3_0 + gid + 1)
                    pl = glg[j2 % 2]
                    g8 = t["gsc"][:, 0:8]
                    b.fence(v, v.tensor_copy(g8, pl[:, 0:E]))
                    b.inc(v.tensor_copy(t["gsc"][:, 8:9], pl[:, 0:1]), "dve2", 1)
                    b.vred(v, "reduce_max", t["gsc"][:, 16 + j2: 17 + j2], g8, AX.X)
                    b.fence(v, v.tensor_scalar(t["gmk"][:, 8 * j2: 8 * j2 + 8], g8,
                                               t["gsc"][:, 16 + j2: 17 + j2], None, ALU.is_equal))
                    b.fence(v, v.scalar_tensor_tensor(t["gsc"][:, 24: 32], t["gmk"][:, 8 * j2: 8 * j2 + 8],
                                                      -1e30, g8, ALU.mult, ALU.add))
                    b.vred(v, "reduce_max", t["gsc"][:, 20 + j2: 21 + j2], t["gsc"][:, 24:32], AX.X)
                    b.fence(v, v.tensor_scalar(t["gmk"][:, 32 + 8 * j2: 40 + 8 * j2], t["gsc"][:, 24:32],
                                               t["gsc"][:, 20 + j2: 21 + j2], None, ALU.is_equal))
                    b.inc(v.tensor_tensor(t["gsc"][:, 40 + j2: 41 + j2], t["gsc"][:, 20 + j2: 21 + j2],
                                          t["gsc"][:, 16 + j2: 17 + j2], ALU.subtract), "dve2", 4)
                b.waitv(v, "act", ac0 + u + 1)
                for j2 in range(4):
                    gid = 4 * u + j2
                    b.fence(v, v.tensor_scalar(t["gsc"][:, 48 + j2: 49 + j2], t["gsc"][:, 44 + j2: 45 + j2],
                                               1.0, None, ALU.add))
                    b.fence(v, v.reciprocal(t["gsc"][:, 52 + j2: 53 + j2], t["gsc"][:, 48 + j2: 49 + j2]))
                    b.fence(v, v.tensor_tensor(t["gsc"][:, 56 + j2: 57 + j2], t["gsc"][:, 44 + j2: 45 + j2],
                                               t["gsc"][:, 52 + j2: 53 + j2], ALU.mult))
                    v.tensor_scalar(t["gsc"][:, 0:8], t["gmk"][:, 8 * j2: 8 * j2 + 8],
                                    t["gsc"][:, 52 + j2: 53 + j2], None, ALU.mult)
                    b.fence(v, v.tensor_scalar(t["gsc"][:, 8:16], t["gmk"][:, 32 + 8 * j2: 40 + 8 * j2],
                                               t["gsc"][:, 56 + j2: 57 + j2], None, ALU.mult))
                    b.fence(v, v.tensor_tensor(t["gsc"][:, 0:8], t["gsc"][:, 0:8], t["gsc"][:, 8:16], ALU.add))
                    b.fence(v, v.tensor_tensor(t["gsc"][:, 0:8], t["gsc"][:, 0:8], t["eselS"][:, :], ALU.mult))
                    b.vred(v, "reduce_sum", t["ewtt"][:, j2: j2 + 1], t["gsc"][:, 0:8], AX.X)
                    b.inc(v.tensor_copy(t["ewtt"][:, 4 + j2: 5 + j2], t["ewtt"][:, j2: j2 + 1]), "dve2", 4)
                for m in range(32):
                    b.waitv(v, "pe", pe0 + 32 * u + m + 1)
                    if m == 0 and u >= 1:
                        b.waitv(v, "pe2", pe2_0 + 4 * u)
                    ph = gph[m % 2]
                    b.inc(v.tensor_scalar(t["HID"][:, 512 * m: 512 * (m + 1)], ph[:, :],
                                          t["b1effc"][:, m: m + 1], 0.0, ALU.add, ALU.max), "dve3", 1)
                for j2 in range(4):
                    b.waitv(v, "pe2", pe2_0 + 4 * u + j2 + 1)
                    if u >= 1 and j2 == 0:
                        b.waitv(v, "ldc", ldc0 + u * 16)
                    for cb in range(2):
                        i6 = v.tensor_scalar(t["MOEST"][:, 1024 * j2 + 512 * cb: 1024 * j2 + 512 * (cb + 1)],
                                             gpm[cb][:, :], t["ewtt"][:, j2: j2 + 1], None, ALU.mult)
                    b.inc(i6, "dve4", 1)

    @blk.gpsimd
    def _(g):
        for kp in range(4):
            b.waitv(g, "ldc", ldc0 + (8 * kp + 8) * 16)
            b.inc(g.collective_compute("ReduceScatter", ALU.add, replica_groups=GROUPS8,
                                       ins=[t["rs_in"][4096 * kp: 4096 * (kp + 1), :].opt()],
                                       outs=[t["rs_out"][512 * kp: 512 * (kp + 1), :].opt()]), "cc", 1)


def phase_H(b, blk):
    nc, t = b.nc, b.t
    ccH = b.cnt["cc"]
    dv0 = b.cnt["dve"]
    dv3_0 = b.cnt["dve3"]
    ac0 = b.cnt["act2"]
    lda0, ldb0 = b.cnt["lda"], b.cnt["ldb"]
    gp0, gp20 = b.cnt["gp"], b.cnt["gp2"]
    ldv0 = b.cnt["ldv"]
    st0 = b.cnt["st"]

    @blk.sync
    def _(s):
        b.waitv(s, "cc", ccH)
        for j in range(NJ):
            sem = "lda" if j % 2 == 0 else "ldb"
            if j >= 2:
                b.waitv(s, "dve", dv0 + 2 * (j - 2) + 1)
            b.inc(s.dma_start(out=(t["xc"] if j % 2 == 0 else t["msc"])[:, :],
                              in_=t["u2_pk"][:, 1024 * j: 1024 * (j + 1)]), sem, 16)
            if j >= 1:
                b.waitv(s, "dve", dv0 + 2 * (j - 1) + 1)
                if j >= 2:
                    b.waitv(s, "st", st0 + 16 * (j - 1))
                b.inc(s.dma_start(out=t["moef"][:, :], in_=t["x1_pk"][:, 1024 * (j - 1): 1024 * j]), "ldv", 16)
                b.waitv(s, "dve", dv0 + 2 * (j - 1) + 2)
                b.inc(s.dma_start(out=t["y"][128 * (j - 1): 128 * j, :], in_=t["moef"][:, :]), "st", 16)
        b.waitv(s, "dve", dv0 + 2 * (NJ - 1) + 1)
        b.waitv(s, "st", st0 + 16 * (NJ - 1))
        b.inc(s.dma_start(out=t["moef"][:, :], in_=t["x1_pk"][:, 1024 * (NJ - 1): 1024 * NJ]), "ldv", 16)
        b.waitv(s, "dve", dv0 + 2 * NJ)
        b.inc(s.dma_start(out=t["y"][128 * (NJ - 1): 128 * NJ, :], in_=t["moef"][:, :]), "st", 16)

    @blk.gpsimd
    def _(g):
        b.waitv(g, "cc", ccH)
        for j in range(NJ):
            if j >= 2:
                b.waitv(g, "dve", dv0 + 2 * (j - 2) + 1)
            b.inc(g.dma_start(out=(t["hsq"] if j % 2 == 0 else t["dsq"])[:, :],
                              in_=t["rs_out"][128 * j: 128 * (j + 1), :]),
                  "gp" if j % 2 == 0 else "gp2", 16)

    @blk.vector
    def _(v):
        for j in range(NJ):
            xb = t["xc"] if j % 2 == 0 else t["msc"]
            mb = t["hsq"] if j % 2 == 0 else t["dsq"]
            sem = "lda" if j % 2 == 0 else "ldb"
            b.waitv(v, sem, (lda0 if j % 2 == 0 else ldb0) + (j // 2 + 1) * 16)
            sem2 = "gp" if j % 2 == 0 else "gp2"
            b.waitv(v, sem2, (gp0 if j % 2 == 0 else gp20) + (j // 2 + 1) * 16)
            b.inc(v.tensor_tensor(xb[:, :], xb[:, :], mb[:, :], ALU.add), "dve", 1)
            st_ = t["stat"]
            b.vred(v, "reduce_sum", st_[:, j: j + 1], xb[:, :], AX.X)
            b.fence(v, v.tensor_scalar(st_[:, 16 + j: 17 + j], st_[:, j: j + 1], float(INV_D), None, ALU.mult))
            b.fence(v, v.tensor_scalar(t["sq"][:, :], xb[:, :], st_[:, 16 + j: 17 + j], None, ALU.subtract))
            b.fence(v, v.tensor_tensor(t["csq"][:, :], t["sq"][:, :], t["sq"][:, :], ALU.mult))
            b.vred(v, "reduce_sum", st_[:, j: j + 1], t["csq"][:, :], AX.X)
            b.inc(v.tensor_scalar(st_[:, j: j + 1], st_[:, j: j + 1], float(INV_D), float(EPS),
                                  ALU.mult, ALU.add), "dve3", 1)
            b.waitv(v, "act2", ac0 + j + 1)
            b.fence(v, v.tensor_scalar(t["sq"][:, :], t["sq"][:, :], t["rstdb"][:, j: j + 1], None, ALU.mult))
            b.waitv(v, "ldv", ldv0 + 16 * (j + 1))
            b.inc(v.tensor_tensor(t["moef"][:, :], t["moef"][:, :], t["sq"][:, :], ALU.add), "dve", 1)

    @blk.scalar
    def _(sc):
        for j in range(NJ):
            b.waitv(sc, "dve3", dv3_0 + j + 1)
            sc.activation(t["stat"][:, 16 + j: 17 + j], t["stat"][:, j: j + 1], AF.Ln)
            b.inc(sc.activation(t["rstdb"][:, j: j + 1], t["stat"][:, 16 + j: 17 + j], AF.Exp,
                                scale=np.float32(-0.5)), "act2", 1)


def emit_av_e(b, tt, u, g, EOP, PBe, ac0):
    t = b.t
    gg = 64 * u + g
    b.waitv(tt, "act", ac0 + gg + 1)
    r2, tile = g // 16, g % 16
    col = 2048 * r2 + 128 * tile
    tt.matmul(EOP[0][:, :], t["VHA"][:, col: col + 128],
              PBe[gg % 2][:, 0:512], start=(g == 0), stop=(g == 63))
    mm = tt.matmul(EOP[1][:, :], t["VHB"][:, col: col + 128],
                   PBe[gg % 2][:, 512:1024], start=(g == 0), stop=(g == 63))
    b.inc(mm, "pe2", 1)


def phase_E_end(b, blk):
    nc, t = b.nc, b.t
    dv3_all = b.cnt["dve3"]

    @blk.gpsimd
    def _(g):
        b.waitv(g, "dve3", dv3_all)
        b.inc(g.dma_start(out=t["WO2"].rearrange("p (k f) -> p k f", f=1024),
                          in_=t["wo2"].rearrange("(k p) f -> p k f", p=128)), "ldg", 16)
        b.inc(g.dma_start(out=t["bob"][0:1, :], in_=t["bo"]), "ldg", 16)
    ldg_e = b.cnt["ldg"]

    pe0 = b.cnt["pe"]
    pe2_0 = b.cnt["pe2"]
    pe3_0 = b.cnt["pe3"]
    ac0 = b.cnt["act2"]
    dv0 = b.cnt["dve"]
    lda0, ldb0 = b.cnt["lda"], b.cnt["ldb"]
    ldc0 = b.cnt["ldc"]
    ld30 = b.cnt["ld3"]

    with nc.psum_tensor("etr2", [128, 1024], BF16) as etr2, \
         nc.psum_tensor("exo0", [128, 512], F32) as exo0, \
         nc.psum_tensor("exo1", [128, 512], F32) as exo1, \
         nc.psum_tensor("etr3", [128, 1024], BF16) as etr3:
        exo = [exo0, exo1]

        @blk.sync
        def _(s):
            b.waitv(s, "dve3", dv3_all)
            for j in range(NJ):
                sem = "lda" if j % 2 == 0 else "ldb"
                if j >= 2:
                    b.waitv(s, "dve", dv0 + 4 * (j - 2) + 2)
                b.inc(s.dma_start(out=(t["xc"] if j % 2 == 0 else t["msc"])[:, :],
                                  in_=t["u_pk"][:, 1024 * j: 1024 * (j + 1)]), sem, 16)
                if j >= 1:
                    b.waitv(s, "dve", dv0 + 4 * (j - 1) + 3)
                    b.inc(s.dma_start(out=t["u2_pk"][:, 1024 * (j - 1): 1024 * j],
                                      in_=(t["xc"] if (j - 1) % 2 == 0 else t["msc"])[:, :]), "ldc", 16)
            b.waitv(s, "dve", dv0 + 4 * (NJ - 1) + 3)
            b.inc(s.dma_start(out=t["u2_pk"][:, 1024 * (NJ - 1): 1024 * NJ],
                              in_=(t["xc"] if (NJ - 1) % 2 == 0 else t["msc"])[:, :]), "ldc", 16)
            b.waitv(s, "dve", dv0 + 4 * NJ)
            for k in range(8):
                b.inc(s.dma_start(out=t["h2t_in"][128 * k: 128 * (k + 1), :],
                                  in_=t["H2TS"][:, 2048 * k: 2048 * (k + 1)]), "ld3", 16)

        @blk.tensor
        def _(tt):
            b.waitv(tt, "ldg", ldg_e)
            for j in range(NJ):
                if j >= 1:
                    b.waitv(tt, "dve", dv0 + 4 * (j - 1) + 1)
                for k in range(8):
                    mm = tt.transpose(etr2[:, 128 * k: 128 * (k + 1)],
                                      t["O2N"][:, 1024 * j + 128 * k: 1024 * j + 128 * (k + 1)],
                                      t["identb"][:, :])
                b.inc(mm, "pe", 1)
                b.waitv(tt, "dve", dv0 + 4 * j + 1)
                if j >= 1:
                    b.waitv(tt, "dve", dv0 + 4 * (j - 1) + 2)
                for cb in range(2):
                    pm = exo[cb]
                    for k in range(8):
                        tt.matmul(pm[:, :], t["o2lh"][:, 128 * k: 128 * (k + 1)],
                                  t["WO2"][:, 1024 * k + 512 * cb: 1024 * k + 512 * (cb + 1)],
                                  start=(k == 0), stop=False)
                    mm = tt.matmul(pm[:, :], t["ones_r"][0:1, 0:128], t["bob"][0:1, 512 * cb: 512 * (cb + 1)],
                                   start=False, stop=True)
                b.inc(mm, "pe2", 1)
                b.waitv(tt, "dve", dv0 + 4 * j + 3)
                for k in range(8):
                    mm = tt.transpose(etr3[:, 128 * k: 128 * (k + 1)], t["n2e"][:, 128 * k: 128 * (k + 1)],
                                      t["identb"][:, :])
                b.inc(mm, "pe3", 1)

        @blk.vector
        def _(v):
            for j in range(NJ):
                xb = t["xc"] if j % 2 == 0 else t["msc"]
                sem = "lda" if j % 2 == 0 else "ldb"
                b.waitv(v, sem, (lda0 if j % 2 == 0 else ldb0) + (j // 2 + 1) * 16)
                b.waitv(v, "pe", pe0 + j + 1)
                if j >= 1:
                    b.waitv(v, "pe2", pe2_0 + j)
                b.inc(v.tensor_copy(t["o2lh"][:, :], etr2[:, :]), "dve", 1)
                b.waitv(v, "pe2", pe2_0 + j + 1)
                v.tensor_tensor(xb[:, 0:512], xb[:, 0:512], exo[0][:, :], ALU.add)
                b.fence(v, v.tensor_tensor(xb[:, 512:1024], xb[:, 512:1024], exo[1][:, :], ALU.add))
                b.inc(v.tensor_copy(t["gsc"][:, 1:2], xb[:, 1023:1024]), "dve", 1)
                st_ = t["stat"]
                b.vred(v, "reduce_sum", st_[:, 4:5], xb[:, :], AX.X)
                b.fence(v, v.tensor_scalar(st_[:, 5:6], st_[:, 4:5], float(INV_D), None, ALU.mult))
                b.fence(v, v.tensor_scalar(t["sq"][:, :], xb[:, :], st_[:, 5:6], None, ALU.subtract))
                b.fence(v, v.tensor_tensor(t["esq"][:, :], t["sq"][:, :], t["sq"][:, :], ALU.mult))
                b.vred(v, "reduce_sum", st_[:, 4:5], t["esq"][:, :], AX.X)
                b.inc(v.tensor_scalar(st_[:, 4:5], st_[:, 4:5], float(INV_D), float(EPS),
                                      ALU.mult, ALU.add), "dve3", 1)
                b.waitv(v, "act2", ac0 + j + 1)
                if j >= 1:
                    b.waitv(v, "pe3", pe3_0 + j)
                b.inc(v.tensor_scalar(t["n2e"][:, :], t["sq"][:, :], t["rstdb"][:, 2:3], None, ALU.mult),
                      "dve", 1)
                b.waitv(v, "pe3", pe3_0 + j + 1)
                for k in range(8):
                    i5 = v.tensor_copy(t["H2TS"][:, 2048 * k + 128 * j: 2048 * k + 128 * (j + 1)],
                                       etr3[:, 128 * k: 128 * (k + 1)])
                b.inc(i5, "dve", 1)

        @blk.scalar
        def _(sc):
            dv3e = b.cnt["dve3"] - NJ
            for j in range(NJ):
                b.waitv(sc, "dve3", dv3e + j + 1)
                sc.activation(t["stat"][:, 5:6], t["stat"][:, 4:5], AF.Ln)
                b.inc(sc.activation(t["rstdb"][:, 2:3], t["stat"][:, 5:6], AF.Exp, scale=np.float32(-0.5)),
                      "act2", 1)

    @blk.gpsimd
    def _(g):
        b.wait(g, "ld3")
        b.inc(g.collective_compute("AllGather", ALU.bypass, replica_groups=GROUPS8,
                                   ins=[t["h2t_in"].opt()], outs=[t["h2t_ag"].opt()]), "cc", 1)


def emit_av(b, tt, units, u, g, bop, PB, ac0, dv0, gg_of_unit, mask_cnt_before):
    t = b.t
    bt, i, qb, nch, groups = units[u]
    base = gg_of_unit[u]
    gg = base + g
    grp = groups[g]
    b.waitv(tt, "act", ac0 + gg + 1)
    nmask_before_and_in = mask_cnt_before[u] + sum(
        1 for gp in range(g + 1) for n in groups[gp]
        if (n - 4 * qb) >= 0 and (n - 4 * qb) < 4)
    if nmask_before_and_in > 0:
        b.waitv(tt, "dve", b.cnt_dv0_B + nmask_before_and_in)
    for k, n in enumerate(grp):
        tile = 64 * bt + n
        mm = tt.matmul(bop[:, :], t["VP"][:, 130 * tile + 65 * i: 130 * tile + 65 * (i + 1)],
                       PB[gg % 2][:, 512 * k: 512 * (k + 1)],
                       start=(grp[0] == 0 and k == 0), stop=(n == nch - 1))
    b.inc(mm, "pe2", 1)


def kernel(**inputs):
    in_maps = make_in_maps(inputs)
    nc, _ = build_nc(debug=False)
    res = run_bass_kernel_spmd(nc, in_maps, core_ids=list(range(8)))
    return np.concatenate([res.results[cid]["y"] for cid in range(8)], 0).reshape(B, S, D)
